# version 05
# baseline (speedup 1.0000x reference)
import contextlib
import numpy as np
import concourse.bass as bass
import concourse.mybir as mybir
from concourse.bass_utils import run_bass_kernel_spmd

F32 = mybir.dt.float32
F16 = mybir.dt.float16
BF16 = mybir.dt.bfloat16
FP8 = mybir.dt.float8e4
AF = mybir.ActivationFunctionType
ALU = mybir.AluOpType
AX = mybir.AxisListType

D = 1024
H = 4096
K = 32
NT = 32
NTL = 33
SEQ = 4096
WROW = 160

SERIAL = True
DEBUG_ATTN_ONLY = False
DEBUG_PHASE0_ONLY = False
import os as _os
P0_CUT = int(_os.environ.get('P0_CUT', '9'))
MAXCYC = int(_os.environ.get('MAXCYC', '99'))
ACUT = int(_os.environ.get('ACUT', '9'))
AIN = int(_os.environ.get('AIN', '9'))
DBGOUT = int(_os.environ.get('DBGOUT', '5'))
FCUT = int(_os.environ.get('FCUT', '9'))
DJN = int(_os.environ.get('DJN', '32'))
MCUT = int(_os.environ.get('MCUT', '9'))
DDC = int(_os.environ.get('DDC', '2'))

(CA_SUM, CA_SQ, CA_MEAN, CA_MSQ, CA_MSQE, CA_VAR, CA_STD, CA_INV, CA_NMI,
 CB_SUM, CB_SQ, CB_MEAN, CB_MSQ, CB_MSQE, CB_VAR, CB_STD, CB_INV, CB_NMI,
 CG_SUM, CG_SQ, CG_M, CG_MSQ, CG_MSQE, CG_VAR, CG_STD, CG_INV,
 CH_A0, CH_A1, CH_SUM, CH_SQ, CH_M, CH_MSQ, CH_MSQE, CH_VAR, CH_STD, CH_INV,
 CQ1_MXR, CQ1_MX, CQ1_T, CQ1_S, CQ1_G,
 CQ2_MXR, CQ2_MX, CQ2_T, CQ2_S, CQ2_G,
 CS_NEGMX, CS_Z, CS_RZ) = range(49)
NCOLS = 50


class Plan:
    ENGINES = ("sync", "vector", "scalar", "tensor", "gpsimd")

    def __init__(self):
        self.ops = {e: [] for e in self.ENGINES}
        self.cnt = {}

    def inc(self, sem, by=1):
        self.cnt[sem] = self.cnt.get(sem, 0) + by
        return self.cnt[sem]

    def val(self, sem):
        return self.cnt.get(sem, 0)

    def add(self, engine, fn):
        self.ops[engine].append(fn)


def build_program():
    nc = bass.Bass("TRN2", target_bir_lowering=False,
                   detect_race_conditions=(_os.environ.get('RACED', '0') == '1'))

    x_in = nc.dram_tensor("x", [NTL * 128, D], F32, kind="ExternalInput")
    w1t_in = nc.dram_tensor("w1t", [D, H], F32, kind="ExternalInput")
    w2t_in = nc.dram_tensor("w2t", [H, D], F32, kind="ExternalInput")
    wat_in = nc.dram_tensor("wat", [D, K], F32, kind="ExternalInput")
    ba_in = nc.dram_tensor("battn", [128, K], F32, kind="ExternalInput")
    g1_in = nc.dram_tensor("g1", [128, D], F32, kind="ExternalInput")
    b1ln_in = nc.dram_tensor("b1ln", [128, D], F32, kind="ExternalInput")
    g2_in = nc.dram_tensor("g2", [128, H], F32, kind="ExternalInput")
    b2ln_in = nc.dram_tensor("b2ln", [128, H], F32, kind="ExternalInput")
    b1_in = nc.dram_tensor("b1", [128, H], F32, kind="ExternalInput")
    b2_in = nc.dram_tensor("b2", [128, D], F32, kind="ExternalInput")
    out_dr = nc.dram_tensor("out", [SEQ, D], F32, kind="ExternalOutput")
    whbm = nc.dram_tensor("whbm", [2 * (128 * WROW + 64)], F32, kind="Internal")

    identf_dr = nc.inline_tensor(np.eye(128, dtype=np.float32), name="identf_c")

    ctx = contextlib.ExitStack()

    def mk_sb(name, shape, dtype):
        return ctx.enter_context(nc.sbuf_tensor(name, shape, dtype))

    def mk_ps(name):
        return ctx.enter_context(nc.psum_tensor(name, [128, 512], F32))

    with ctx:
        W1T = [mk_sb(f"w1t_{g}", [128, H], FP8) for g in range(8)]
        W2T = [mk_sb(f"w2t_{j}", [128, D], FP8) for j in range(32)]
        WT = mk_sb("wt", [128, 8 * K], F32)
        G1 = mk_sb("g1t", [128, D], F32)
        B1LN = mk_sb("b1lnt", [128, D], F32)
        G2 = mk_sb("g2t", [128, H], F32)
        B2LN = mk_sb("b2lnt", [128, H], F32)
        B1 = mk_sb("b1t", [128, H], F32)
        B2 = mk_sb("b2t", [128, D], F32)
        BATB = mk_sb("batb", [128, K], F32)
        IDENT = mk_sb("ident", [128, 128], F32)
        ONES128 = mk_sb("ones128", [128, 128], F32)
        ONECOL = mk_sb("onecol", [128, 1], F32)
        EPS = mk_sb("epst", [128, 1], F32)
        C127 = mk_sb("c127", [128, 1], F32)
        INV127 = mk_sb("inv127", [128, 1], F32)
        MW1 = mk_sb("mw1", [128, 1], F32)
        MW2 = mk_sb("mw2", [128, 1], F32)
        SCL1 = mk_sb("scl1", [128, 1], F32)
        SCL2 = mk_sb("scl2", [128, 1], F32)
        WPART = mk_sb("wpart", [128, 68], F32)
        ZEROT = mk_sb("zerot", [128, 128], F32)
        SPACER = mk_sb("spacer", [128, 128], F32)

        XT = [mk_sb(f"x_{r}", [128, D], F32) for r in range(2)]
        XLN = [mk_sb(f"xln_{r}", [128, D], F32) for r in range(2)]
        SCR16 = mk_sb("scr16", [128, D], F16)
        if _os.environ.get('XLNFM_REAL', '0') == '1':
            XLNFM = mk_sb("xlnfm", [128, 512], F32)[:]
        else:
            XLNFM = SCR16[:].bitcast(F32)
        LG = mk_sb("lg", [32, 128], F32)
        LOGITS = mk_sb("logits", [128, K], F32)
        WP = mk_sb("wp", [128, K], F32)
        MT_SB = mk_sb("mt_sb", [128, WROW], F32)
        MA = mk_sb("ma", [128, 128], F32)
        MB = mk_sb("mb", [32, 128], F32)
        X2 = [mk_sb(f"x2_{r}", [128, D], F32) for r in range(3)]
        ZY = mk_sb("zy", [128, D], F32)
        Q1 = mk_sb("q1", [128, D], BF16)
        Q1T = mk_sb("q1t", [128, D], BF16)
        Q2 = mk_sb("q2", [128, H], BF16)
        Q2T = mk_sb("q2t", [128, H], BF16)
        GEL = mk_sb("gel", [128, H], F32)
        OUT_T = mk_sb("out_t", [128, D], F32)
        ST = [mk_sb(f"st_{r}", [128, NCOLS], F32) for r in range(3)]
        GSUM = [mk_sb(f"gsum_{r}", [128, 8], F32) for r in range(2)]

        psA = mk_ps("psA")
        psM = mk_ps("psM")
        psB0 = mk_ps("psB0")
        psB1 = mk_ps("psB1")
        psC0 = mk_ps("psC0")
        psC1 = mk_ps("psC1")
        psD0 = mk_ps("psD0")
        psD1 = mk_ps("psD1")

        sems = {}
        for s in ("dx", "dw", "dt1", "dt2", "db", "do", "pe", "ve", "se", "bar",
                  "vh", "fn"):
            sems[s] = ctx.enter_context(nc.semaphore(s))

        P = Plan()
        pe_done, ve_done, se_done, dt_done, db_done, dx_done, do_done = \
            {}, {}, {}, {}, {}, {}, {}

        def op(eng, fn):
            P.add(eng, fn)

        vhc = [0]

        def VG(e, inst):
            e.tensor_copy(SPACER[:], ZEROT[:])

        def FN(e, inst):
            inst.then_inc(sems["fn"], 1)

        def col(r, c):
            return ST[r][:, c:c + 1]

        def barrier():
            snap = {s: P.val(s) for s in
                    ("pe", "ve", "se", "dx", "dw", "dt1", "dt2", "db", "do")}
            for eng in Plan.ENGINES:
                def b(e, snap=snap):
                    for s, v in snap.items():
                        if v > 0:
                            e.wait_ge(sems[s], v)
                op(eng, b)

        def ph0_memsets(e):
            e.memset(ONES128[:], 1.0)
            FN(e, e.memset(WPART[:, 64:68], 0.0))
            FN(e, e.memset(ONECOL[:], 1.0))
            FN(e, e.memset(EPS[:], 1e-5))
            e.memset(ZEROT[:], 0.0)
            FN(e, e.memset(C127[:], 127.0))
            VG(e, None)
            FN(e, e.reciprocal(INV127[:], C127[:]))
            e.tensor_copy(SPACER[:], ZEROT[:]).then_inc(sems["ve"], 1)
        op("vector", ph0_memsets)
        ve_ms = P.inc("ve")

        def ph0_loads(e):
            e.dma_start(out=IDENT[:], in_=identf_dr.ap()).then_inc(sems["dw"], 16)
            for g in range(8):
                e.dma_start(out=WT[:, g * K:(g + 1) * K],
                            in_=wat_in.ap()[g * 128:(g + 1) * 128, :]).then_inc(sems["dw"], 16)
            e.dma_start(out=BATB[:], in_=ba_in.ap()).then_inc(sems["dw"], 16)
            for src_d, dst_t in ((g1_in, G1), (b1ln_in, B1LN), (b2_in, B2),
                                 (g2_in, G2), (b2ln_in, B2LN), (b1_in, B1)):
                e.dma_start(out=dst_t[:], in_=src_d.ap()).then_inc(sems["dw"], 16)
        op("sync", ph0_loads)
        P.inc("dw", 16 * 16)

        def ph0_whbm(e):
            e.wait_ge(sems["ve"], ve_ms)
            for r in range(2):
                e.dma_start(
                    out=bass.AP(whbm, r * (128 * WROW + 64) + 32, [[WROW, 128], [1, 128]]),
                    in_=ZEROT[:],
                ).then_inc(sems["db"], 16)
        op("gpsimd", ph0_whbm)
        P.inc("db", 32)

        op("tensor", lambda e: e.wait_ge(sems["ve"], ve_ms))

        def wpass1(nrowchunks, rowfree, src_dr, pcol0):
            npieces_per = rowfree // D
            pc = pcol0
            for cidx in range(nrowchunks):
                for si in range(npieces_per):
                    a = si * D
                    buf_off = ((cidx * npieces_per + si) % 4) * D
                    prev_se = P.val("se") - 3
                    def ld(e, cidx=cidx, a=a, buf_off=buf_off, src_dr=src_dr,
                           prev_se=prev_se):
                        if prev_se > 0:
                            e.wait_ge(sems["se"], prev_se)
                        e.dma_start(out=GEL[:, buf_off:buf_off + D],
                                    in_=src_dr.ap()[cidx * 128:(cidx + 1) * 128, a:a + D]
                                    ).then_inc(sems["dw"], 16)
                    op("sync", ld)
                    dwv = P.inc("dw", 16)
                    def ab(e, dwv=dwv + 16, buf_off=buf_off, pc=pc):
                        e.wait_ge(sems["dw"], dwv)
                        e.activation(out=SCR16[:], in_=GEL[:, buf_off:buf_off + D],
                                     func=AF.Abs, bias=0.0, scale=1.0,
                                     accum_out=WPART[:, pc:pc + 1]).then_inc(sems["se"], 1)
                    op("scalar", ab)
                    P.inc("se")
                    pc += 1

        if P0_CUT >= 3:
            wpass1(8, H, w1t_in, 0)
            wpass1(32, D, w2t_in, 32)

        sev_abs = P.val("se")

        def wcomb_v(e):
            e.wait_ge(sems["se"], sev_abs)
            FN(e, e.reduce_sum(WPART[:, 64:65], WPART[:, 0:32], axis=AX.X))
            e.reduce_sum(WPART[:, 65:66], WPART[:, 32:64], axis=AX.X).then_inc(sems["ve"], 1)
        if P0_CUT >= 4:
            op("vector", wcomb_v)
        vev_c = P.inc("ve") if P0_CUT >= 4 else 0

        def wcomb_t(e):
            e.wait_ge(sems["ve"], vev_c)
            e.matmul(psM[0:1, 0:1], WPART[:, 64:65], ONECOL[:, 0:1], start=True, stop=True)
            e.matmul(psM[0:1, 1:2], WPART[:, 65:66], ONECOL[:, 0:1], start=True, stop=True).then_inc(sems["pe"], 1)
        if P0_CUT >= 4:
            op("tensor", wcomb_t)
        pev_c = P.inc("pe") if P0_CUT >= 4 else 0

        def wmean_v(e):
            e.wait_ge(sems["pe"], pev_c)
            e.tensor_scalar(out=WPART[0:1, 66:68], in0=psM[0:1, 0:2],
                            scalar1=float(1.0 / (D * H)), scalar2=1e-5,
                            op0=ALU.mult, op1=ALU.max).then_inc(sems["ve"], 1)
        if P0_CUT >= 4:
            op("vector", wmean_v)
        vev_m = P.inc("ve") if P0_CUT >= 4 else 0

        def wbc_t(e):
            e.wait_ge(sems["ve"], vev_m)
            e.matmul(psM[:, 2:3], ONES128[:], WPART[:, 66:67], start=True, stop=True)
            e.matmul(psM[:, 3:4], ONES128[:], WPART[:, 67:68], start=True, stop=True).then_inc(sems["pe"], 1)
        if P0_CUT >= 4:
            op("tensor", wbc_t)
        pev_b = P.inc("pe") if P0_CUT >= 4 else 0

        def wbc_v(e):
            e.wait_ge(sems["pe"], pev_b)
            FN(e, e.tensor_copy(MW1[:], psM[:, 2:3]))
            FN(e, e.tensor_copy(MW2[:], psM[:, 3:4]))
            VG(e, None)
            FN(e, e.reciprocal(SCL1[:], MW1[:]))
            e.reciprocal(SCL2[:], MW2[:]).then_inc(sems["ve"], 1)
        if P0_CUT >= 4:
            op("vector", wbc_v)
        vev_scl = P.inc("ve") if P0_CUT >= 4 else 0

        def wpass2(nrowchunks, rowfree, src_dr, dst_fn):
            npieces_per = rowfree // D
            first_overall = True
            for cidx in range(nrowchunks):
                for si in range(npieces_per):
                    a = si * D
                    buf_off = ((cidx * npieces_per + si) % 4) * D
                    prev_ve = P.val("ve") - 3
                    def ld(e, cidx=cidx, a=a, buf_off=buf_off, src_dr=src_dr,
                           prev_ve=prev_ve):
                        if prev_ve > 0:
                            e.wait_ge(sems["ve"], prev_ve)
                        e.dma_start(out=GEL[:, buf_off:buf_off + D],
                                    in_=src_dr.ap()[cidx * 128:(cidx + 1) * 128, a:a + D]
                                    ).then_inc(sems["dw"], 16)
                    op("sync", ld)
                    dwv = P.inc("dw", 16)
                    dst_t, dst_off = dst_fn(cidx, a)
                    scl = SCL1 if src_dr is w1t_in else SCL2
                    fo = first_overall
                    first_overall = False
                    def qz(e, dwv=dwv + 16, buf_off=buf_off, dst_t=dst_t, dst_off=dst_off,
                           scl=scl, fo=fo):
                        e.wait_ge(sems["dw"], dwv)
                        if fo:
                            e.wait_ge(sems["ve"], vev_scl)
                            e.wait_ge(sems["se"], sev_abs)
                        e.tensor_scalar(out=ZY[:], in0=GEL[:, buf_off:buf_off + D],
                                        scalar1=scl[:, 0:1], scalar2=12582912.0,
                                        op0=ALU.mult, op1=ALU.add)
                        e.tensor_scalar(out=ZY[:], in0=ZY[:],
                                        scalar1=12582912.0, scalar2=None, op0=ALU.subtract)
                        e.tensor_scalar(out=dst_t[:, dst_off:dst_off + D],
                                        in0=ZY[:], scalar1=1.0, scalar2=-1.0,
                                        op0=ALU.min, op1=ALU.max).then_inc(sems["ve"], 1)
                    op("vector", qz)
                    P.inc("ve")

        if P0_CUT >= 5:
            wpass2(8, H, w1t_in, lambda cidx, a: (W1T[cidx], a))
            wpass2(32, D, w2t_in, lambda cidx, a: (W2T[cidx], 0))
            nlast = (8 * 4 + 32)
            ve_slot_free = P.val("ve") - 3
            def ld_tail(e, slot=(nlast % 4) * D, wv=ve_slot_free):
                e.wait_ge(sems["ve"], wv)
                e.dma_start(out=GEL[:, slot:slot + D],
                            in_=w1t_in.ap()[0:128, 0:D]).then_inc(sems["dw"], 16)
            op("sync", ld_tail)
            P.inc("dw", 16)

        barrier()

        if DBGOUT == 14:
            wv_all = P.val("ve")
            def wdump(e, wv_all=wv_all):
                e.wait_ge(sems["ve"], wv_all)
                for g in range(8):
                    e.dma_start(out=out_dr.ap()[g * 128:(g + 1) * 128, :],
                                in_=W1T[g][:].bitcast(F32)).then_inc(sems["do"], 16)
                for j in range(24):
                    e.dma_start(out=out_dr.ap()[1024 + j * 128:1024 + (j + 1) * 128, 0:256],
                                in_=W2T[j][:].bitcast(F32)).then_inc(sems["do"], 16)
            op("sync", wdump)
            P.inc("do", 16 * 32)

        if DEBUG_PHASE0_ONLY:
            def dbg(e):
                e.tensor_copy(ZY[:], W1T[0][:, 0:D])
                e.tensor_copy(X2[0][:], W2T[0][:])
                e.tensor_copy(OUT_T[:, 0:1], MW1[:])
                e.tensor_copy(OUT_T[:, 1:2], MW2[:])
                e.tensor_copy(OUT_T[:, 2:3], SCL1[:])
                e.tensor_copy(OUT_T[:, 3:4], SCL2[:])
                e.tensor_copy(OUT_T[:, 4:132], IDENT[:])
                e.tensor_copy(OUT_T[:, 132:164], WT[:, 0:32])
                e.tensor_copy(OUT_T[:, 164:232], WPART[:]).then_inc(sems["ve"], 1)
            op("vector", dbg)
            vd = P.inc("ve")
            def dbg_st(e, vd=vd):
                e.wait_ge(sems["ve"], vd)
                e.dma_start(out=out_dr.ap()[0:128, :], in_=ZY[:]).then_inc(sems["do"], 16)
                e.dma_start(out=out_dr.ap()[128:256, :], in_=X2[0][:]).then_inc(sems["do"], 16)
                e.dma_start(out=out_dr.ap()[256:384, :], in_=G2[:, 0:D]).then_inc(sems["do"], 16)
                e.dma_start(out=out_dr.ap()[384:512, :], in_=B1[:, 0:D]).then_inc(sems["do"], 16)
                e.dma_start(out=out_dr.ap()[512:640, :], in_=OUT_T[:]).then_inc(sems["do"], 16)
                e.wait_ge(sems["do"], 80)
            op("sync", dbg_st)
            P.inc("do", 80)

        def emit_load_x(i):
            wv = ve_done.get(("attnev", i - 2), 0)
            def f(e, i=i, wv=wv):
                if wv:
                    e.wait_ge(sems["ve"], wv)
                e.dma_start(out=XT[i % 2][:],
                            in_=x_in.ap()[i * 128:(i + 1) * 128, :]).then_inc(sems["dx"], 16)
            op("sync", f)
            dx_done[i] = P.inc("dx", 16)

        def emit_stats_alpha(i):
            r = i % 3
            dxv = dx_done[i]
            w_r1 = ve_done.get(("round1", i - 2), 0)
            w_r2 = ve_done.get(("round2", i - 3), 0)
            def fs(e, i=i, r=r, dxv=dxv, w_r1=w_r1, w_r2=w_r2):
                e.wait_ge(sems["dx"], dxv)
                if w_r1:
                    e.wait_ge(sems["ve"], w_r1)
                if w_r2:
                    e.wait_ge(sems["ve"], w_r2)
                e.activation(out=SCR16[:], in_=XT[i % 2][:], func=AF.Square,
                             bias=0.0, scale=1.0,
                             accum_out=col(r, CA_SQ)).then_inc(sems["se"], 1)
            op("scalar", fs)
            se_sq = P.inc("se")

            def fv(e, i=i, r=r, dxv=dxv, se_sq=se_sq):
                e.wait_ge(sems["dx"], dxv)
                FN(e, e.reduce_sum(col(r, CA_SUM), XT[i % 2][:], axis=AX.X))
                VG(e, None)
                e.wait_ge(sems["se"], se_sq)
                FN(e, e.tensor_scalar(out=col(r, CA_MEAN), in0=col(r, CA_SUM),
                                      scalar1=1.0 / D, scalar2=None, op0=ALU.mult))
                VG(e, None)
                FN(e, e.tensor_tensor(col(r, CA_MSQ), col(r, CA_MEAN),
                                      col(r, CA_MEAN), ALU.mult))
                VG(e, None)
                FN(e, e.tensor_scalar(out=col(r, CA_MSQE), in0=col(r, CA_MSQ),
                                      scalar1=1.0, scalar2=1e-5,
                                      op0=ALU.mult, op1=ALU.subtract))
                VG(e, None)
                e.tensor_scalar(out=col(r, CA_VAR), in0=col(r, CA_SQ),
                                scalar1=1.0 / D, scalar2=col(r, CA_MSQE),
                                op0=ALU.mult, op1=ALU.subtract).then_inc(sems["ve"], 1)
            op("vector", fv)
            ve_var = P.inc("ve")

            def fs2(e, r=r, ve_var=ve_var):
                e.wait_ge(sems["ve"], ve_var)
                e.activation(out=col(r, CA_STD), in_=col(r, CA_VAR), func=AF.Sqrt,
                             bias=0.0, scale=1.0).then_inc(sems["se"], 1)
            op("scalar", fs2)
            se_sd = P.inc("se")

            def fv2(e, r=r, se_sd=se_sd):
                e.wait_ge(sems["se"], se_sd)
                FN(e, e.reciprocal(col(r, CA_INV), col(r, CA_STD)))
                VG(e, None)
                e.tensor_scalar(out=col(r, CA_NMI), in0=col(r, CA_MEAN), scalar1=-1.0,
                                scalar2=col(r, CA_INV), op0=ALU.mult, op1=ALU.mult).then_inc(sems["ve"], 1)
            op("vector", fv2)
            ve_done[("statsA", i)] = P.inc("ve")

        def emit_xln(i):
            r = i % 3
            wv = ve_done[("statsA", i)]
            wpe = pe_done.get(("band", i - 2), 0)
            def f(e, i=i, r=r, wv=wv, wpe=wpe):
                e.wait_ge(sems["ve"], wv)
                if wpe:
                    e.wait_ge(sems["pe"], wpe)
                e.activation(out=XLN[i % 2][:], in_=XT[i % 2][:], func=AF.Identity,
                             bias=col(r, CA_NMI), scale=col(r, CA_INV)
                             ).then_inc(sems["se"], 1)
            op("scalar", f)
            se_done[("xln", i)] = P.inc("se")

        def emit_A(i):
            sxln = se_done[("xln", i)]
            wv_prev = ve_done.get(("Ae2", i - 1), 0)

            def t1(e, i=i, sxln=sxln, wv_prev=wv_prev):
                e.wait_ge(sems["se"], sxln)
                if wv_prev:
                    e.wait_ge(sems["ve"], wv_prev)
                for g in range(4):
                    last = e.transpose(psA[:, g * 128:(g + 1) * 128],
                                       XLN[i % 2][:, g * 128:(g + 1) * 128], IDENT[:])
                last.then_inc(sems["pe"], 1)
            op("tensor", t1)
            pe_w1 = P.inc("pe")

            def v1(e, pe_w1=pe_w1):
                e.wait_ge(sems["pe"], pe_w1)
                e.tensor_copy(XLNFM[:, 0:512], psA[:, 0:512]).then_inc(sems["ve"], 1)
            op("vector", v1)
            ve_e1 = P.inc("ve")
            if AIN < 2:
                ve_done[("softmax", i)] = 0
                ve_done[("Ae2", i)] = ve_e1
                ve_done[("lgmx", i)] = 0
                return

            def t2(e, i=i, ve_e1=ve_e1):
                e.wait_ge(sems["ve"], ve_e1)
                for g in range(4):
                    e.transpose(psA[:, g * 128:(g + 1) * 128],
                                XLN[i % 2][:, 512 + g * 128:512 + (g + 1) * 128], IDENT[:])
                for g in range(4):
                    last = e.matmul(psM[0:32, 256:384], WT[:, g * K:(g + 1) * K],
                                    XLNFM[:, g * 128:(g + 1) * 128],
                                    start=(g == 0), stop=False)
                last.then_inc(sems["pe"], 1)
            op("tensor", t2)
            pe_w2 = P.inc("pe")

            def v2(e, pe_w2=pe_w2):
                e.wait_ge(sems["pe"], pe_w2)
                e.tensor_copy(XLNFM[:, 0:512], psA[:, 0:512]).then_inc(sems["ve"], 1)
            op("vector", v2)
            ve_e2 = P.inc("ve")
            ve_done[("Ae2", i)] = ve_e2
            if AIN < 3:
                ve_done[("softmax", i)] = 0
                ve_done[("lgmx", i)] = 0
                return

            def t3(e, ve_e2=ve_e2):
                e.wait_ge(sems["ve"], ve_e2)
                for g in range(4):
                    last = e.matmul(psM[0:32, 256:384], WT[:, (4 + g) * K:(5 + g) * K],
                                    XLNFM[:, g * 128:(g + 1) * 128],
                                    start=False, stop=(g == 3))
                last.then_inc(sems["pe"], 1)
            op("tensor", t3)
            pe_lmm = P.inc("pe")

            def s1(e, pe_lmm=pe_lmm):
                e.wait_ge(sems["pe"], pe_lmm)
                e.activation(out=LG[:], in_=psM[0:32, 256:384], func=AF.Identity,
                             bias=0.0, scale=1.0).then_inc(sems["se"], 1)
            op("scalar", s1)
            se_lg = P.inc("se")

            def t4(e, se_lg=se_lg):
                e.wait_ge(sems["se"], se_lg)
                e.transpose(psM[:, 384:416], LG[:], IDENT[0:32, 0:32]).then_inc(sems["pe"], 1)
            op("tensor", t4)
            pe_lgt = P.inc("pe")
            if AIN < 4:
                ve_done[("softmax", i)] = 0
                ve_done[("lgmx", i)] = 0
                return

            r = i % 3
            def v3(e, r=r, pe_lgt=pe_lgt):
                e.wait_ge(sems["pe"], pe_lgt)
                e.tensor_tensor(LOGITS[:], psM[:, 384:416], BATB[:], ALU.add)
                e.reduce_max(col(r, CS_NEGMX), LOGITS[:], axis=AX.X, negate=True).then_inc(sems["ve"], 1)
            op("vector", v3)
            ve_mx = P.inc("ve")
            ve_done[("lgmx", i)] = ve_mx

            wdb = db_done.get(("wst", i - 1), 0)
            def s2(e, r=r, ve_mx=ve_mx, wdb=wdb):
                e.wait_ge(sems["ve"], ve_mx)
                if wdb:
                    e.wait_ge(sems["db"], wdb)
                e.activation(out=WP[:], in_=LOGITS[:], func=AF.Exp,
                             bias=col(r, CS_NEGMX), scale=1.0,
                             accum_out=col(r, CS_Z)).then_inc(sems["se"], 1)
            op("scalar", s2)
            se_exp = P.inc("se")

            def v4(e, r=r, se_exp=se_exp):
                e.wait_ge(sems["se"], se_exp)
                FN(e, e.reciprocal(col(r, CS_RZ), col(r, CS_Z)))
                VG(e, None)
                e.tensor_scalar(out=WP[:], in0=WP[:], scalar1=col(r, CS_RZ),
                                scalar2=0.0, op0=ALU.mult, op1=ALU.add).then_inc(sems["ve"], 1)
            op("vector", v4)
            ve_done[("softmax", i)] = P.inc("ve")

        def emit_band_dma(i):
            wv = ve_done[("softmax", i)]
            base = (i % 2) * (128 * WROW + 64)
            wprev = db_done.get(("mtl", i - 2), 0)
            wdt1 = P.val("dt1")
            wdt2 = P.val("dt2")
            def f(e, wv=wv, base=base, wprev=wprev, wdt1=wdt1, wdt2=wdt2):
                e.wait_ge(sems["ve"], wv)
                if wprev:
                    e.wait_ge(sems["db"], wprev)
                if wdt1:
                    e.wait_ge(sems["dt1"], wdt1)
                if wdt2:
                    e.wait_ge(sems["dt2"], wdt2)
                e.dma_start(out=bass.AP(whbm, base, [[WROW, 128], [1, K]]),
                            in_=WP[:]).then_inc(sems["db"], 16)
            op("gpsimd", f)
            st_v = P.inc("db", 16)
            db_done[("wst", i)] = st_v

            wmt = pe_done.get(("mt", i - 1), 0)
            def f2(e, st_v=st_v, base=base, wmt=wmt):
                e.wait_ge(sems["db"], st_v)
                if wmt:
                    e.wait_ge(sems["pe"], wmt)
                e.dma_start(out=MT_SB[:],
                            in_=bass.AP(whbm, base, [[WROW - 1, 128], [1, WROW]])
                            ).then_inc(sems["db"], 16)
                e.dma_start(out=MT_SB[:, 0:4],
                            in_=bass.AP(whbm, base, [[WROW - 1, 128], [1, 4]])
                            ).then_inc(sems["db"], 16)
            op("gpsimd", f2)
            P.inc("db", 16)
            db_done[("mtl", i)] = P.inc("db", 16)

        def emit_MT(i):
            wdb = db_done[("mtl", i)]
            wv = ve_done.get(("lgmx", i), 0)
            def f(e, wdb=wdb, wv=wv):
                e.wait_ge(sems["db"], wdb)
                if wv:
                    e.wait_ge(sems["ve"], wv)
                e.transpose(psM[:, 0:128], MT_SB[:, 0:128], IDENT[:])
                e.transpose(psM[0:32, 128:256], MT_SB[:, 128:160], IDENT[:]).then_inc(sems["pe"], 1)
            op("tensor", f)
            pe_done[("mt", i)] = P.inc("pe")

        def emit_Mevac(i):
            wpe = pe_done[("mt", i)]
            def f(e, wpe=wpe):
                e.wait_ge(sems["pe"], wpe)
                e.tensor_copy(MA[:], psM[:, 0:128])
                e.tensor_copy(MB[0:32, :], psM[0:32, 128:256]).then_inc(sems["ve"], 1)
            op("vector", f)
            ve_done[("mev", i)] = P.inc("ve")

        def emit_B(i):
            wv = ve_done[("mev", i)]
            wx = se_done[("xln", i + 1)]
            wev = ve_done.get(("attnev", i - 1), 0)
            def f(e, i=i, wv=wv, wx=wx, wev=wev):
                e.wait_ge(sems["ve"], wv)
                e.wait_ge(sems["se"], wx)
                if wev:
                    e.wait_ge(sems["ve"], wev)
                for dm, ps in ((0, psB0), (1, psB1)):
                    e.matmul(ps[:], MA[:], XLN[i % 2][:, dm * 512:(dm + 1) * 512],
                             start=True, stop=False)
                    last = e.matmul(ps[:], MB[0:32, :],
                                    XLN[(i + 1) % 2][0:32, dm * 512:(dm + 1) * 512],
                                    start=False, stop=True)
                last.then_inc(sems["pe"], 1)
            op("tensor", f)
            pe_done[("band", i)] = P.inc("pe")

        def emit_attn_evac(i):
            wpe = pe_done[("band", i)]
            wv2 = ve_done.get(("evac2", i - 3), 0)
            def f(e, i=i, wpe=wpe, wv2=wv2):
                e.wait_ge(sems["pe"], wpe)
                if wv2:
                    e.wait_ge(sems["ve"], wv2)
                for dm, ps in ((0, psB0), (1, psB1)):
                    last = e.scalar_tensor_tensor(out=X2[i % 3][:, dm * 512:(dm + 1) * 512],
                                                  in0=ps[:], scalar=1.0,
                                                  in1=XT[i % 2][:, dm * 512:(dm + 1) * 512],
                                                  op0=ALU.mult, op1=ALU.add)
                last.then_inc(sems["ve"], 1)
            op("vector", f)
            ve_done[("attnev", i)] = P.inc("ve")

        def emit_chain_head(i):
            r = i % 3
            wev = ve_done[("attnev", i)]
            x2t = X2[i % 3]

            def s_sqB(e, x2t=x2t, r=r, wev=wev):
                e.wait_ge(sems["ve"], wev)
                e.activation(out=SCR16[:], in_=x2t[:], func=AF.Square,
                             bias=0.0, scale=1.0,
                             accum_out=col(r, CB_SQ)).then_inc(sems["se"], 1)
            op("scalar", s_sqB)
            se_sqB = P.inc("se")

            def v_stB(e, x2t=x2t, r=r, wev=wev, se_sqB=se_sqB):
                e.wait_ge(sems["ve"], wev)
                FN(e, e.reduce_sum(col(r, CB_SUM), x2t[:], axis=AX.X))
                VG(e, None)
                e.wait_ge(sems["se"], se_sqB)
                FN(e, e.tensor_scalar(out=col(r, CB_MEAN), in0=col(r, CB_SUM),
                                      scalar1=1.0 / D, scalar2=None, op0=ALU.mult))
                VG(e, None)
                FN(e, e.tensor_tensor(col(r, CB_MSQ), col(r, CB_MEAN),
                                      col(r, CB_MEAN), ALU.mult))
                VG(e, None)
                FN(e, e.tensor_scalar(out=col(r, CB_MSQE), in0=col(r, CB_MSQ),
                                      scalar1=1.0, scalar2=1e-5,
                                      op0=ALU.mult, op1=ALU.subtract))
                VG(e, None)
                e.tensor_scalar(out=col(r, CB_VAR), in0=col(r, CB_SQ),
                                scalar1=1.0 / D, scalar2=col(r, CB_MSQE),
                                op0=ALU.mult, op1=ALU.subtract).then_inc(sems["ve"], 1)
            op("vector", v_stB)
            ve_vB = P.inc("ve")

            def s_sdB(e, r=r, ve_vB=ve_vB):
                e.wait_ge(sems["ve"], ve_vB)
                e.activation(out=col(r, CB_STD), in_=col(r, CB_VAR), func=AF.Sqrt,
                             bias=0.0, scale=1.0).then_inc(sems["se"], 1)
            op("scalar", s_sdB)
            se_sdB = P.inc("se")

            def v_invB(e, r=r, se_sdB=se_sdB):
                e.wait_ge(sems["se"], se_sdB)
                FN(e, e.reciprocal(col(r, CB_INV), col(r, CB_STD)))
                VG(e, None)
                e.tensor_scalar(out=col(r, CB_NMI), in0=col(r, CB_MEAN), scalar1=-1.0,
                                scalar2=col(r, CB_INV), op0=ALU.mult, op1=ALU.mult).then_inc(sems["ve"], 1)
            op("vector", v_invB)
            ve_invB = P.inc("ve")

            def s_z(e, x2t=x2t, r=r, ve_invB=ve_invB):
                e.wait_ge(sems["ve"], ve_invB)
                e.activation(out=ZY[:], in_=x2t[:], func=AF.Identity,
                             bias=col(r, CB_NMI), scale=col(r, CB_INV)
                             ).then_inc(sems["se"], 1)
            op("scalar", s_z)
            se_z = P.inc("se")

            def v_b2(e, x2t=x2t, se_z=se_z):
                e.wait_ge(sems["se"], se_z)
                e.tensor_tensor(x2t[:], x2t[:], B2[:], ALU.add).then_inc(sems["ve"], 1)
            op("vector", v_b2)
            P.inc("ve")

            def s_sqG(e, r=r, se_z=se_z):
                e.activation(out=SCR16[:], in_=ZY[:], func=AF.Square,
                             bias=0.0, scale=1.0,
                             accum_out=col(r, CG_SQ)).then_inc(sems["se"], 1)
            op("scalar", s_sqG)
            se_sqG = P.inc("se")

            def v_stG(e, r=r, se_sqG=se_sqG):
                e.wait_ge(sems["se"], se_sqG)
                FN(e, e.reduce_sum(col(r, CG_SUM), ZY[:], axis=AX.X))
                VG(e, None)
                FN(e, e.tensor_scalar(out=col(r, CG_M), in0=col(r, CG_SUM),
                                      scalar1=1.0 / D, scalar2=None, op0=ALU.mult))
                VG(e, None)
                FN(e, e.tensor_tensor(col(r, CG_MSQ), col(r, CG_M),
                                      col(r, CG_M), ALU.mult))
                VG(e, None)
                FN(e, e.tensor_scalar(out=col(r, CG_MSQE), in0=col(r, CG_MSQ),
                                      scalar1=1.0, scalar2=1e-5,
                                      op0=ALU.mult, op1=ALU.subtract))
                VG(e, None)
                e.tensor_scalar(out=col(r, CG_VAR), in0=col(r, CG_SQ),
                                scalar1=1.0 / D, scalar2=col(r, CG_MSQE),
                                op0=ALU.mult, op1=ALU.subtract).then_inc(sems["ve"], 1)
            op("vector", v_stG)
            ve_vG = P.inc("ve")

            def s_sdG(e, r=r, ve_vG=ve_vG):
                e.wait_ge(sems["ve"], ve_vG)
                e.activation(out=col(r, CG_STD), in_=col(r, CG_VAR), func=AF.Sqrt,
                             bias=0.0, scale=1.0).then_inc(sems["se"], 1)
            op("scalar", s_sdG)
            se_sdG = P.inc("se")

            def v_ln1a(e, r=r, se_sdG=se_sdG):
                e.wait_ge(sems["se"], se_sdG)
                FN(e, e.reciprocal(col(r, CG_INV), col(r, CG_STD)))
                e.scalar_tensor_tensor(out=ZY[:], in0=ZY[:], scalar=col(r, CG_M),
                                       in1=G1[:], op0=ALU.subtract, op1=ALU.mult).then_inc(sems["ve"], 1)
            op("vector", v_ln1a)
            ve_ln1a = P.inc("ve")

            def s_ln1b(e, r=r, ve_ln1a=ve_ln1a):
                e.wait_ge(sems["ve"], ve_ln1a)
                e.activation(out=ZY[:], in_=ZY[:], func=AF.Identity,
                             bias=0.0, scale=col(r, CG_INV)).then_inc(sems["se"], 1)
            op("scalar", s_ln1b)
            se_ln1b = P.inc("se")

            wdt1 = dt_done.get(("q1t", i - 1), 0)
            def v_ln1c(e, r=r, se_ln1b=se_ln1b, wdt1=wdt1):
                e.wait_ge(sems["se"], se_ln1b)
                e.tensor_tensor(ZY[:], ZY[:], B1LN[:], ALU.add)
                FN(e, e.reduce_max(col(r, CQ1_MXR), ZY[:], axis=AX.X,
                                   apply_absolute_value=True))
                VG(e, None)
                FN(e, e.tensor_scalar(out=col(r, CQ1_MX), in0=col(r, CQ1_MXR),
                                      scalar1=1e-5, scalar2=None, op0=ALU.max))
                VG(e, None)
                FN(e, e.reciprocal(col(r, CQ1_T), col(r, CQ1_MX)))
                VG(e, None)
                FN(e, e.tensor_scalar(out=col(r, CQ1_S), in0=col(r, CQ1_T), scalar1=127.0,
                                      scalar2=0.0, op0=ALU.mult, op1=ALU.add))
                FN(e, e.tensor_scalar(out=col(r, CQ1_G), in0=col(r, CQ1_MX),
                                      scalar1=MW1[:, 0:1], scalar2=INV127[:, 0:1],
                                      op0=ALU.mult, op1=ALU.mult))
                if wdt1:
                    e.wait_ge(sems["dt1"], wdt1)
                e.tensor_scalar(out=ZY[:], in0=ZY[:], scalar1=col(r, CQ1_S),
                                scalar2=12582912.0, op0=ALU.mult, op1=ALU.add)
                e.tensor_scalar(out=Q1[:], in0=ZY[:], scalar1=12582912.0,
                                scalar2=None, op0=ALU.subtract).then_inc(sems["ve"], 1)
            op("vector", v_ln1c)
            ve_done[("round1", i)] = P.inc("ve")

        def emit_q1T(i):
            wv = ve_done[("round1", i)]
            wpe = pe_done.get(("c7", i - 1), 0)
            wdb = P.val("db")
            def f(e, i=i, wv=wv, wpe=wpe, wdb=wdb):
                e.wait_ge(sems["ve"], wv)
                if wpe:
                    e.wait_ge(sems["pe"], wpe)
                if wdb:
                    e.wait_ge(sems["db"], wdb)
                for g in range(8):
                    e.dma_start_transpose(Q1T[:, g * 128:(g + 1) * 128],
                                          Q1[:, g * 128:(g + 1) * 128]
                                          ).then_inc(sems["dt1"], 16)
                e.dma_start_transpose(Q1T[:, 0:128],
                                      Q1[:, 0:128]).then_inc(sems["dt1"], 16)
            op("sync", f)
            dt_done[("q1t", i)] = P.inc("dt1", 16 * 9)
            if DBGOUT == 10:
                wv2 = ve_done[("round1", i)]
                def fd(e, i=i, wv2=wv2):
                    e.wait_ge(sems["ve"], wv2)
                    e.dma_start(out=out_dr.ap()[i * 128:(i + 1) * 128, :],
                                in_=ZY[:]).then_inc(sems["do"], 16)
                op("sync", fd)
                P.inc("do", 16)

        def emit_C(i):
            wdt = dt_done[("q1t", i)]
            w_r2 = ve_done.get(("round2", i - 1), 0)
            for hc in range(8):
                ps = psC0 if hc % 2 == 0 else psC1
                wev = ve_done.get(("evac1", i, hc - 2), 0)
                if hc == 0:
                    wev = ve_done.get(("evac1", i - 1, 6), 0)
                elif hc == 1:
                    wev = ve_done.get(("evac1", i - 1, 7), 0)
                def f(e, i=i, hc=hc, ps=ps, wdt=wdt, wev=wev):
                    if hc == 0:
                        e.wait_ge(sems["dt1"], wdt)
                    if wev:
                        e.wait_ge(sems["ve"], wev)
                    for g in range(8):
                        last = e.matmul(ps[:], Q1T[:, g * 128:(g + 1) * 128],
                                        W1T[g][:, hc * 512:(hc + 1) * 512],
                                        start=(g == 0), stop=(g == 7))
                    last.then_inc(sems["pe"], 1)
                op("tensor", f)
                pe_done[("c", i, hc)] = P.inc("pe")
                if hc == 7:
                    pe_done[("c7", i)] = pe_done[("c", i, hc)]

                r = i % 3
                wpe = pe_done[("c", i, hc)]
                def fv(e, i=i, hc=hc, ps=ps, r=r, wpe=wpe, w_r2=w_r2):
                    e.wait_ge(sems["pe"], wpe)
                    if hc == 0 and w_r2:
                        e.wait_ge(sems["ve"], w_r2)
                    e.scalar_tensor_tensor(out=GEL[:, hc * 512:(hc + 1) * 512],
                                           in0=ps[:], scalar=col(r, CQ1_G),
                                           in1=B1[:, hc * 512:(hc + 1) * 512],
                                           op0=ALU.mult, op1=ALU.add).then_inc(sems["ve"], 1)
                op("vector", fv)
                ve_done[("evac1", i, hc)] = P.inc("ve")

                wv1 = ve_done[("evac1", i, hc)]
                def fs(e, i=i, hc=hc, wv1=wv1):
                    e.wait_ge(sems["ve"], wv1)
                    e.activation(out=GEL[:, hc * 512:(hc + 1) * 512],
                                 in_=GEL[:, hc * 512:(hc + 1) * 512], func=AF.Gelu,
                                 bias=0.0, scale=1.0,
                                 accum_out=GSUM[i % 2][:, hc:hc + 1]).then_inc(sems["se"], 1)
                op("scalar", fs)
                se_done[("gelu", i, hc)] = P.inc("se")
            if DBGOUT == 13:
                ws2 = se_done[("gelu", i, 7)]
                def fd(e, i=i, ws2=ws2):
                    e.wait_ge(sems["se"], ws2)
                    e.dma_start(out=out_dr.ap()[i * 128:(i + 1) * 128, :],
                                in_=GEL[:, 0:D]).then_inc(sems["do"], 16)
                op("sync", fd)
                P.inc("do", 16)

        def emit_chain_mid(i):
            r = i % 3
            wse = se_done[("gelu", i, 7)]
            wdt2 = dt_done.get(("q2t", i - 1), 0)

            def s_sqH(e, i=i, r=r, wse=wse, wdt2=wdt2):
                e.wait_ge(sems["se"], wse)
                if wdt2:
                    e.wait_ge(sems["dt2"], wdt2)
                FN(e, e.activation(out=Q2[:, 0:2048], in_=GEL[:, 0:2048], func=AF.Square,
                                   bias=0.0, scale=1.0, accum_out=col(r, CH_A0)))
                e.activation(out=Q2[:, 2048:4096], in_=GEL[:, 2048:4096], func=AF.Square,
                             bias=0.0, scale=1.0, accum_out=col(r, CH_A1)
                             ).then_inc(sems["se"], 1)
            op("scalar", s_sqH)
            se_sqH = P.inc("se")
            if MCUT < 2:
                ve_done[("round2", i)] = 0
                return

            def v_stH(e, i=i, r=r, se_sqH=se_sqH):
                e.wait_ge(sems["se"], se_sqH)
                FN(e, e.reduce_sum(col(r, CH_SUM), GSUM[i % 2][:], axis=AX.X))
                FN(e, e.tensor_tensor(col(r, CH_SQ), col(r, CH_A0), col(r, CH_A1), ALU.add))
                FN(e, e.tensor_scalar(out=col(r, CH_M), in0=col(r, CH_SUM),
                                      scalar1=1.0 / H, scalar2=None, op0=ALU.mult))
                VG(e, None)
                FN(e, e.tensor_tensor(col(r, CH_MSQ), col(r, CH_M), col(r, CH_M), ALU.mult))
                VG(e, None)
                FN(e, e.tensor_scalar(out=col(r, CH_MSQE), in0=col(r, CH_MSQ),
                                      scalar1=1.0, scalar2=1e-5,
                                      op0=ALU.mult, op1=ALU.subtract))
                VG(e, None)
                e.tensor_scalar(out=col(r, CH_VAR), in0=col(r, CH_SQ),
                                scalar1=1.0 / H, scalar2=col(r, CH_MSQE),
                                op0=ALU.mult, op1=ALU.subtract).then_inc(sems["ve"], 1)
            op("vector", v_stH)
            ve_vH = P.inc("ve")

            def s_sdH(e, r=r, ve_vH=ve_vH):
                e.wait_ge(sems["ve"], ve_vH)
                e.activation(out=col(r, CH_STD), in_=col(r, CH_VAR), func=AF.Sqrt,
                             bias=0.0, scale=1.0).then_inc(sems["se"], 1)
            op("scalar", s_sdH)
            se_sdH = P.inc("se")
            if MCUT < 3:
                ve_done[("round2", i)] = 0
                return

            def v_ln2a(e, r=r, se_sdH=se_sdH):
                e.wait_ge(sems["se"], se_sdH)
                FN(e, e.reciprocal(col(r, CH_INV), col(r, CH_STD)))
                e.scalar_tensor_tensor(out=GEL[:], in0=GEL[:], scalar=col(r, CH_M),
                                       in1=G2[:], op0=ALU.subtract, op1=ALU.mult).then_inc(sems["ve"], 1)
            op("vector", v_ln2a)
            ve_ln2a = P.inc("ve")
            if MCUT == 4:
                ve_done[("round2", i)] = 0
                return

            def s_ln2b(e, r=r, ve_ln2a=ve_ln2a):
                e.wait_ge(sems["ve"], ve_ln2a)
                e.activation(out=GEL[:], in_=GEL[:], func=AF.Identity,
                             bias=0.0, scale=col(r, CH_INV)).then_inc(sems["se"], 1)
            op("scalar", s_ln2b)
            se_ln2b = P.inc("se")
            if MCUT == 5:
                ve_done[("round2", i)] = 0
                return

            def v_ln2c(e, r=r, se_ln2b=se_ln2b):
                e.wait_ge(sems["se"], se_ln2b)
                e.tensor_tensor(GEL[:], GEL[:], B2LN[:], ALU.add)
                FN(e, e.reduce_max(col(r, CQ2_MXR), GEL[:], axis=AX.X,
                                   apply_absolute_value=True))
                VG(e, None)
                FN(e, e.tensor_scalar(out=col(r, CQ2_MX), in0=col(r, CQ2_MXR),
                                      scalar1=1e-5, scalar2=None, op0=ALU.max))
                VG(e, None)
                FN(e, e.reciprocal(col(r, CQ2_T), col(r, CQ2_MX)))
                VG(e, None)
                FN(e, e.tensor_scalar(out=col(r, CQ2_S), in0=col(r, CQ2_T), scalar1=127.0,
                                      scalar2=0.0, op0=ALU.mult, op1=ALU.add))
                FN(e, e.tensor_scalar(out=col(r, CQ2_G), in0=col(r, CQ2_MX),
                                      scalar1=MW2[:, 0:1], scalar2=INV127[:, 0:1],
                                      op0=ALU.mult, op1=ALU.mult))

                for qq in range(4):
                    e.tensor_scalar(out=GEL[:, qq * D:(qq + 1) * D],
                                    in0=GEL[:, qq * D:(qq + 1) * D],
                                    scalar1=col(r, CQ2_S), scalar2=12582912.0,
                                    op0=ALU.mult, op1=ALU.add)
                    last = e.tensor_scalar(out=Q2[:, qq * D:(qq + 1) * D],
                                           in0=GEL[:, qq * D:(qq + 1) * D],
                                           scalar1=12582912.0, scalar2=None, op0=ALU.subtract)
                last.then_inc(sems["ve"], 1)
            op("vector", v_ln2c)
            ve_done[("round2", i)] = P.inc("ve")

        def emit_q2T(i):
            wv = ve_done[("round2", i)]
            wpe = pe_done.get(("d1", i - 1), 0)
            wdb = P.val("db")
            def f(e, i=i, wv=wv, wpe=wpe, wdb=wdb):
                e.wait_ge(sems["ve"], wv)
                if wpe:
                    e.wait_ge(sems["pe"], wpe)
                if wdb:
                    e.wait_ge(sems["db"], wdb)
                for j in range(32):
                    e.dma_start_transpose(Q2T[:, j * 128:(j + 1) * 128],
                                          Q2[:, j * 128:(j + 1) * 128]
                                          ).then_inc(sems["dt2"], 16)
                e.dma_start_transpose(Q2T[:, 0:128],
                                      Q2[:, 0:128]).then_inc(sems["dt2"], 16)
            op("scalar", f)
            dt_done[("q2t", i)] = P.inc("dt2", 16 * 33)
            if DBGOUT in (11, 12):
                wv2 = ve_done[("round2", i)]
                off = 0 if DBGOUT == 11 else 1024
                def fd(e, i=i, wv2=wv2, off=off):
                    e.wait_ge(sems["ve"], wv2)
                    e.dma_start(out=out_dr.ap()[i * 128:(i + 1) * 128, :],
                                in_=GEL[:, off:off + D]).then_inc(sems["do"], 16)
                op("sync", fd)
                P.inc("do", 16)

        def emit_D(i):
            wdt = dt_done[("q2t", i)]
            wev = ve_done.get(("evac2", i - 1), 0)
            for dc, ps in [(0, psD0), (1, psD1)][:DDC]:
                nsub = (DJN + 7) // 8
                base = P.val("pe")
                def f(e, i=i, dc=dc, ps=ps, wdt=wdt, wev=wev, base=base, nsub=nsub):
                    if dc == 0:
                        e.wait_ge(sems["dt2"], wdt)
                        if wev:
                            e.wait_ge(sems["ve"], wev)
                    for sub in range(nsub):
                        if sub >= 2:
                            e.wait_ge(sems["pe"], base + sub - 1)
                        for jj in range(8):
                            j = sub * 8 + jj
                            if j >= DJN:
                                break
                            last = e.matmul(ps[:], Q2T[:, j * 128:(j + 1) * 128],
                                            W2T[j][:, dc * 512:(dc + 1) * 512],
                                            start=(j == 0), stop=(j == DJN - 1))
                        last.then_inc(sems["pe"], 1)
                op("tensor", f)
                for _ in range(nsub):
                    pe_done[("d", i, dc)] = P.inc("pe")
            pe_done[("d1", i)] = pe_done[("d", i, DDC - 1)]

        def emit_evac2(i):
            r = i % 3
            wpe = pe_done[("d", i, DDC - 1)]
            wdo = do_done.get(i - 1, 0)
            def f(e, i=i, r=r, wpe=wpe, wdo=wdo):
                e.wait_ge(sems["pe"], wpe)
                if wdo:
                    e.wait_ge(sems["do"], wdo)
                for dc, ps in [(0, psD0), (1, psD1)][:DDC]:
                    last = e.scalar_tensor_tensor(out=OUT_T[:, dc * 512:(dc + 1) * 512],
                                                  in0=ps[:], scalar=col(r, CQ2_G),
                                                  in1=X2[i % 3][:, dc * 512:(dc + 1) * 512],
                                                  op0=ALU.mult, op1=ALU.add)
                last.then_inc(sems["ve"], 1)
            op("vector", f)
            ve_done[("evac2", i)] = P.inc("ve")

        def emit_store(i):
            wv = ve_done[("evac2", i)]
            def f(e, i=i, wv=wv):
                e.wait_ge(sems["ve"], wv)
                e.dma_start(out=out_dr.ap()[i * 128:(i + 1) * 128, :],
                            in_=OUT_T[:]).then_inc(sems["do"], 16)
            op("sync", f)
            do_done[i] = P.inc("do", 16)

        def emit_store_debug_x2(i):
            wv = ve_done[("attnev", i)]
            def f(e, i=i, wv=wv):
                e.wait_ge(sems["ve"], wv)
                if DBGOUT == 1:
                    e.dma_start(out=out_dr.ap()[i * 128:(i + 1) * 128, :],
                                in_=XT[i % 2][:]).then_inc(sems["do"], 16)
                elif DBGOUT == 2:
                    e.dma_start(out=out_dr.ap()[i * 128:(i + 1) * 128, :],
                                in_=XLN[i % 2][:]).then_inc(sems["do"], 16)
                elif DBGOUT == 3:
                    e.dma_start(out=out_dr.ap()[i * 128:(i + 1) * 128, 0:K],
                                in_=WP[:]).then_inc(sems["do"], 16)
                elif DBGOUT == 4:
                    e.dma_start(out=out_dr.ap()[i * 128:(i + 1) * 128, 0:WROW],
                                in_=MT_SB[:]).then_inc(sems["do"], 16)
                elif DBGOUT == 8:
                    e.dma_start(out=out_dr.ap()[i * 128:i * 128 + 32, 0:128],
                                in_=MB[0:32, :]).then_inc(sems["do"], 16)
                elif DBGOUT == 7:
                    e.dma_start(out=out_dr.ap()[i * 128:(i + 1) * 128, 0:128],
                                in_=MA[:]).then_inc(sems["do"], 16)
                elif DBGOUT == 6:
                    e.dma_start(out=out_dr.ap()[i * 128:(i + 1) * 128, 0:NCOLS],
                                in_=ST[i % 3][:]).then_inc(sems["do"], 16)
                else:
                    e.dma_start(out=out_dr.ap()[i * 128:(i + 1) * 128, :],
                                in_=X2[i % 3][:]).then_inc(sems["do"], 16)
            op("sync", f)
            do_done[i] = P.inc("do", 16)

        for k in (range(-2, min(NT + 3, MAXCYC)) if not DEBUG_PHASE0_ONLY else []):
            if 0 <= k - 3 <= NT - 1 and not DEBUG_ATTN_ONLY and FCUT >= 6:
                emit_D(k - 3)
                if FCUT >= 7:
                    emit_evac2(k - 3)
                    if FCUT >= 8:
                        emit_store(k - 3)
            if 0 <= k <= NT - 1 and ACUT >= 7:
                emit_Mevac(k)
            if k < 0 and 0 <= k + 2 <= NTL - 1:
                emit_load_x(k + 2)
            if 0 <= k + 1 <= NTL - 1 and ACUT >= 2:
                emit_stats_alpha(k + 1)
                if ACUT >= 3:
                    emit_xln(k + 1)
            if 0 <= k - 2 <= NT - 1 and not DEBUG_ATTN_ONLY and FCUT >= 4:
                emit_chain_mid(k - 2)
                if FCUT >= 5 and MCUT >= 3:
                    emit_q2T(k - 2)
            if 0 <= k + 1 <= NT - 1 and ACUT >= 4:
                emit_A(k + 1)
                if ACUT >= 5:
                    emit_band_dma(k + 1)
            if 0 <= k <= NT - 1 and ACUT >= 7:
                emit_B(k)
                emit_attn_evac(k)
                if DEBUG_ATTN_ONLY or FCUT < 8:
                    emit_store_debug_x2(k)
            if False:
                pass
            if k >= 0 and 0 <= k + 2 <= NTL - 1:
                emit_load_x(k + 2)
            if 0 <= k - 1 <= NT - 1 and not DEBUG_ATTN_ONLY and FCUT >= 3:
                emit_C(k - 1)
            if 0 <= k <= NT - 1 and not DEBUG_ATTN_ONLY and FCUT >= 1:
                emit_chain_head(k)
                if FCUT >= 2:
                    emit_q1T(k)
            if 0 <= k + 1 <= NT - 1 and ACUT >= 6:
                emit_MT(k + 1)
            if SERIAL:
                barrier()

        do_total = P.val("do")
        op("sync", lambda e, v=do_total: e.wait_ge(sems["do"], v))

        with nc.Block() as block:
            @block.sync
            def _(eng):
                for f in P.ops["sync"]:
                    f(eng)

            @block.vector
            def _(eng):
                for f in P.ops["vector"]:
                    f(eng)

            @block.scalar
            def _(eng):
                for f in P.ops["scalar"]:
                    f(eng)

            @block.tensor
            def _(eng):
                for f in P.ops["tensor"]:
                    f(eng)

            @block.gpsimd
            def _(eng):
                for f in P.ops["gpsimd"]:
                    f(eng)

    nc._dbg_plan = dict(P.cnt)
    return nc


def _bcast128(v):
    v = np.asarray(v, np.float32)
    return np.ascontiguousarray(np.broadcast_to(v[None, :], (128, v.shape[0])))


def _shard_inputs(L0_attn_x, L0_attn_W, L0_attn_b, L0_ffn_ln1_g, L0_ffn_ln1_b,
                  L0_ffn_w1, L0_ffn_b1, L0_ffn_ln2_g, L0_ffn_ln2_b, L0_ffn_w2,
                  L0_ffn_b2):
    x = np.ascontiguousarray(np.asarray(L0_attn_x, dtype=np.float32))
    B, L, Dm = x.shape
    shared = dict(
        w1t=np.ascontiguousarray(np.asarray(L0_ffn_w1, np.float32).T),
        w2t=np.ascontiguousarray(np.asarray(L0_ffn_w2, np.float32).T),
        wat=np.ascontiguousarray(np.asarray(L0_attn_W, np.float32).T),
        battn=_bcast128(L0_attn_b),
        g1=_bcast128(L0_ffn_ln1_g),
        b1ln=_bcast128(L0_ffn_ln1_b),
        g2=_bcast128(L0_ffn_ln2_g),
        b2ln=_bcast128(L0_ffn_ln2_b),
        b1=_bcast128(L0_ffn_b1),
        b2=_bcast128(L0_ffn_b2),
    )
    in_maps = []
    for c in range(8):
        b = c // 2
        half = c % 2
        s = half * SEQ
        xs = np.zeros((NTL * 128, Dm), np.float32)
        avail = min(NTL * 128, L - s)
        xs[:avail] = x[b, s:s + avail]
        m = dict(shared)
        m["x"] = xs
        in_maps.append(m)
    return in_maps


def kernel(**inputs):
    in_maps = _shard_inputs(**inputs)
    nc = build_program()
    res = run_bass_kernel_spmd(nc, in_maps, core_ids=list(range(8)))
    out = np.empty((4, 8192, 1024), np.float32)
    for c in range(8):
        b = c // 2
        half = c % 2
        out[b, half * SEQ:(half + 1) * SEQ] = np.asarray(res.results[c]["out"],
                                                         np.float32)
    return out


# version 09
# speedup vs baseline: 1.2194x; 1.2194x over previous
import contextlib
import numpy as np
import concourse.bass as bass
import concourse.mybir as mybir
from concourse.bass_utils import run_bass_kernel_spmd

F32 = mybir.dt.float32
F16 = mybir.dt.float16
BF16 = mybir.dt.bfloat16
FP8 = mybir.dt.float8e4
AF = mybir.ActivationFunctionType
ALU = mybir.AluOpType
AX = mybir.AxisListType

D = 1024
H = 4096
K = 32
NT = 32
NTL = 33
SEQ = 4096
WROW = 160

SERIAL = False
DEBUG_ATTN_ONLY = False
DEBUG_PHASE0_ONLY = False
import os as _os
P0_CUT = int(_os.environ.get('P0_CUT', '9'))
MAXCYC = int(_os.environ.get('MAXCYC', '99'))
ACUT = int(_os.environ.get('ACUT', '9'))
AIN = int(_os.environ.get('AIN', '9'))
DBGOUT = int(_os.environ.get('DBGOUT', '5'))
FCUT = int(_os.environ.get('FCUT', '9'))
DJN = int(_os.environ.get('DJN', '32'))
MCUT = int(_os.environ.get('MCUT', '9'))
DDC = int(_os.environ.get('DDC', '2'))

(CA_SUM, CA_SQ, CA_MEAN, CA_MSQ, CA_MSQE, CA_VAR, CA_STD, CA_INV, CA_NMI,
 CB_SUM, CB_SQ, CB_MEAN, CB_MSQ, CB_MSQE, CB_VAR, CB_STD, CB_INV, CB_NMI,
 CG_SUM, CG_SQ, CG_M, CG_MSQ, CG_MSQE, CG_VAR, CG_STD, CG_INV,
 CH_A0, CH_A1, CH_SUM, CH_SQ, CH_M, CH_MSQ, CH_MSQE, CH_VAR, CH_STD, CH_INV,
 CQ1_MXR, CQ1_MX, CQ1_T, CQ1_S, CQ1_G,
 CQ2_MXR, CQ2_MX, CQ2_T, CQ2_S, CQ2_G,
 CS_NEGMX, CS_Z, CS_RZ) = range(49)
NCOLS = 50


class Plan:
    ENGINES = ("sync", "vector", "scalar", "tensor", "gpsimd")

    def __init__(self):
        self.ops = {e: [] for e in self.ENGINES}
        self.cnt = {}

    def inc(self, sem, by=1):
        self.cnt[sem] = self.cnt.get(sem, 0) + by
        return self.cnt[sem]

    def val(self, sem):
        return self.cnt.get(sem, 0)

    def add(self, engine, fn):
        self.ops[engine].append(fn)


def build_program():
    nc = bass.Bass("TRN2", target_bir_lowering=False,
                   detect_race_conditions=(_os.environ.get('RACED', '0') == '1'))

    x_in = nc.dram_tensor("x", [NTL * 128, D], F32, kind="ExternalInput")
    w1t_in = nc.dram_tensor("w1t", [D, H], F32, kind="ExternalInput")
    w2t_in = nc.dram_tensor("w2t", [H, D], F32, kind="ExternalInput")
    wat_in = nc.dram_tensor("wat", [D, K], F32, kind="ExternalInput")
    ba_in = nc.dram_tensor("battn", [128, K], F32, kind="ExternalInput")
    g1_in = nc.dram_tensor("g1", [128, D], F32, kind="ExternalInput")
    b1ln_in = nc.dram_tensor("b1ln", [128, D], F32, kind="ExternalInput")
    g2_in = nc.dram_tensor("g2", [128, H], F32, kind="ExternalInput")
    b2ln_in = nc.dram_tensor("b2ln", [128, H], F32, kind="ExternalInput")
    b1_in = nc.dram_tensor("b1", [128, H], F32, kind="ExternalInput")
    b2_in = nc.dram_tensor("b2", [128, D], F32, kind="ExternalInput")
    out_dr = nc.dram_tensor("out", [SEQ, D], F32, kind="ExternalOutput")
    whbm = nc.dram_tensor("whbm", [2 * (128 * WROW + 64)], F32, kind="Internal")

    identf_dr = nc.inline_tensor(np.eye(128, dtype=np.float32), name="identf_c")

    ctx = contextlib.ExitStack()

    def mk_sb(name, shape, dtype):
        return ctx.enter_context(nc.sbuf_tensor(name, shape, dtype))

    def mk_ps(name):
        return ctx.enter_context(nc.psum_tensor(name, [128, 512], F32))

    with ctx:
        W1T = [mk_sb(f"w1t_{g}", [128, H], FP8) for g in range(8)]
        W2T = [mk_sb(f"w2t_{j}", [128, D], FP8) for j in range(32)]
        WT = mk_sb("wt", [128, 8 * K], F32)
        G1 = mk_sb("g1t", [128, D], F32)
        B1LN = mk_sb("b1lnt", [128, D], F32)
        G2 = mk_sb("g2t", [128, H], F32)
        B2LN = mk_sb("b2lnt", [128, H], F32)
        B1 = mk_sb("b1t", [128, H], F32)
        B2 = mk_sb("b2t", [128, D], F32)
        BATB = mk_sb("batb", [128, K], F32)
        IDENT = mk_sb("ident", [128, 128], F32)
        ONES128 = mk_sb("ones128", [128, 128], F32)
        ONECOL = mk_sb("onecol", [128, 1], F32)
        EPS = mk_sb("epst", [128, 1], F32)
        C127 = mk_sb("c127", [128, 1], F32)
        INV127 = mk_sb("inv127", [128, 1], F32)
        MW1 = mk_sb("mw1", [128, 1], F32)
        MW2 = mk_sb("mw2", [128, 1], F32)
        SCL1 = mk_sb("scl1", [128, 1], F32)
        SCL2 = mk_sb("scl2", [128, 1], F32)
        WPART = mk_sb("wpart", [128, 68], F32)
        ZEROT = mk_sb("zerot", [128, 128], F32)
        SPACER = mk_sb("spacer", [128, 128], F32)

        XT = [mk_sb(f"x_{r}", [128, D], F32) for r in range(2)]
        XLN = [mk_sb(f"xln_{r}", [128, D], F32) for r in range(2)]
        SCR16 = mk_sb("scr16", [128, D], F16)
        if _os.environ.get('XLNFM_REAL', '0') == '1':
            XLNFM = mk_sb("xlnfm", [128, 512], F32)[:]
        else:
            XLNFM = SCR16[:].bitcast(F32)
        LG = mk_sb("lg", [32, 128], F32)
        LOGITS = mk_sb("logits", [128, K], F32)
        WP = mk_sb("wp", [128, K], F32)
        MT_SB = mk_sb("mt_sb", [128, WROW], F32)
        MA = mk_sb("ma", [128, 128], F32)
        MB = mk_sb("mb", [32, 128], F32)
        X2 = [mk_sb(f"x2_{r}", [128, D], F32) for r in range(3)]
        ZY = mk_sb("zy", [128, D], F32)
        Q1 = mk_sb("q1", [128, D], BF16)
        Q1T = mk_sb("q1t", [128, D], BF16)
        Q2 = mk_sb("q2", [128, H], BF16)
        Q2T = mk_sb("q2t", [128, H], BF16)
        GEL = mk_sb("gel", [128, H], F32)
        OUT_T = mk_sb("out_t", [128, D], F32)
        ST = [mk_sb(f"st_{r}", [128, NCOLS], F32) for r in range(3)]
        GSUM = [mk_sb(f"gsum_{r}", [128, 8], F32) for r in range(2)]

        psA = mk_ps("psA")
        psM = mk_ps("psM")
        psB0 = mk_ps("psB0")
        psB1 = mk_ps("psB1")
        psC0 = mk_ps("psC0")
        psC1 = mk_ps("psC1")
        psD0 = mk_ps("psD0")
        psD1 = mk_ps("psD1")

        sems = {}
        for s in ("dx", "dw", "dt1", "dt2", "db", "do", "pe", "ve", "se", "bar",
                  "vh", "fn"):
            sems[s] = ctx.enter_context(nc.semaphore(s))

        P = Plan()
        pe_done, ve_done, se_done, dt_done, db_done, dx_done, do_done = \
            {}, {}, {}, {}, {}, {}, {}

        def op(eng, fn):
            P.add(eng, fn)

        vhc = [0]

        def VG(e, inst):
            e.tensor_copy(SPACER[:], ZEROT[:])

        def FN(e, inst):
            inst.then_inc(sems["fn"], 1)

        def col(r, c):
            return ST[r][:, c:c + 1]

        def barrier():
            snap = {s: P.val(s) for s in
                    ("pe", "ve", "se", "dx", "dw", "dt1", "dt2", "db", "do")}
            for eng in Plan.ENGINES:
                def b(e, snap=snap):
                    for s, v in snap.items():
                        if v > 0:
                            e.wait_ge(sems[s], v)
                op(eng, b)

        def ph0_memsets(e):
            e.memset(ONES128[:], 1.0)
            FN(e, e.memset(WPART[:, 64:68], 0.0))
            FN(e, e.memset(ONECOL[:], 1.0))
            FN(e, e.memset(EPS[:], 1e-5))
            e.memset(ZEROT[:], 0.0)
            FN(e, e.memset(C127[:], 127.0))
            VG(e, None)
            FN(e, e.reciprocal(INV127[:], C127[:]))
            e.tensor_copy(SPACER[:], ZEROT[:]).then_inc(sems["ve"], 1)
        op("vector", ph0_memsets)
        ve_ms = P.inc("ve")

        def ph0_loads(e):
            e.dma_start(out=IDENT[:], in_=identf_dr.ap()).then_inc(sems["dw"], 16)
            for g in range(8):
                e.dma_start(out=WT[:, g * K:(g + 1) * K],
                            in_=wat_in.ap()[g * 128:(g + 1) * 128, :]).then_inc(sems["dw"], 16)
            e.dma_start(out=BATB[:], in_=ba_in.ap()).then_inc(sems["dw"], 16)
            for src_d, dst_t in ((g1_in, G1), (b1ln_in, B1LN), (b2_in, B2),
                                 (g2_in, G2), (b2ln_in, B2LN), (b1_in, B1)):
                e.dma_start(out=dst_t[:], in_=src_d.ap()).then_inc(sems["dw"], 16)
        op("sync", ph0_loads)
        P.inc("dw", 16 * 16)

        def ph0_whbm(e):
            e.wait_ge(sems["ve"], ve_ms)
            for r in range(2):
                e.dma_start(
                    out=bass.AP(whbm, r * (128 * WROW + 64) + 32, [[WROW, 128], [1, 128]]),
                    in_=ZEROT[:],
                ).then_inc(sems["db"], 16)
        op("gpsimd", ph0_whbm)
        P.inc("db", 32)

        op("tensor", lambda e: e.wait_ge(sems["ve"], ve_ms))

        def wpass1(nrowchunks, rowfree, src_dr, pcol0):
            npieces_per = rowfree // D
            pc = pcol0
            for cidx in range(nrowchunks):
                for si in range(npieces_per):
                    a = si * D
                    buf_off = ((cidx * npieces_per + si) % 4) * D
                    prev_se = P.val("se") - 3
                    def ld(e, cidx=cidx, a=a, buf_off=buf_off, src_dr=src_dr,
                           prev_se=prev_se):
                        if prev_se > 0:
                            e.wait_ge(sems["se"], prev_se)
                        e.dma_start(out=GEL[:, buf_off:buf_off + D],
                                    in_=src_dr.ap()[cidx * 128:(cidx + 1) * 128, a:a + D]
                                    ).then_inc(sems["dw"], 16)
                    op("sync", ld)
                    dwv = P.inc("dw", 16)
                    def ab(e, dwv=dwv + 16, buf_off=buf_off, pc=pc):
                        e.wait_ge(sems["dw"], dwv)
                        e.activation(out=SCR16[:], in_=GEL[:, buf_off:buf_off + D],
                                     func=AF.Abs, bias=0.0, scale=1.0,
                                     accum_out=WPART[:, pc:pc + 1]).then_inc(sems["se"], 1)
                    op("scalar", ab)
                    P.inc("se")
                    pc += 1

        if P0_CUT >= 3:
            wpass1(8, H, w1t_in, 0)
            wpass1(32, D, w2t_in, 32)

        sev_abs = P.val("se")

        def wcomb_v(e):
            e.wait_ge(sems["se"], sev_abs)
            FN(e, e.reduce_sum(WPART[:, 64:65], WPART[:, 0:32], axis=AX.X))
            e.reduce_sum(WPART[:, 65:66], WPART[:, 32:64], axis=AX.X).then_inc(sems["ve"], 1)
        if P0_CUT >= 4:
            op("vector", wcomb_v)
        vev_c = P.inc("ve") if P0_CUT >= 4 else 0

        def wcomb_t(e):
            e.wait_ge(sems["ve"], vev_c)
            e.matmul(psM[0:1, 0:1], WPART[:, 64:65], ONECOL[:, 0:1], start=True, stop=True)
            e.matmul(psM[0:1, 1:2], WPART[:, 65:66], ONECOL[:, 0:1], start=True, stop=True).then_inc(sems["pe"], 1)
        if P0_CUT >= 4:
            op("tensor", wcomb_t)
        pev_c = P.inc("pe") if P0_CUT >= 4 else 0

        def wmean_v(e):
            e.wait_ge(sems["pe"], pev_c)
            e.tensor_scalar(out=WPART[0:1, 66:68], in0=psM[0:1, 0:2],
                            scalar1=float(1.0 / (D * H)), scalar2=1e-5,
                            op0=ALU.mult, op1=ALU.max).then_inc(sems["ve"], 1)
        if P0_CUT >= 4:
            op("vector", wmean_v)
        vev_m = P.inc("ve") if P0_CUT >= 4 else 0

        def wbc_t(e):
            e.wait_ge(sems["ve"], vev_m)
            e.matmul(psM[:, 2:3], ONES128[:], WPART[:, 66:67], start=True, stop=True)
            e.matmul(psM[:, 3:4], ONES128[:], WPART[:, 67:68], start=True, stop=True).then_inc(sems["pe"], 1)
        if P0_CUT >= 4:
            op("tensor", wbc_t)
        pev_b = P.inc("pe") if P0_CUT >= 4 else 0

        def wbc_v(e):
            e.wait_ge(sems["pe"], pev_b)
            FN(e, e.tensor_copy(MW1[:], psM[:, 2:3]))
            FN(e, e.tensor_copy(MW2[:], psM[:, 3:4]))
            VG(e, None)
            FN(e, e.reciprocal(SCL1[:], MW1[:]))
            e.reciprocal(SCL2[:], MW2[:]).then_inc(sems["ve"], 1)
        if P0_CUT >= 4:
            op("vector", wbc_v)
        vev_scl = P.inc("ve") if P0_CUT >= 4 else 0

        def wpass2(nrowchunks, rowfree, src_dr, dst_fn):
            npieces_per = rowfree // D
            first_overall = True
            for cidx in range(nrowchunks):
                for si in range(npieces_per):
                    a = si * D
                    buf_off = ((cidx * npieces_per + si) % 4) * D
                    prev_ve = P.val("ve") - 3
                    def ld(e, cidx=cidx, a=a, buf_off=buf_off, src_dr=src_dr,
                           prev_ve=prev_ve):
                        if prev_ve > 0:
                            e.wait_ge(sems["ve"], prev_ve)
                        e.dma_start(out=GEL[:, buf_off:buf_off + D],
                                    in_=src_dr.ap()[cidx * 128:(cidx + 1) * 128, a:a + D]
                                    ).then_inc(sems["dw"], 16)
                    op("sync", ld)
                    dwv = P.inc("dw", 16)
                    dst_t, dst_off = dst_fn(cidx, a)
                    scl = SCL1 if src_dr is w1t_in else SCL2
                    fo = first_overall
                    first_overall = False
                    def qz(e, dwv=dwv + 16, buf_off=buf_off, dst_t=dst_t, dst_off=dst_off,
                           scl=scl, fo=fo):
                        e.wait_ge(sems["dw"], dwv)
                        if fo:
                            e.wait_ge(sems["ve"], vev_scl)
                            e.wait_ge(sems["se"], sev_abs)
                        e.tensor_scalar(out=ZY[:], in0=GEL[:, buf_off:buf_off + D],
                                        scalar1=scl[:, 0:1], scalar2=12582912.0,
                                        op0=ALU.mult, op1=ALU.add)
                        e.tensor_scalar(out=ZY[:], in0=ZY[:],
                                        scalar1=12582912.0, scalar2=None, op0=ALU.subtract)
                        e.tensor_scalar(out=dst_t[:, dst_off:dst_off + D],
                                        in0=ZY[:], scalar1=1.0, scalar2=-1.0,
                                        op0=ALU.min, op1=ALU.max).then_inc(sems["ve"], 1)
                    op("vector", qz)
                    P.inc("ve")

        if P0_CUT >= 5:
            wpass2(8, H, w1t_in, lambda cidx, a: (W1T[cidx], a))
            wpass2(32, D, w2t_in, lambda cidx, a: (W2T[cidx], 0))
            nlast = (8 * 4 + 32)
            ve_slot_free = P.val("ve") - 3
            def ld_tail(e, slot=(nlast % 4) * D, wv=ve_slot_free):
                e.wait_ge(sems["ve"], wv)
                e.dma_start(out=GEL[:, slot:slot + D],
                            in_=w1t_in.ap()[0:128, 0:D]).then_inc(sems["dw"], 16)
            op("sync", ld_tail)
            P.inc("dw", 16)

        barrier()

        if DBGOUT == 14:
            wv_all = P.val("ve")
            def wdump(e, wv_all=wv_all):
                e.wait_ge(sems["ve"], wv_all)
                for g in range(8):
                    e.dma_start(out=out_dr.ap()[g * 128:(g + 1) * 128, :],
                                in_=W1T[g][:].bitcast(F32)).then_inc(sems["do"], 16)
                for j in range(24):
                    e.dma_start(out=out_dr.ap()[1024 + j * 128:1024 + (j + 1) * 128, 0:256],
                                in_=W2T[j][:].bitcast(F32)).then_inc(sems["do"], 16)
            op("sync", wdump)
            P.inc("do", 16 * 32)

        if DEBUG_PHASE0_ONLY:
            def dbg(e):
                e.tensor_copy(ZY[:], W1T[0][:, 0:D])
                e.tensor_copy(X2[0][:], W2T[0][:])
                e.tensor_copy(OUT_T[:, 0:1], MW1[:])
                e.tensor_copy(OUT_T[:, 1:2], MW2[:])
                e.tensor_copy(OUT_T[:, 2:3], SCL1[:])
                e.tensor_copy(OUT_T[:, 3:4], SCL2[:])
                e.tensor_copy(OUT_T[:, 4:132], IDENT[:])
                e.tensor_copy(OUT_T[:, 132:164], WT[:, 0:32])
                e.tensor_copy(OUT_T[:, 164:232], WPART[:]).then_inc(sems["ve"], 1)
            op("vector", dbg)
            vd = P.inc("ve")
            def dbg_st(e, vd=vd):
                e.wait_ge(sems["ve"], vd)
                e.dma_start(out=out_dr.ap()[0:128, :], in_=ZY[:]).then_inc(sems["do"], 16)
                e.dma_start(out=out_dr.ap()[128:256, :], in_=X2[0][:]).then_inc(sems["do"], 16)
                e.dma_start(out=out_dr.ap()[256:384, :], in_=G2[:, 0:D]).then_inc(sems["do"], 16)
                e.dma_start(out=out_dr.ap()[384:512, :], in_=B1[:, 0:D]).then_inc(sems["do"], 16)
                e.dma_start(out=out_dr.ap()[512:640, :], in_=OUT_T[:]).then_inc(sems["do"], 16)
                e.wait_ge(sems["do"], 80)
            op("sync", dbg_st)
            P.inc("do", 80)

        def emit_load_x(i):
            wv = ve_done.get(("attnev", i - 2), 0)
            def f(e, i=i, wv=wv):
                if wv:
                    e.wait_ge(sems["ve"], wv)
                e.dma_start(out=XT[i % 2][:],
                            in_=x_in.ap()[i * 128:(i + 1) * 128, :]).then_inc(sems["dx"], 16)
            op("sync", f)
            dx_done[i] = P.inc("dx", 16)

        def emit_stats_alpha(i):
            r = i % 3
            dxv = dx_done[i]
            w_r1 = ve_done.get(("round1", i - 2), 0)
            w_r2 = ve_done.get(("round2", i - 3), 0)
            def fs(e, i=i, r=r, dxv=dxv, w_r1=w_r1, w_r2=w_r2):
                e.wait_ge(sems["dx"], dxv)
                if w_r1:
                    e.wait_ge(sems["ve"], w_r1)
                if w_r2:
                    e.wait_ge(sems["ve"], w_r2)
                e.activation(out=SCR16[:], in_=XT[i % 2][:], func=AF.Square,
                             bias=0.0, scale=1.0,
                             accum_out=col(r, CA_SQ)).then_inc(sems["se"], 1)
            op("scalar", fs)
            se_sq = P.inc("se")

            def fv(e, i=i, r=r, dxv=dxv, se_sq=se_sq):
                e.wait_ge(sems["dx"], dxv)
                FN(e, e.reduce_sum(col(r, CA_SUM), XT[i % 2][:], axis=AX.X))
                VG(e, None)
                e.wait_ge(sems["se"], se_sq)
                FN(e, e.tensor_scalar(out=col(r, CA_MEAN), in0=col(r, CA_SUM),
                                      scalar1=1.0 / D, scalar2=None, op0=ALU.mult))
                VG(e, None)
                FN(e, e.tensor_tensor(col(r, CA_MSQ), col(r, CA_MEAN),
                                      col(r, CA_MEAN), ALU.mult))
                VG(e, None)
                FN(e, e.tensor_scalar(out=col(r, CA_MSQE), in0=col(r, CA_MSQ),
                                      scalar1=1.0, scalar2=1e-5,
                                      op0=ALU.mult, op1=ALU.subtract))
                VG(e, None)
                e.tensor_scalar(out=col(r, CA_VAR), in0=col(r, CA_SQ),
                                scalar1=1.0 / D, scalar2=col(r, CA_MSQE),
                                op0=ALU.mult, op1=ALU.subtract).then_inc(sems["ve"], 1)
            op("vector", fv)
            ve_var = P.inc("ve")

            def fs2(e, r=r, ve_var=ve_var):
                e.wait_ge(sems["ve"], ve_var)
                e.activation(out=col(r, CA_STD), in_=col(r, CA_VAR), func=AF.Sqrt,
                             bias=0.0, scale=1.0).then_inc(sems["se"], 1)
            op("scalar", fs2)
            se_sd = P.inc("se")

            def fv2(e, r=r, se_sd=se_sd):
                e.wait_ge(sems["se"], se_sd)
                FN(e, e.reciprocal(col(r, CA_INV), col(r, CA_STD)))
                VG(e, None)
                e.tensor_scalar(out=col(r, CA_NMI), in0=col(r, CA_MEAN), scalar1=-1.0,
                                scalar2=col(r, CA_INV), op0=ALU.mult, op1=ALU.mult).then_inc(sems["ve"], 1)
            op("vector", fv2)
            ve_done[("statsA", i)] = P.inc("ve")

        def emit_xln(i):
            r = i % 3
            wv = ve_done[("statsA", i)]
            wpe = pe_done.get(("band", i - 2), 0)
            def f(e, i=i, r=r, wv=wv, wpe=wpe):
                e.wait_ge(sems["ve"], wv)
                if wpe:
                    e.wait_ge(sems["pe"], wpe)
                e.activation(out=XLN[i % 2][:], in_=XT[i % 2][:], func=AF.Identity,
                             bias=col(r, CA_NMI), scale=col(r, CA_INV)
                             ).then_inc(sems["se"], 1)
            op("scalar", f)
            se_done[("xln", i)] = P.inc("se")

        def emit_A(i):
            sxln = se_done[("xln", i)]
            wv_prev = ve_done.get(("Ae2", i - 1), 0)

            def t1(e, i=i, sxln=sxln, wv_prev=wv_prev):
                e.wait_ge(sems["se"], sxln)
                if wv_prev:
                    e.wait_ge(sems["ve"], wv_prev)
                for g in range(4):
                    last = e.transpose(psA[:, g * 128:(g + 1) * 128],
                                       XLN[i % 2][:, g * 128:(g + 1) * 128], IDENT[:])
                last.then_inc(sems["pe"], 1)
            op("tensor", t1)
            pe_w1 = P.inc("pe")

            def v1(e, pe_w1=pe_w1):
                e.wait_ge(sems["pe"], pe_w1)
                e.tensor_copy(XLNFM[:, 0:512], psA[:, 0:512]).then_inc(sems["ve"], 1)
            op("vector", v1)
            ve_e1 = P.inc("ve")
            if AIN < 2:
                ve_done[("softmax", i)] = 0
                ve_done[("Ae2", i)] = ve_e1
                ve_done[("lgmx", i)] = 0
                return

            def t2(e, i=i, ve_e1=ve_e1):
                e.wait_ge(sems["ve"], ve_e1)
                for g in range(4):
                    e.transpose(psA[:, g * 128:(g + 1) * 128],
                                XLN[i % 2][:, 512 + g * 128:512 + (g + 1) * 128], IDENT[:])
                for g in range(4):
                    last = e.matmul(psM[0:32, 256:384], WT[:, g * K:(g + 1) * K],
                                    XLNFM[:, g * 128:(g + 1) * 128],
                                    start=(g == 0), stop=False)
                last.then_inc(sems["pe"], 1)
            op("tensor", t2)
            pe_w2 = P.inc("pe")

            def v2(e, pe_w2=pe_w2):
                e.wait_ge(sems["pe"], pe_w2)
                e.tensor_copy(XLNFM[:, 0:512], psA[:, 0:512]).then_inc(sems["ve"], 1)
            op("vector", v2)
            ve_e2 = P.inc("ve")
            ve_done[("Ae2", i)] = ve_e2
            if AIN < 3:
                ve_done[("softmax", i)] = 0
                ve_done[("lgmx", i)] = 0
                return

            def t3(e, ve_e2=ve_e2):
                e.wait_ge(sems["ve"], ve_e2)
                for g in range(4):
                    last = e.matmul(psM[0:32, 256:384], WT[:, (4 + g) * K:(5 + g) * K],
                                    XLNFM[:, g * 128:(g + 1) * 128],
                                    start=False, stop=(g == 3))
                last.then_inc(sems["pe"], 1)
            op("tensor", t3)
            pe_lmm = P.inc("pe")

            def s1(e, pe_lmm=pe_lmm):
                e.wait_ge(sems["pe"], pe_lmm)
                e.activation(out=LG[:], in_=psM[0:32, 256:384], func=AF.Identity,
                             bias=0.0, scale=1.0).then_inc(sems["se"], 1)
            op("scalar", s1)
            se_lg = P.inc("se")

            def t4(e, se_lg=se_lg):
                e.wait_ge(sems["se"], se_lg)
                e.transpose(psM[:, 384:416], LG[:], IDENT[0:32, 0:32]).then_inc(sems["pe"], 1)
            op("tensor", t4)
            pe_lgt = P.inc("pe")
            if AIN < 4:
                ve_done[("softmax", i)] = 0
                ve_done[("lgmx", i)] = 0
                return

            r = i % 3
            def v3(e, r=r, pe_lgt=pe_lgt):
                e.wait_ge(sems["pe"], pe_lgt)
                e.tensor_tensor(LOGITS[:], psM[:, 384:416], BATB[:], ALU.add)
                e.reduce_max(col(r, CS_NEGMX), LOGITS[:], axis=AX.X, negate=True).then_inc(sems["ve"], 1)
            op("vector", v3)
            ve_mx = P.inc("ve")
            ve_done[("lgmx", i)] = ve_mx

            wdb = db_done.get(("wst", i - 1), 0)
            def s2(e, r=r, ve_mx=ve_mx, wdb=wdb):
                e.wait_ge(sems["ve"], ve_mx)
                if wdb:
                    e.wait_ge(sems["db"], wdb)
                e.activation(out=WP[:], in_=LOGITS[:], func=AF.Exp,
                             bias=col(r, CS_NEGMX), scale=1.0,
                             accum_out=col(r, CS_Z)).then_inc(sems["se"], 1)
            op("scalar", s2)
            se_exp = P.inc("se")

            def v4(e, r=r, se_exp=se_exp):
                e.wait_ge(sems["se"], se_exp)
                FN(e, e.reciprocal(col(r, CS_RZ), col(r, CS_Z)))
                VG(e, None)
                e.tensor_scalar(out=WP[:], in0=WP[:], scalar1=col(r, CS_RZ),
                                scalar2=0.0, op0=ALU.mult, op1=ALU.add).then_inc(sems["ve"], 1)
            op("vector", v4)
            ve_done[("softmax", i)] = P.inc("ve")

        def emit_band_dma(i):
            wv = ve_done[("softmax", i)]
            base = (i % 2) * (128 * WROW + 64)
            wprev = db_done.get(("mtl", i - 2), 0)
            wdt1 = P.val("dt1")
            wdt2 = P.val("dt2")
            def f(e, wv=wv, base=base, wprev=wprev, wdt1=wdt1, wdt2=wdt2):
                e.wait_ge(sems["ve"], wv)
                if wprev:
                    e.wait_ge(sems["db"], wprev)
                if wdt1:
                    e.wait_ge(sems["dt1"], wdt1)
                if wdt2:
                    e.wait_ge(sems["dt2"], wdt2)
                e.dma_start(out=bass.AP(whbm, base, [[WROW, 128], [1, K]]),
                            in_=WP[:]).then_inc(sems["db"], 16)
            op("gpsimd", f)
            st_v = P.inc("db", 16)
            db_done[("wst", i)] = st_v

            wmt = pe_done.get(("mt", i - 1), 0)
            def f2(e, st_v=st_v, base=base, wmt=wmt):
                e.wait_ge(sems["db"], st_v)
                if wmt:
                    e.wait_ge(sems["pe"], wmt)
                e.dma_start(out=MT_SB[:],
                            in_=bass.AP(whbm, base, [[WROW - 1, 128], [1, WROW]])
                            ).then_inc(sems["db"], 16)
                e.dma_start(out=MT_SB[:, 0:4],
                            in_=bass.AP(whbm, base, [[WROW - 1, 128], [1, 4]])
                            ).then_inc(sems["db"], 16)
            op("gpsimd", f2)
            P.inc("db", 16)
            db_done[("mtl", i)] = P.inc("db", 16)

        def emit_MT(i):
            wdb = db_done[("mtl", i)]
            wv = ve_done.get(("lgmx", i), 0)
            def f(e, wdb=wdb, wv=wv):
                e.wait_ge(sems["db"], wdb)
                if wv:
                    e.wait_ge(sems["ve"], wv)
                e.transpose(psM[:, 0:128], MT_SB[:, 0:128], IDENT[:])
                e.transpose(psM[0:32, 128:256], MT_SB[:, 128:160], IDENT[:]).then_inc(sems["pe"], 1)
            op("tensor", f)
            pe_done[("mt", i)] = P.inc("pe")

        def emit_Mevac(i):
            wpe = pe_done[("mt", i)]
            def f(e, wpe=wpe):
                e.wait_ge(sems["pe"], wpe)
                e.tensor_copy(MA[:], psM[:, 0:128])
                e.tensor_copy(MB[0:32, :], psM[0:32, 128:256]).then_inc(sems["ve"], 1)
            op("vector", f)
            ve_done[("mev", i)] = P.inc("ve")

        def emit_B(i):
            wv = ve_done[("mev", i)]
            wx = se_done[("xln", i + 1)]
            wev = ve_done.get(("attnev", i - 1), 0)
            def f(e, i=i, wv=wv, wx=wx, wev=wev):
                e.wait_ge(sems["ve"], wv)
                e.wait_ge(sems["se"], wx)
                if wev:
                    e.wait_ge(sems["ve"], wev)
                for dm, ps in ((0, psB0), (1, psB1)):
                    e.matmul(ps[:], MA[:], XLN[i % 2][:, dm * 512:(dm + 1) * 512],
                             start=True, stop=False)
                    last = e.matmul(ps[:], MB[0:32, :],
                                    XLN[(i + 1) % 2][0:32, dm * 512:(dm + 1) * 512],
                                    start=False, stop=True)
                last.then_inc(sems["pe"], 1)
            op("tensor", f)
            pe_done[("band", i)] = P.inc("pe")

        def emit_attn_evac(i):
            wpe = pe_done[("band", i)]
            wv2 = ve_done.get(("evac2", i - 3), 0)
            def f(e, i=i, wpe=wpe, wv2=wv2):
                e.wait_ge(sems["pe"], wpe)
                if wv2:
                    e.wait_ge(sems["ve"], wv2)
                for dm, ps in ((0, psB0), (1, psB1)):
                    last = e.scalar_tensor_tensor(out=X2[i % 3][:, dm * 512:(dm + 1) * 512],
                                                  in0=ps[:], scalar=1.0,
                                                  in1=XT[i % 2][:, dm * 512:(dm + 1) * 512],
                                                  op0=ALU.mult, op1=ALU.add)
                last.then_inc(sems["ve"], 1)
            op("vector", f)
            ve_done[("attnev", i)] = P.inc("ve")

        def emit_chain_head(i):
            r = i % 3
            wev = ve_done[("attnev", i)]
            x2t = X2[i % 3]

            def s_sqB(e, x2t=x2t, r=r, wev=wev):
                e.wait_ge(sems["ve"], wev)
                e.activation(out=SCR16[:], in_=x2t[:], func=AF.Square,
                             bias=0.0, scale=1.0,
                             accum_out=col(r, CB_SQ)).then_inc(sems["se"], 1)
            op("scalar", s_sqB)
            se_sqB = P.inc("se")

            def v_stB(e, x2t=x2t, r=r, wev=wev, se_sqB=se_sqB):
                e.wait_ge(sems["ve"], wev)
                FN(e, e.reduce_sum(col(r, CB_SUM), x2t[:], axis=AX.X))
                VG(e, None)
                e.wait_ge(sems["se"], se_sqB)
                FN(e, e.tensor_scalar(out=col(r, CB_MEAN), in0=col(r, CB_SUM),
                                      scalar1=1.0 / D, scalar2=None, op0=ALU.mult))
                VG(e, None)
                FN(e, e.tensor_tensor(col(r, CB_MSQ), col(r, CB_MEAN),
                                      col(r, CB_MEAN), ALU.mult))
                VG(e, None)
                FN(e, e.tensor_scalar(out=col(r, CB_MSQE), in0=col(r, CB_MSQ),
                                      scalar1=1.0, scalar2=1e-5,
                                      op0=ALU.mult, op1=ALU.subtract))
                VG(e, None)
                e.tensor_scalar(out=col(r, CB_VAR), in0=col(r, CB_SQ),
                                scalar1=1.0 / D, scalar2=col(r, CB_MSQE),
                                op0=ALU.mult, op1=ALU.subtract).then_inc(sems["ve"], 1)
            op("vector", v_stB)
            ve_vB = P.inc("ve")

            def s_sdB(e, r=r, ve_vB=ve_vB):
                e.wait_ge(sems["ve"], ve_vB)
                e.activation(out=col(r, CB_STD), in_=col(r, CB_VAR), func=AF.Sqrt,
                             bias=0.0, scale=1.0).then_inc(sems["se"], 1)
            op("scalar", s_sdB)
            se_sdB = P.inc("se")

            def v_invB(e, r=r, se_sdB=se_sdB):
                e.wait_ge(sems["se"], se_sdB)
                FN(e, e.reciprocal(col(r, CB_INV), col(r, CB_STD)))
                VG(e, None)
                e.tensor_scalar(out=col(r, CB_NMI), in0=col(r, CB_MEAN), scalar1=-1.0,
                                scalar2=col(r, CB_INV), op0=ALU.mult, op1=ALU.mult).then_inc(sems["ve"], 1)
            op("vector", v_invB)
            ve_invB = P.inc("ve")

            def s_z(e, x2t=x2t, r=r, ve_invB=ve_invB):
                e.wait_ge(sems["ve"], ve_invB)
                e.activation(out=ZY[:], in_=x2t[:], func=AF.Identity,
                             bias=col(r, CB_NMI), scale=col(r, CB_INV)
                             ).then_inc(sems["se"], 1)
            op("scalar", s_z)
            se_z = P.inc("se")

            def v_b2(e, x2t=x2t, se_z=se_z):
                e.wait_ge(sems["se"], se_z)
                e.tensor_tensor(x2t[:], x2t[:], B2[:], ALU.add).then_inc(sems["ve"], 1)
            op("vector", v_b2)
            P.inc("ve")

            def s_sqG(e, r=r, se_z=se_z):
                e.activation(out=SCR16[:], in_=ZY[:], func=AF.Square,
                             bias=0.0, scale=1.0,
                             accum_out=col(r, CG_SQ)).then_inc(sems["se"], 1)
            op("scalar", s_sqG)
            se_sqG = P.inc("se")

            def v_stG(e, r=r, se_sqG=se_sqG):
                e.wait_ge(sems["se"], se_sqG)
                FN(e, e.reduce_sum(col(r, CG_SUM), ZY[:], axis=AX.X))
                VG(e, None)
                FN(e, e.tensor_scalar(out=col(r, CG_M), in0=col(r, CG_SUM),
                                      scalar1=1.0 / D, scalar2=None, op0=ALU.mult))
                VG(e, None)
                FN(e, e.tensor_tensor(col(r, CG_MSQ), col(r, CG_M),
                                      col(r, CG_M), ALU.mult))
                VG(e, None)
                FN(e, e.tensor_scalar(out=col(r, CG_MSQE), in0=col(r, CG_MSQ),
                                      scalar1=1.0, scalar2=1e-5,
                                      op0=ALU.mult, op1=ALU.subtract))
                VG(e, None)
                e.tensor_scalar(out=col(r, CG_VAR), in0=col(r, CG_SQ),
                                scalar1=1.0 / D, scalar2=col(r, CG_MSQE),
                                op0=ALU.mult, op1=ALU.subtract).then_inc(sems["ve"], 1)
            op("vector", v_stG)
            ve_vG = P.inc("ve")

            def s_sdG(e, r=r, ve_vG=ve_vG):
                e.wait_ge(sems["ve"], ve_vG)
                e.activation(out=col(r, CG_STD), in_=col(r, CG_VAR), func=AF.Sqrt,
                             bias=0.0, scale=1.0).then_inc(sems["se"], 1)
            op("scalar", s_sdG)
            se_sdG = P.inc("se")

            def v_ln1a(e, r=r, se_sdG=se_sdG):
                e.wait_ge(sems["se"], se_sdG)
                FN(e, e.reciprocal(col(r, CG_INV), col(r, CG_STD)))
                e.scalar_tensor_tensor(out=ZY[:], in0=ZY[:], scalar=col(r, CG_M),
                                       in1=G1[:], op0=ALU.subtract, op1=ALU.mult).then_inc(sems["ve"], 1)
            op("vector", v_ln1a)
            ve_ln1a = P.inc("ve")

            def s_ln1b(e, r=r, ve_ln1a=ve_ln1a):
                e.wait_ge(sems["ve"], ve_ln1a)
                e.activation(out=ZY[:], in_=ZY[:], func=AF.Identity,
                             bias=0.0, scale=col(r, CG_INV)).then_inc(sems["se"], 1)
            op("scalar", s_ln1b)
            se_ln1b = P.inc("se")

            wdt1 = dt_done.get(("q1t", i - 1), 0)
            def v_ln1c(e, r=r, se_ln1b=se_ln1b, wdt1=wdt1):
                e.wait_ge(sems["se"], se_ln1b)
                e.tensor_tensor(ZY[:], ZY[:], B1LN[:], ALU.add)
                FN(e, e.reduce_max(col(r, CQ1_MXR), ZY[:], axis=AX.X,
                                   apply_absolute_value=True))
                VG(e, None)
                FN(e, e.tensor_scalar(out=col(r, CQ1_MX), in0=col(r, CQ1_MXR),
                                      scalar1=1e-5, scalar2=None, op0=ALU.max))
                VG(e, None)
                FN(e, e.reciprocal(col(r, CQ1_T), col(r, CQ1_MX)))
                VG(e, None)
                FN(e, e.tensor_scalar(out=col(r, CQ1_S), in0=col(r, CQ1_T), scalar1=127.0,
                                      scalar2=0.0, op0=ALU.mult, op1=ALU.add))
                FN(e, e.tensor_scalar(out=col(r, CQ1_G), in0=col(r, CQ1_MX),
                                      scalar1=MW1[:, 0:1], scalar2=INV127[:, 0:1],
                                      op0=ALU.mult, op1=ALU.mult))
                if wdt1:
                    e.wait_ge(sems["dt1"], wdt1)
                e.tensor_scalar(out=ZY[:], in0=ZY[:], scalar1=col(r, CQ1_S),
                                scalar2=12582912.0, op0=ALU.mult, op1=ALU.add)
                e.tensor_scalar(out=Q1[:], in0=ZY[:], scalar1=12582912.0,
                                scalar2=None, op0=ALU.subtract).then_inc(sems["ve"], 1)
            op("vector", v_ln1c)
            ve_done[("round1", i)] = P.inc("ve")

        def emit_q1T(i):
            wv = ve_done[("round1", i)]
            wpe = pe_done.get(("c7", i - 1), 0)
            wdb = P.val("db")
            def f(e, i=i, wv=wv, wpe=wpe, wdb=wdb):
                e.wait_ge(sems["ve"], wv)
                if wpe:
                    e.wait_ge(sems["pe"], wpe)
                if wdb:
                    e.wait_ge(sems["db"], wdb)
                for g in range(8):
                    e.dma_start_transpose(Q1T[:, g * 128:(g + 1) * 128],
                                          Q1[:, g * 128:(g + 1) * 128]
                                          ).then_inc(sems["dt1"], 16)
                e.dma_start_transpose(Q1T[:, 0:128],
                                      Q1[:, 0:128]).then_inc(sems["dt1"], 16)
            op("sync", f)
            dt_done[("q1t", i)] = P.inc("dt1", 16 * 9)
            if DBGOUT == 10:
                wv2 = ve_done[("round1", i)]
                def fd(e, i=i, wv2=wv2):
                    e.wait_ge(sems["ve"], wv2)
                    e.dma_start(out=out_dr.ap()[i * 128:(i + 1) * 128, :],
                                in_=ZY[:]).then_inc(sems["do"], 16)
                op("sync", fd)
                P.inc("do", 16)

        def emit_C(i):
            wdt = dt_done[("q1t", i)]
            w_r2 = ve_done.get(("round2", i - 1), 0)
            for hc in range(8):
                ps = psC0 if hc % 2 == 0 else psC1
                wev = ve_done.get(("evac1", i, hc - 2), 0)
                if hc == 0:
                    wev = ve_done.get(("evac1", i - 1, 6), 0)
                elif hc == 1:
                    wev = ve_done.get(("evac1", i - 1, 7), 0)
                def f(e, i=i, hc=hc, ps=ps, wdt=wdt, wev=wev):
                    if hc == 0:
                        e.wait_ge(sems["dt1"], wdt)
                    if wev:
                        e.wait_ge(sems["ve"], wev)
                    for g in range(8):
                        last = e.matmul(ps[:], Q1T[:, g * 128:(g + 1) * 128],
                                        W1T[g][:, hc * 512:(hc + 1) * 512],
                                        start=(g == 0), stop=(g == 7))
                    last.then_inc(sems["pe"], 1)
                op("tensor", f)
                pe_done[("c", i, hc)] = P.inc("pe")
                if hc == 7:
                    pe_done[("c7", i)] = pe_done[("c", i, hc)]

                r = i % 3
                wpe = pe_done[("c", i, hc)]
                def fv(e, i=i, hc=hc, ps=ps, r=r, wpe=wpe, w_r2=w_r2):
                    e.wait_ge(sems["pe"], wpe)
                    if hc == 0 and w_r2:
                        e.wait_ge(sems["ve"], w_r2)
                    e.scalar_tensor_tensor(out=GEL[:, hc * 512:(hc + 1) * 512],
                                           in0=ps[:], scalar=col(r, CQ1_G),
                                           in1=B1[:, hc * 512:(hc + 1) * 512],
                                           op0=ALU.mult, op1=ALU.add).then_inc(sems["ve"], 1)
                op("vector", fv)
                ve_done[("evac1", i, hc)] = P.inc("ve")

                wv1 = ve_done[("evac1", i, hc)]
                def fs(e, i=i, hc=hc, wv1=wv1):
                    e.wait_ge(sems["ve"], wv1)
                    e.activation(out=GEL[:, hc * 512:(hc + 1) * 512],
                                 in_=GEL[:, hc * 512:(hc + 1) * 512], func=AF.Gelu,
                                 bias=0.0, scale=1.0,
                                 accum_out=GSUM[i % 2][:, hc:hc + 1]).then_inc(sems["se"], 1)
                op("scalar", fs)
                se_done[("gelu", i, hc)] = P.inc("se")
            if DBGOUT == 13:
                ws2 = se_done[("gelu", i, 7)]
                def fd(e, i=i, ws2=ws2):
                    e.wait_ge(sems["se"], ws2)
                    e.dma_start(out=out_dr.ap()[i * 128:(i + 1) * 128, :],
                                in_=GEL[:, 0:D]).then_inc(sems["do"], 16)
                op("sync", fd)
                P.inc("do", 16)

        def emit_chain_mid(i):
            r = i % 3
            wse = se_done[("gelu", i, 7)]
            wdt2 = dt_done.get(("q2t", i - 1), 0)

            def s_sqH(e, i=i, r=r, wse=wse, wdt2=wdt2):
                e.wait_ge(sems["se"], wse)
                if wdt2:
                    e.wait_ge(sems["dt2"], wdt2)
                FN(e, e.activation(out=Q2[:, 0:2048], in_=GEL[:, 0:2048], func=AF.Square,
                                   bias=0.0, scale=1.0, accum_out=col(r, CH_A0)))
                e.activation(out=Q2[:, 2048:4096], in_=GEL[:, 2048:4096], func=AF.Square,
                             bias=0.0, scale=1.0, accum_out=col(r, CH_A1)
                             ).then_inc(sems["se"], 1)
            op("scalar", s_sqH)
            se_sqH = P.inc("se")
            if MCUT < 2:
                ve_done[("round2", i)] = 0
                return

            def v_stH(e, i=i, r=r, se_sqH=se_sqH):
                e.wait_ge(sems["se"], se_sqH)
                FN(e, e.reduce_sum(col(r, CH_SUM), GSUM[i % 2][:], axis=AX.X))
                FN(e, e.tensor_tensor(col(r, CH_SQ), col(r, CH_A0), col(r, CH_A1), ALU.add))
                FN(e, e.tensor_scalar(out=col(r, CH_M), in0=col(r, CH_SUM),
                                      scalar1=1.0 / H, scalar2=None, op0=ALU.mult))
                VG(e, None)
                FN(e, e.tensor_tensor(col(r, CH_MSQ), col(r, CH_M), col(r, CH_M), ALU.mult))
                VG(e, None)
                FN(e, e.tensor_scalar(out=col(r, CH_MSQE), in0=col(r, CH_MSQ),
                                      scalar1=1.0, scalar2=1e-5,
                                      op0=ALU.mult, op1=ALU.subtract))
                VG(e, None)
                e.tensor_scalar(out=col(r, CH_VAR), in0=col(r, CH_SQ),
                                scalar1=1.0 / H, scalar2=col(r, CH_MSQE),
                                op0=ALU.mult, op1=ALU.subtract).then_inc(sems["ve"], 1)
            op("vector", v_stH)
            ve_vH = P.inc("ve")

            def s_sdH(e, r=r, ve_vH=ve_vH):
                e.wait_ge(sems["ve"], ve_vH)
                e.activation(out=col(r, CH_STD), in_=col(r, CH_VAR), func=AF.Sqrt,
                             bias=0.0, scale=1.0).then_inc(sems["se"], 1)
            op("scalar", s_sdH)
            se_sdH = P.inc("se")
            if MCUT < 3:
                ve_done[("round2", i)] = 0
                return

            def v_ln2a(e, r=r, se_sdH=se_sdH):
                e.wait_ge(sems["se"], se_sdH)
                FN(e, e.reciprocal(col(r, CH_INV), col(r, CH_STD)))
                e.scalar_tensor_tensor(out=GEL[:], in0=GEL[:], scalar=col(r, CH_M),
                                       in1=G2[:], op0=ALU.subtract, op1=ALU.mult).then_inc(sems["ve"], 1)
            op("vector", v_ln2a)
            ve_ln2a = P.inc("ve")
            if MCUT == 4:
                ve_done[("round2", i)] = 0
                return

            def s_ln2b(e, r=r, ve_ln2a=ve_ln2a):
                e.wait_ge(sems["ve"], ve_ln2a)
                e.activation(out=GEL[:], in_=GEL[:], func=AF.Identity,
                             bias=0.0, scale=col(r, CH_INV)).then_inc(sems["se"], 1)
            op("scalar", s_ln2b)
            se_ln2b = P.inc("se")
            if MCUT == 5:
                ve_done[("round2", i)] = 0
                return

            def v_ln2c(e, r=r, se_ln2b=se_ln2b):
                e.wait_ge(sems["se"], se_ln2b)
                e.tensor_tensor(GEL[:], GEL[:], B2LN[:], ALU.add)
                FN(e, e.reduce_max(col(r, CQ2_MXR), GEL[:], axis=AX.X,
                                   apply_absolute_value=True))
                VG(e, None)
                FN(e, e.tensor_scalar(out=col(r, CQ2_MX), in0=col(r, CQ2_MXR),
                                      scalar1=1e-5, scalar2=None, op0=ALU.max))
                VG(e, None)
                FN(e, e.reciprocal(col(r, CQ2_T), col(r, CQ2_MX)))
                VG(e, None)
                FN(e, e.tensor_scalar(out=col(r, CQ2_S), in0=col(r, CQ2_T), scalar1=127.0,
                                      scalar2=0.0, op0=ALU.mult, op1=ALU.add))
                FN(e, e.tensor_scalar(out=col(r, CQ2_G), in0=col(r, CQ2_MX),
                                      scalar1=MW2[:, 0:1], scalar2=INV127[:, 0:1],
                                      op0=ALU.mult, op1=ALU.mult))

                for qq in range(4):
                    e.tensor_scalar(out=GEL[:, qq * D:(qq + 1) * D],
                                    in0=GEL[:, qq * D:(qq + 1) * D],
                                    scalar1=col(r, CQ2_S), scalar2=12582912.0,
                                    op0=ALU.mult, op1=ALU.add)
                    last = e.tensor_scalar(out=Q2[:, qq * D:(qq + 1) * D],
                                           in0=GEL[:, qq * D:(qq + 1) * D],
                                           scalar1=12582912.0, scalar2=None, op0=ALU.subtract)
                last.then_inc(sems["ve"], 1)
            op("vector", v_ln2c)
            ve_done[("round2", i)] = P.inc("ve")

        def emit_q2T(i):
            wv = ve_done[("round2", i)]
            wpe = pe_done.get(("d1", i - 1), 0)
            wdb = P.val("db")
            def mk(js, dummy_j):
                def f(e, i=i, wv=wv, wpe=wpe, wdb=wdb, js=js, dummy_j=dummy_j):
                    e.wait_ge(sems["ve"], wv)
                    if wpe:
                        e.wait_ge(sems["pe"], wpe)
                    if wdb:
                        e.wait_ge(sems["db"], wdb)
                    for j in js:
                        e.dma_start_transpose(Q2T[:, j * 128:(j + 1) * 128],
                                              Q2[:, j * 128:(j + 1) * 128]
                                              ).then_inc(sems["dt2"], 16)
                    e.dma_start_transpose(Q2T[:, dummy_j * 128:(dummy_j + 1) * 128],
                                          Q2[:, dummy_j * 128:(dummy_j + 1) * 128]
                                          ).then_inc(sems["dt2"], 16)
                return f
            op("scalar", mk(list(range(0, 32)), 0))
            dt_done[("q2t", i)] = P.inc("dt2", 16 * 33)
            if DBGOUT in (11, 12):
                wv2 = ve_done[("round2", i)]
                off = 0 if DBGOUT == 11 else 1024
                def fd(e, i=i, wv2=wv2, off=off):
                    e.wait_ge(sems["ve"], wv2)
                    e.dma_start(out=out_dr.ap()[i * 128:(i + 1) * 128, :],
                                in_=GEL[:, off:off + D]).then_inc(sems["do"], 16)
                op("sync", fd)
                P.inc("do", 16)

        def emit_D(i):
            wdt = dt_done[("q2t", i)]
            wev = ve_done.get(("evac2", i - 1), 0)
            for dc, ps in [(0, psD0), (1, psD1)][:DDC]:
                nsub = (DJN + 7) // 8
                base = P.val("pe")
                def f(e, i=i, dc=dc, ps=ps, wdt=wdt, wev=wev, base=base, nsub=nsub):
                    if dc == 0:
                        e.wait_ge(sems["dt2"], wdt)
                        if wev:
                            e.wait_ge(sems["ve"], wev)
                    for sub in range(nsub):
                        if sub >= 2:
                            e.wait_ge(sems["pe"], base + sub - 1)
                        for jj in range(8):
                            j = sub * 8 + jj
                            if j >= DJN:
                                break
                            last = e.matmul(ps[:], Q2T[:, j * 128:(j + 1) * 128],
                                            W2T[j][:, dc * 512:(dc + 1) * 512],
                                            start=(j == 0), stop=(j == DJN - 1))
                        last.then_inc(sems["pe"], 1)
                op("tensor", f)
                for _ in range(nsub):
                    pe_done[("d", i, dc)] = P.inc("pe")
            pe_done[("d1", i)] = pe_done[("d", i, DDC - 1)]

        def emit_evac2(i):
            r = i % 3
            wpe = pe_done[("d", i, DDC - 1)]
            wdo = do_done.get(i - 1, 0)
            def f(e, i=i, r=r, wpe=wpe, wdo=wdo):
                e.wait_ge(sems["pe"], wpe)
                if wdo:
                    e.wait_ge(sems["do"], wdo)
                for dc, ps in [(0, psD0), (1, psD1)][:DDC]:
                    last = e.scalar_tensor_tensor(out=OUT_T[:, dc * 512:(dc + 1) * 512],
                                                  in0=ps[:], scalar=col(r, CQ2_G),
                                                  in1=X2[i % 3][:, dc * 512:(dc + 1) * 512],
                                                  op0=ALU.mult, op1=ALU.add)
                last.then_inc(sems["ve"], 1)
            op("vector", f)
            ve_done[("evac2", i)] = P.inc("ve")

        def emit_store(i):
            wv = ve_done[("evac2", i)]
            def f(e, i=i, wv=wv):
                e.wait_ge(sems["ve"], wv)
                e.dma_start(out=out_dr.ap()[i * 128:(i + 1) * 128, :],
                            in_=OUT_T[:]).then_inc(sems["do"], 16)
            op("sync", f)
            do_done[i] = P.inc("do", 16)

        def emit_store_debug_x2(i):
            wv = ve_done[("attnev", i)]
            def f(e, i=i, wv=wv):
                e.wait_ge(sems["ve"], wv)
                if DBGOUT == 1:
                    e.dma_start(out=out_dr.ap()[i * 128:(i + 1) * 128, :],
                                in_=XT[i % 2][:]).then_inc(sems["do"], 16)
                elif DBGOUT == 2:
                    e.dma_start(out=out_dr.ap()[i * 128:(i + 1) * 128, :],
                                in_=XLN[i % 2][:]).then_inc(sems["do"], 16)
                elif DBGOUT == 3:
                    e.dma_start(out=out_dr.ap()[i * 128:(i + 1) * 128, 0:K],
                                in_=WP[:]).then_inc(sems["do"], 16)
                elif DBGOUT == 4:
                    e.dma_start(out=out_dr.ap()[i * 128:(i + 1) * 128, 0:WROW],
                                in_=MT_SB[:]).then_inc(sems["do"], 16)
                elif DBGOUT == 8:
                    e.dma_start(out=out_dr.ap()[i * 128:i * 128 + 32, 0:128],
                                in_=MB[0:32, :]).then_inc(sems["do"], 16)
                elif DBGOUT == 7:
                    e.dma_start(out=out_dr.ap()[i * 128:(i + 1) * 128, 0:128],
                                in_=MA[:]).then_inc(sems["do"], 16)
                elif DBGOUT == 6:
                    e.dma_start(out=out_dr.ap()[i * 128:(i + 1) * 128, 0:NCOLS],
                                in_=ST[i % 3][:]).then_inc(sems["do"], 16)
                else:
                    e.dma_start(out=out_dr.ap()[i * 128:(i + 1) * 128, :],
                                in_=X2[i % 3][:]).then_inc(sems["do"], 16)
            op("sync", f)
            do_done[i] = P.inc("do", 16)

        for k in (range(-2, min(NT + 3, MAXCYC)) if not DEBUG_PHASE0_ONLY else []):
            if 0 <= k - 3 <= NT - 1 and not DEBUG_ATTN_ONLY and FCUT >= 6:
                emit_D(k - 3)
                if FCUT >= 7:
                    emit_evac2(k - 3)
                    if FCUT >= 8:
                        emit_store(k - 3)
            if 0 <= k <= NT - 1 and ACUT >= 7:
                emit_Mevac(k)
            if k < 0 and 0 <= k + 2 <= NTL - 1:
                emit_load_x(k + 2)
            if 0 <= k + 1 <= NTL - 1 and ACUT >= 2:
                emit_stats_alpha(k + 1)
                if ACUT >= 3:
                    emit_xln(k + 1)
            if 0 <= k - 2 <= NT - 1 and not DEBUG_ATTN_ONLY and FCUT >= 4:
                emit_chain_mid(k - 2)
                if FCUT >= 5 and MCUT >= 3:
                    emit_q2T(k - 2)
            if 0 <= k + 1 <= NT - 1 and ACUT >= 4:
                emit_A(k + 1)
                if ACUT >= 5:
                    emit_band_dma(k + 1)
            if 0 <= k <= NT - 1 and ACUT >= 7:
                emit_B(k)
                emit_attn_evac(k)
                if DEBUG_ATTN_ONLY or FCUT < 8:
                    emit_store_debug_x2(k)
            if False:
                pass
            if k >= 0 and 0 <= k + 2 <= NTL - 1:
                emit_load_x(k + 2)
            if 0 <= k - 1 <= NT - 1 and not DEBUG_ATTN_ONLY and FCUT >= 3:
                emit_C(k - 1)
            if 0 <= k <= NT - 1 and not DEBUG_ATTN_ONLY and FCUT >= 1:
                emit_chain_head(k)
                if FCUT >= 2:
                    emit_q1T(k)
            if 0 <= k + 1 <= NT - 1 and ACUT >= 6:
                emit_MT(k + 1)
            if SERIAL:
                barrier()

        do_total = P.val("do")
        op("sync", lambda e, v=do_total: e.wait_ge(sems["do"], v))

        with nc.Block() as block:
            @block.sync
            def _(eng):
                for f in P.ops["sync"]:
                    f(eng)

            @block.vector
            def _(eng):
                for f in P.ops["vector"]:
                    f(eng)

            @block.scalar
            def _(eng):
                for f in P.ops["scalar"]:
                    f(eng)

            @block.tensor
            def _(eng):
                for f in P.ops["tensor"]:
                    f(eng)

            @block.gpsimd
            def _(eng):
                for f in P.ops["gpsimd"]:
                    f(eng)

    nc._dbg_plan = dict(P.cnt)
    return nc


def _bcast128(v):
    v = np.asarray(v, np.float32)
    return np.ascontiguousarray(np.broadcast_to(v[None, :], (128, v.shape[0])))


def _shard_inputs(L0_attn_x, L0_attn_W, L0_attn_b, L0_ffn_ln1_g, L0_ffn_ln1_b,
                  L0_ffn_w1, L0_ffn_b1, L0_ffn_ln2_g, L0_ffn_ln2_b, L0_ffn_w2,
                  L0_ffn_b2):
    x = np.ascontiguousarray(np.asarray(L0_attn_x, dtype=np.float32))
    B, L, Dm = x.shape
    shared = dict(
        w1t=np.ascontiguousarray(np.asarray(L0_ffn_w1, np.float32).T),
        w2t=np.ascontiguousarray(np.asarray(L0_ffn_w2, np.float32).T),
        wat=np.ascontiguousarray(np.asarray(L0_attn_W, np.float32).T),
        battn=_bcast128(L0_attn_b),
        g1=_bcast128(L0_ffn_ln1_g),
        b1ln=_bcast128(L0_ffn_ln1_b),
        g2=_bcast128(L0_ffn_ln2_g),
        b2ln=_bcast128(L0_ffn_ln2_b),
        b1=_bcast128(L0_ffn_b1),
        b2=_bcast128(L0_ffn_b2),
    )
    in_maps = []
    for c in range(8):
        b = c // 2
        half = c % 2
        s = half * SEQ
        xs = np.zeros((NTL * 128, Dm), np.float32)
        avail = min(NTL * 128, L - s)
        xs[:avail] = x[b, s:s + avail]
        m = dict(shared)
        m["x"] = xs
        in_maps.append(m)
    return in_maps


def kernel(**inputs):
    in_maps = _shard_inputs(**inputs)
    nc = build_program()
    res = run_bass_kernel_spmd(nc, in_maps, core_ids=list(range(8)))
    out = np.empty((4, 8192, 1024), np.float32)
    for c in range(8):
        b = c // 2
        half = c % 2
        out[b, half * SEQ:(half + 1) * SEQ] = np.asarray(res.results[c]["out"],
                                                         np.float32)
    return out


# version 14
# speedup vs baseline: 1.6039x; 1.3154x over previous
import contextlib
import numpy as np
import concourse.bass as bass
import concourse.mybir as mybir
from concourse.bass_utils import run_bass_kernel_spmd

F32 = mybir.dt.float32
F16 = mybir.dt.float16
BF16 = mybir.dt.bfloat16
FP8 = mybir.dt.float8e4
AF = mybir.ActivationFunctionType
ALU = mybir.AluOpType
AX = mybir.AxisListType

D = 1024
H = 4096
K = 32
NT = 32
NTL = 33
SEQ = 4096
WROW = 160

SERIAL = False
DEBUG_ATTN_ONLY = False
DEBUG_PHASE0_ONLY = False
import os as _os
P0_CUT = int(_os.environ.get('P0_CUT', '9'))
MAXCYC = int(_os.environ.get('MAXCYC', '99'))
ACUT = int(_os.environ.get('ACUT', '9'))
AIN = int(_os.environ.get('AIN', '9'))
DBGOUT = int(_os.environ.get('DBGOUT', '5'))
FCUT = int(_os.environ.get('FCUT', '9'))
DJN = int(_os.environ.get('DJN', '32'))
MCUT = int(_os.environ.get('MCUT', '9'))
DDC = int(_os.environ.get('DDC', '2'))

(CA_SUM, CA_SQ, CA_MEAN, CA_MSQ, CA_MSQE, CA_VAR, CA_STD, CA_INV, CA_NMI,
 CB_SUM, CB_SQ, CB_MEAN, CB_MSQ, CB_MSQE, CB_VAR, CB_STD, CB_INV, CB_NMI,
 CG_SUM, CG_SQ, CG_M, CG_MSQ, CG_MSQE, CG_VAR, CG_STD, CG_INV,
 CH_A0, CH_A1, CH_SUM, CH_SQ, CH_M, CH_MSQ, CH_MSQE, CH_VAR, CH_STD, CH_INV,
 CQ1_MXR, CQ1_MX, CQ1_T, CQ1_S, CQ1_G,
 CQ2_MXR, CQ2_MX, CQ2_T, CQ2_S, CQ2_G,
 CS_NEGMX, CS_Z, CS_RZ) = range(49)
NCOLS = 50


class Plan:
    ENGINES = ("sync", "vector", "scalar", "tensor", "gpsimd")

    def __init__(self):
        self.ops = {e: [] for e in self.ENGINES}
        self.cnt = {}

    def inc(self, sem, by=1):
        self.cnt[sem] = self.cnt.get(sem, 0) + by
        return self.cnt[sem]

    def val(self, sem):
        return self.cnt.get(sem, 0)

    def add(self, engine, fn):
        self.ops[engine].append(fn)


def build_program():
    nc = bass.Bass("TRN2", target_bir_lowering=False,
                   detect_race_conditions=(_os.environ.get('RACED', '0') == '1'))

    x_in = nc.dram_tensor("x", [NTL * 128, D], F32, kind="ExternalInput")
    w1t_in = nc.dram_tensor("w1t", [D, H], F32, kind="ExternalInput")
    w2t_in = nc.dram_tensor("w2t", [H, D], F32, kind="ExternalInput")
    wat_in = nc.dram_tensor("wat", [D, K], F32, kind="ExternalInput")
    ba_in = nc.dram_tensor("battn", [128, K], F32, kind="ExternalInput")
    g1_in = nc.dram_tensor("g1", [128, D], F32, kind="ExternalInput")
    b1ln_in = nc.dram_tensor("b1ln", [128, D], F32, kind="ExternalInput")
    g2_in = nc.dram_tensor("g2", [128, H], F32, kind="ExternalInput")
    b2ln_in = nc.dram_tensor("b2ln", [128, H], F32, kind="ExternalInput")
    b1_in = nc.dram_tensor("b1", [128, H], F32, kind="ExternalInput")
    b2_in = nc.dram_tensor("b2", [128, D], F32, kind="ExternalInput")
    out_dr = nc.dram_tensor("out", [SEQ, D], F32, kind="ExternalOutput")
    whbm = nc.dram_tensor("whbm", [2 * (128 * WROW + 64)], F32, kind="Internal")

    identf_dr = nc.inline_tensor(np.eye(128, dtype=np.float32), name="identf_c")

    ctx = contextlib.ExitStack()

    def mk_sb(name, shape, dtype):
        return ctx.enter_context(nc.sbuf_tensor(name, shape, dtype))

    def mk_ps(name):
        return ctx.enter_context(nc.psum_tensor(name, [128, 512], F32))

    with ctx:
        W1T = [mk_sb(f"w1t_{g}", [128, H], FP8) for g in range(8)]
        W2T = [mk_sb(f"w2t_{j}", [128, D], FP8) for j in range(32)]
        WT = mk_sb("wt", [128, 8 * K], F32)
        G1 = mk_sb("g1t", [128, D], F32)
        B1LN = mk_sb("b1lnt", [128, D], F32)
        G2 = mk_sb("g2t", [128, H], F32)
        B2LN = mk_sb("b2lnt", [128, H], F32)
        B1 = mk_sb("b1t", [128, H], F32)
        B2 = mk_sb("b2t", [128, D], F32)
        BATB = mk_sb("batb", [128, K], F32)
        IDENT = mk_sb("ident", [128, 128], F32)
        IDENTB = mk_sb("identb", [128, 128], BF16)
        ONES128 = mk_sb("ones128", [128, 128], F32)
        ONECOL = mk_sb("onecol", [128, 1], F32)
        EPS = mk_sb("epst", [128, 1], F32)
        C127 = mk_sb("c127", [128, 1], F32)
        INV127 = mk_sb("inv127", [128, 1], F32)
        MW1 = mk_sb("mw1", [128, 1], F32)
        MW2 = mk_sb("mw2", [128, 1], F32)
        SCL1 = mk_sb("scl1", [128, 1], F32)
        SCL2 = mk_sb("scl2", [128, 1], F32)
        WPART = mk_sb("wpart", [128, 68], F32)
        ZEROT = mk_sb("zerot", [128, 128], F32)
        SPACER = mk_sb("spacer", [128, 128], F32)

        XT = [mk_sb(f"x_{r}", [128, D], F32) for r in range(2)]
        XLN = [mk_sb(f"xln_{r}", [128, D], F32) for r in range(2)]
        SCR16 = mk_sb("scr16", [128, D], F16)
        if _os.environ.get('XLNFM_REAL', '0') == '1':
            XLNFM = mk_sb("xlnfm", [128, 512], F32)[:]
        else:
            XLNFM = SCR16[:].bitcast(F32)
        LG = mk_sb("lg", [32, 128], F32)
        LOGITS = mk_sb("logits", [128, K], F32)
        WP = mk_sb("wp", [128, K], F32)
        MT_SB = mk_sb("mt_sb", [128, WROW], F32)
        MA = mk_sb("ma", [128, 128], F32)
        MB = mk_sb("mb", [32, 128], F32)
        X2 = [mk_sb(f"x2_{r}", [128, D], F32) for r in range(3)]
        ZY = mk_sb("zy", [128, D], F32)
        Q1 = mk_sb("q1", [128, D], BF16)
        Q1T = mk_sb("q1t", [128, D], BF16)
        Q2 = mk_sb("q2", [128, H], BF16)
        Q2T = mk_sb("q2t", [128, H], BF16)
        GEL = mk_sb("gel", [128, H], F32)
        OUT_T = mk_sb("out_t", [128, D], F32)
        ST = [mk_sb(f"st_{r}", [128, NCOLS], F32) for r in range(3)]
        GSUM = [mk_sb(f"gsum_{r}", [128, 8], F32) for r in range(2)]

        psA = mk_ps("psA")
        psM = mk_ps("psM")
        psB0 = mk_ps("psB0")
        psB1 = mk_ps("psB1")
        psC0 = mk_ps("psC0")
        psC1 = mk_ps("psC1")
        psD0 = mk_ps("psD0")
        psD1 = mk_ps("psD1")

        sems = {}
        for s in ("dx", "dw", "dc", "dt1", "dt2", "db", "do", "pe", "ve", "se",
                  "bar", "vh", "fn"):
            sems[s] = ctx.enter_context(nc.semaphore(s))

        P = Plan()
        pe_done, ve_done, se_done, dt_done, db_done, dx_done, do_done = \
            {}, {}, {}, {}, {}, {}, {}

        def op(eng, fn):
            P.add(eng, fn)

        vhc = [0]

        def VG(e, inst):
            e.tensor_copy(SPACER[:], ZEROT[:])

        def FN(e, inst):
            inst.then_inc(sems["fn"], 1)

        def col(r, c):
            return ST[r][:, c:c + 1]

        def barrier():
            snap = {s: P.val(s) for s in
                    ("pe", "ve", "se", "dx", "dw", "dt1", "dt2", "db", "do")}
            for eng in Plan.ENGINES:
                def b(e, snap=snap):
                    for s, v in snap.items():
                        if v > 0:
                            e.wait_ge(sems[s], v)
                op(eng, b)

        def ph0_memsets(e):
            e.memset(ONES128[:], 1.0)
            FN(e, e.memset(WPART[:, 64:68], 0.0))
            FN(e, e.memset(ONECOL[:], 1.0))
            FN(e, e.memset(EPS[:], 1e-5))
            e.memset(ZEROT[:], 0.0)
            FN(e, e.memset(C127[:], 127.0))
            VG(e, None)
            FN(e, e.reciprocal(INV127[:], C127[:]))
            e.tensor_copy(SPACER[:], ZEROT[:]).then_inc(sems["ve"], 1)
        op("vector", ph0_memsets)
        ve_ms = P.inc("ve")

        def ph0_loads(e):
            e.dma_start(out=IDENT[:], in_=identf_dr.ap()).then_inc(sems["dc"], 16)
            for g in range(8):
                e.dma_start(out=WT[:, g * K:(g + 1) * K],
                            in_=wat_in.ap()[g * 128:(g + 1) * 128, :]).then_inc(sems["dc"], 16)
            e.dma_start(out=BATB[:], in_=ba_in.ap()).then_inc(sems["dc"], 16)
            for src_d, dst_t in ((g1_in, G1), (b1ln_in, B1LN), (b2_in, B2),
                                 (g2_in, G2), (b2ln_in, B2LN), (b1_in, B1)):
                e.dma_start(out=dst_t[:], in_=src_d.ap()).then_inc(sems["dc"], 16)
            e.dma_start(out=BATB[:], in_=ba_in.ap()).then_inc(sems["dc"], 16)
        op("sync", ph0_loads)
        DC_ALL = 16 * 17
        P.inc("dc", DC_ALL)

        def ph0_identb(e):
            e.wait_ge(sems["dc"], 48)
            e.tensor_copy(IDENTB[:], IDENT[:]).then_inc(sems["ve"], 1)
        op("vector", ph0_identb)
        ve_identb = P.inc("ve")

        def ph0_whbm(e):
            e.wait_ge(sems["ve"], ve_ms)
            for r in range(2):
                e.dma_start(
                    out=bass.AP(whbm, r * (128 * WROW + 64) + 32, [[WROW, 128], [1, 128]]),
                    in_=ZEROT[:],
                ).then_inc(sems["db"], 16)
        op("gpsimd", ph0_whbm)
        P.inc("db", 32)

        op("tensor", lambda e: e.wait_ge(sems["ve"], ve_ms))

        def wpass1(nrowchunks, rowfree, src_dr, pcol0):
            npieces_per = rowfree // D
            pc = pcol0
            for cidx in range(nrowchunks):
                for si in range(npieces_per):
                    a = si * D
                    buf_off = ((cidx * npieces_per + si) % 4) * D
                    prev_se = P.val("se") - 3
                    def ld(e, cidx=cidx, a=a, buf_off=buf_off, src_dr=src_dr,
                           prev_se=prev_se):
                        if prev_se > 0:
                            e.wait_ge(sems["se"], prev_se)
                        e.dma_start(out=GEL[:, buf_off:buf_off + D],
                                    in_=src_dr.ap()[cidx * 128:(cidx + 1) * 128, a:a + D]
                                    ).then_inc(sems["dw"], 16)
                    op("gpsimd", ld)
                    dwv = P.inc("dw", 16)
                    def ab(e, dwv=dwv + 16, buf_off=buf_off, pc=pc):
                        e.wait_ge(sems["dw"], dwv)
                        e.activation(out=Q2[:, 0:D], in_=GEL[:, buf_off:buf_off + D],
                                     func=AF.Abs, bias=0.0, scale=1.0,
                                     accum_out=WPART[:, pc:pc + 1]).then_inc(sems["se"], 1)
                    op("scalar", ab)
                    P.inc("se")
                    pc += 1

        if P0_CUT >= 3:
            wpass1(8, H, w1t_in, 0)
            wpass1(32, D, w2t_in, 32)

        sev_abs = P.val("se")

        def wcomb_v(e):
            e.wait_ge(sems["se"], sev_abs)
            FN(e, e.reduce_sum(WPART[:, 64:65], WPART[:, 0:32], axis=AX.X))
            e.reduce_sum(WPART[:, 65:66], WPART[:, 32:64], axis=AX.X).then_inc(sems["ve"], 1)
        if P0_CUT >= 4:
            op("vector", wcomb_v)
        vev_c = P.inc("ve") if P0_CUT >= 4 else 0

        def wcomb_t(e):
            e.wait_ge(sems["ve"], vev_c)
            e.matmul(psM[0:1, 0:1], WPART[:, 64:65], ONECOL[:, 0:1], start=True, stop=True)
            e.matmul(psM[0:1, 1:2], WPART[:, 65:66], ONECOL[:, 0:1], start=True, stop=True).then_inc(sems["pe"], 1)
        if P0_CUT >= 4:
            op("tensor", wcomb_t)
        pev_c = P.inc("pe") if P0_CUT >= 4 else 0

        def wmean_v(e):
            e.wait_ge(sems["pe"], pev_c)
            e.tensor_scalar(out=WPART[0:1, 66:68], in0=psM[0:1, 0:2],
                            scalar1=float(1.0 / (D * H)), scalar2=1e-5,
                            op0=ALU.mult, op1=ALU.max).then_inc(sems["ve"], 1)
        if P0_CUT >= 4:
            op("vector", wmean_v)
        vev_m = P.inc("ve") if P0_CUT >= 4 else 0

        def wbc_t(e):
            e.wait_ge(sems["ve"], vev_m)
            e.matmul(psM[:, 2:3], ONES128[:], WPART[:, 66:67], start=True, stop=True)
            e.matmul(psM[:, 3:4], ONES128[:], WPART[:, 67:68], start=True, stop=True).then_inc(sems["pe"], 1)
        if P0_CUT >= 4:
            op("tensor", wbc_t)
        pev_b = P.inc("pe") if P0_CUT >= 4 else 0

        def wbc_v(e):
            e.wait_ge(sems["pe"], pev_b)
            FN(e, e.tensor_copy(MW1[:], psM[:, 2:3]))
            FN(e, e.tensor_copy(MW2[:], psM[:, 3:4]))
            VG(e, None)
            FN(e, e.reciprocal(SCL1[:], MW1[:]))
            e.reciprocal(SCL2[:], MW2[:]).then_inc(sems["ve"], 1)
        if P0_CUT >= 4:
            op("vector", wbc_v)
        vev_scl = P.inc("ve") if P0_CUT >= 4 else 0

        def wpass2(nrowchunks, rowfree, src_dr, dst_fn):
            npieces_per = rowfree // D
            first_overall = True
            for cidx in range(nrowchunks):
                for si in range(npieces_per):
                    a = si * D
                    buf_off = ((cidx * npieces_per + si) % 4) * D
                    prev_ve = P.val("ve") - 3
                    def ld(e, cidx=cidx, a=a, buf_off=buf_off, src_dr=src_dr,
                           prev_ve=prev_ve):
                        if prev_ve > 0:
                            e.wait_ge(sems["ve"], prev_ve)
                        e.dma_start(out=GEL[:, buf_off:buf_off + D],
                                    in_=src_dr.ap()[cidx * 128:(cidx + 1) * 128, a:a + D]
                                    ).then_inc(sems["dw"], 16)
                    op("gpsimd", ld)
                    dwv = P.inc("dw", 16)
                    dst_t, dst_off = dst_fn(cidx, a)
                    scl = SCL1 if src_dr is w1t_in else SCL2
                    fo = first_overall
                    first_overall = False
                    def qz(e, dwv=dwv + 16, buf_off=buf_off, dst_t=dst_t, dst_off=dst_off,
                           scl=scl, fo=fo):
                        e.wait_ge(sems["dw"], dwv)
                        if fo:
                            e.wait_ge(sems["ve"], vev_scl)
                            e.wait_ge(sems["se"], sev_abs)
                        e.tensor_scalar(out=ZY[:], in0=GEL[:, buf_off:buf_off + D],
                                        scalar1=scl[:, 0:1], scalar2=12582912.0,
                                        op0=ALU.mult, op1=ALU.add)
                        e.tensor_scalar(out=ZY[:], in0=ZY[:],
                                        scalar1=12582912.0, scalar2=None, op0=ALU.subtract)
                        e.tensor_scalar(out=dst_t[:, dst_off:dst_off + D],
                                        in0=ZY[:], scalar1=1.0, scalar2=-1.0,
                                        op0=ALU.min, op1=ALU.max).then_inc(sems["ve"], 1)
                    op("vector", qz)
                    P.inc("ve")

        if P0_CUT >= 5:
            wpass2(8, H, w1t_in, lambda cidx, a: (W1T[cidx], a))
            wpass2(32, D, w2t_in, lambda cidx, a: (W2T[cidx], 0))
            nlast = (8 * 4 + 32)
            ve_slot_free = P.val("ve") - 3
            def ld_tail(e, slot=(nlast % 4) * D, wv=ve_slot_free):
                e.wait_ge(sems["ve"], wv)
                e.dma_start(out=GEL[:, slot:slot + D],
                            in_=w1t_in.ap()[0:128, 0:D]).then_inc(sems["dw"], 16)
            op("gpsimd", ld_tail)
            P.inc("dw", 16)

        ve_wq_done = P.val("ve")
        if P0_CUT < 5:
            barrier()
            ve_wq_done = 0

        if DBGOUT == 14:
            wv_all = P.val("ve")
            def wdump(e, wv_all=wv_all):
                e.wait_ge(sems["ve"], wv_all)
                for g in range(8):
                    e.dma_start(out=out_dr.ap()[g * 128:(g + 1) * 128, :],
                                in_=W1T[g][:].bitcast(F32)).then_inc(sems["do"], 16)
                for j in range(24):
                    e.dma_start(out=out_dr.ap()[1024 + j * 128:1024 + (j + 1) * 128, 0:256],
                                in_=W2T[j][:].bitcast(F32)).then_inc(sems["do"], 16)
            op("sync", wdump)
            P.inc("do", 16 * 32)

        if DEBUG_PHASE0_ONLY:
            def dbg(e):
                e.tensor_copy(ZY[:], W1T[0][:, 0:D])
                e.tensor_copy(X2[0][:], W2T[0][:])
                e.tensor_copy(OUT_T[:, 0:1], MW1[:])
                e.tensor_copy(OUT_T[:, 1:2], MW2[:])
                e.tensor_copy(OUT_T[:, 2:3], SCL1[:])
                e.tensor_copy(OUT_T[:, 3:4], SCL2[:])
                e.tensor_copy(OUT_T[:, 4:132], IDENT[:])
                e.tensor_copy(OUT_T[:, 132:164], WT[:, 0:32])
                e.tensor_copy(OUT_T[:, 164:232], WPART[:]).then_inc(sems["ve"], 1)
            op("vector", dbg)
            vd = P.inc("ve")
            def dbg_st(e, vd=vd):
                e.wait_ge(sems["ve"], vd)
                e.dma_start(out=out_dr.ap()[0:128, :], in_=ZY[:]).then_inc(sems["do"], 16)
                e.dma_start(out=out_dr.ap()[128:256, :], in_=X2[0][:]).then_inc(sems["do"], 16)
                e.dma_start(out=out_dr.ap()[256:384, :], in_=G2[:, 0:D]).then_inc(sems["do"], 16)
                e.dma_start(out=out_dr.ap()[384:512, :], in_=B1[:, 0:D]).then_inc(sems["do"], 16)
                e.dma_start(out=out_dr.ap()[512:640, :], in_=OUT_T[:]).then_inc(sems["do"], 16)
                e.wait_ge(sems["do"], 80)
            op("sync", dbg_st)
            P.inc("do", 80)

        def emit_load_x(i):
            wv = ve_done.get(("attnev", i - 2), 0)
            def f(e, i=i, wv=wv):
                if wv:
                    e.wait_ge(sems["ve"], wv)
                e.dma_start(out=XT[i % 2][:],
                            in_=x_in.ap()[i * 128:(i + 1) * 128, :]).then_inc(sems["dx"], 16)
            op("sync", f)
            dx_done[i] = P.inc("dx", 16)

        def emit_stats_alpha(i):
            r = i % 3
            dxv = dx_done[i]
            w_r1 = ve_done.get(("round1", i - 2), 0)
            w_r2 = ve_done.get(("round2", i - 3), 0)
            def fs(e, i=i, r=r, dxv=dxv, w_r1=w_r1, w_r2=w_r2):
                e.wait_ge(sems["dx"], dxv)
                if w_r1:
                    e.wait_ge(sems["ve"], w_r1)
                if w_r2:
                    e.wait_ge(sems["ve"], w_r2)
                e.activation(out=SCR16[:], in_=XT[i % 2][:], func=AF.Square,
                             bias=0.0, scale=1.0,
                             accum_out=col(r, CA_SQ)).then_inc(sems["se"], 1)
            op("scalar", fs)
            se_sq = P.inc("se")

            def fv(e, i=i, r=r, dxv=dxv, se_sq=se_sq):
                e.wait_ge(sems["dx"], dxv)
                FN(e, e.reduce_sum(col(r, CA_SUM), XT[i % 2][:], axis=AX.X))
                VG(e, None)
                e.wait_ge(sems["se"], se_sq)
                FN(e, e.tensor_scalar(out=col(r, CA_MEAN), in0=col(r, CA_SUM),
                                      scalar1=1.0 / D, scalar2=None, op0=ALU.mult))
                VG(e, None)
                FN(e, e.tensor_tensor(col(r, CA_MSQ), col(r, CA_MEAN),
                                      col(r, CA_MEAN), ALU.mult))
                VG(e, None)
                FN(e, e.tensor_scalar(out=col(r, CA_MSQE), in0=col(r, CA_MSQ),
                                      scalar1=1.0, scalar2=1e-5,
                                      op0=ALU.mult, op1=ALU.subtract))
                VG(e, None)
                e.tensor_scalar(out=col(r, CA_VAR), in0=col(r, CA_SQ),
                                scalar1=1.0 / D, scalar2=col(r, CA_MSQE),
                                op0=ALU.mult, op1=ALU.subtract).then_inc(sems["ve"], 1)
            op("vector", fv)
            ve_var = P.inc("ve")

            def fs2(e, r=r, ve_var=ve_var):
                e.wait_ge(sems["ve"], ve_var)
                e.activation(out=col(r, CA_STD), in_=col(r, CA_VAR), func=AF.Sqrt,
                             bias=0.0, scale=1.0).then_inc(sems["se"], 1)
            op("scalar", fs2)
            se_sd = P.inc("se")

            def fv2(e, r=r, se_sd=se_sd):
                e.wait_ge(sems["se"], se_sd)
                FN(e, e.reciprocal(col(r, CA_INV), col(r, CA_STD)))
                VG(e, None)
                e.tensor_scalar(out=col(r, CA_NMI), in0=col(r, CA_MEAN), scalar1=-1.0,
                                scalar2=col(r, CA_INV), op0=ALU.mult, op1=ALU.mult).then_inc(sems["ve"], 1)
            op("vector", fv2)
            ve_done[("statsA", i)] = P.inc("ve")

        def emit_xln(i):
            r = i % 3
            wv = ve_done[("statsA", i)]
            wpe = pe_done.get(("band", i - 2), 0)
            def f(e, i=i, r=r, wv=wv, wpe=wpe):
                e.wait_ge(sems["ve"], wv)
                if wpe:
                    e.wait_ge(sems["pe"], wpe)
                e.activation(out=XLN[i % 2][:], in_=XT[i % 2][:], func=AF.Identity,
                             bias=col(r, CA_NMI), scale=col(r, CA_INV)
                             ).then_inc(sems["se"], 1)
            op("scalar", f)
            se_done[("xln", i)] = P.inc("se")

        def emit_A(i):
            sxln = se_done[("xln", i)]
            wv_prev = ve_done.get(("Ae2", i - 1), 0)

            def t1(e, i=i, sxln=sxln, wv_prev=wv_prev):
                e.wait_ge(sems["se"], sxln)
                if i <= 4 and vev_scl:
                    e.wait_ge(sems["ve"], vev_scl)
                if i <= 2:
                    e.wait_ge(sems["dc"], DC_ALL)
                if wv_prev:
                    e.wait_ge(sems["ve"], wv_prev)
                for g in range(4):
                    last = e.transpose(psA[:, g * 128:(g + 1) * 128],
                                       XLN[i % 2][:, g * 128:(g + 1) * 128], IDENT[:])
                last.then_inc(sems["pe"], 1)
            op("tensor", t1)
            pe_w1 = P.inc("pe")

            def v1(e, pe_w1=pe_w1):
                e.wait_ge(sems["pe"], pe_w1)
                e.tensor_copy(XLNFM[:, 0:512], psA[:, 0:512]).then_inc(sems["ve"], 1)
            op("vector", v1)
            ve_e1 = P.inc("ve")
            if AIN < 2:
                ve_done[("softmax", i)] = 0
                ve_done[("Ae2", i)] = ve_e1
                ve_done[("lgmx", i)] = 0
                return

            def t2(e, i=i, ve_e1=ve_e1):
                e.wait_ge(sems["ve"], ve_e1)
                for g in range(4):
                    e.transpose(psA[:, g * 128:(g + 1) * 128],
                                XLN[i % 2][:, 512 + g * 128:512 + (g + 1) * 128], IDENT[:])
                for g in range(4):
                    last = e.matmul(psM[0:32, 256:384], WT[:, g * K:(g + 1) * K],
                                    XLNFM[:, g * 128:(g + 1) * 128],
                                    start=(g == 0), stop=False)
                last.then_inc(sems["pe"], 1)
            op("tensor", t2)
            pe_w2 = P.inc("pe")

            def v2(e, pe_w2=pe_w2):
                e.wait_ge(sems["pe"], pe_w2)
                e.tensor_copy(XLNFM[:, 0:512], psA[:, 0:512]).then_inc(sems["ve"], 1)
            op("vector", v2)
            ve_e2 = P.inc("ve")
            ve_done[("Ae2", i)] = ve_e2
            if AIN < 3:
                ve_done[("softmax", i)] = 0
                ve_done[("lgmx", i)] = 0
                return

            def t3(e, ve_e2=ve_e2):
                e.wait_ge(sems["ve"], ve_e2)
                for g in range(4):
                    last = e.matmul(psM[0:32, 256:384], WT[:, (4 + g) * K:(5 + g) * K],
                                    XLNFM[:, g * 128:(g + 1) * 128],
                                    start=False, stop=(g == 3))
                last.then_inc(sems["pe"], 1)
            op("tensor", t3)
            pe_lmm = P.inc("pe")

            def s1(e, pe_lmm=pe_lmm):
                e.wait_ge(sems["pe"], pe_lmm)
                e.activation(out=LG[:], in_=psM[0:32, 256:384], func=AF.Identity,
                             bias=0.0, scale=1.0).then_inc(sems["se"], 1)
            op("scalar", s1)
            se_lg = P.inc("se")

            def t4(e, se_lg=se_lg):
                e.wait_ge(sems["se"], se_lg)
                e.transpose(psM[:, 384:416], LG[:], IDENT[0:32, 0:32]).then_inc(sems["pe"], 1)
            op("tensor", t4)
            pe_lgt = P.inc("pe")
            if AIN < 4:
                ve_done[("softmax", i)] = 0
                ve_done[("lgmx", i)] = 0
                return

            r = i % 3
            def v3(e, r=r, pe_lgt=pe_lgt, i=i):
                if i <= 2:
                    e.wait_ge(sems["dc"], DC_ALL)
                e.wait_ge(sems["pe"], pe_lgt)
                e.tensor_tensor(LOGITS[:], psM[:, 384:416], BATB[:], ALU.add)
                e.reduce_max(col(r, CS_NEGMX), LOGITS[:], axis=AX.X, negate=True).then_inc(sems["ve"], 1)
            op("vector", v3)
            ve_mx = P.inc("ve")
            ve_done[("lgmx", i)] = ve_mx

            wdb = db_done.get(("wst", i - 1), 0)
            def s2(e, r=r, ve_mx=ve_mx, wdb=wdb):
                e.wait_ge(sems["ve"], ve_mx)
                if wdb:
                    e.wait_ge(sems["db"], wdb)
                e.activation(out=WP[:], in_=LOGITS[:], func=AF.Exp,
                             bias=col(r, CS_NEGMX), scale=1.0,
                             accum_out=col(r, CS_Z)).then_inc(sems["se"], 1)
            op("scalar", s2)
            se_exp = P.inc("se")

            def v4(e, r=r, se_exp=se_exp):
                e.wait_ge(sems["se"], se_exp)
                FN(e, e.reciprocal(col(r, CS_RZ), col(r, CS_Z)))
                VG(e, None)
                e.tensor_scalar(out=WP[:], in0=WP[:], scalar1=col(r, CS_RZ),
                                scalar2=0.0, op0=ALU.mult, op1=ALU.add).then_inc(sems["ve"], 1)
            op("vector", v4)
            ve_done[("softmax", i)] = P.inc("ve")

        def emit_band_dma(i):
            wv = ve_done[("softmax", i)]
            base = (i % 2) * (128 * WROW + 64)
            wprev = db_done.get(("mtl", i - 2), 0)
            wdt1 = P.val("dt1")
            wdt2 = P.val("dt2")
            def f(e, wv=wv, base=base, wprev=wprev, wdt1=wdt1, wdt2=wdt2):
                e.wait_ge(sems["ve"], wv)
                if wprev:
                    e.wait_ge(sems["db"], wprev)
                if wdt1:
                    e.wait_ge(sems["dt1"], wdt1)
                if wdt2:
                    e.wait_ge(sems["dt2"], wdt2)
                e.dma_start(out=bass.AP(whbm, base, [[WROW, 128], [1, K]]),
                            in_=WP[:]).then_inc(sems["db"], 16)
            op("gpsimd", f)
            st_v = P.inc("db", 16)
            db_done[("wst", i)] = st_v

            wmt = pe_done.get(("mt", i - 1), 0)
            def f2(e, st_v=st_v, base=base, wmt=wmt):
                e.wait_ge(sems["db"], st_v)
                if wmt:
                    e.wait_ge(sems["pe"], wmt)
                e.dma_start(out=MT_SB[:],
                            in_=bass.AP(whbm, base, [[WROW - 1, 128], [1, WROW]])
                            ).then_inc(sems["db"], 16)
                e.dma_start(out=MT_SB[:, 0:4],
                            in_=bass.AP(whbm, base, [[WROW - 1, 128], [1, 4]])
                            ).then_inc(sems["db"], 16)
            op("gpsimd", f2)
            P.inc("db", 16)
            db_done[("mtl", i)] = P.inc("db", 16)

        def emit_MT(i):
            wdb = db_done[("mtl", i)]
            wv = ve_done.get(("lgmx", i), 0)
            def f(e, wdb=wdb, wv=wv):
                e.wait_ge(sems["db"], wdb)
                if wv:
                    e.wait_ge(sems["ve"], wv)
                e.transpose(psM[:, 0:128], MT_SB[:, 0:128], IDENT[:])
                e.transpose(psM[0:32, 128:256], MT_SB[:, 128:160], IDENT[:]).then_inc(sems["pe"], 1)
            op("tensor", f)
            pe_done[("mt", i)] = P.inc("pe")

        def emit_Mevac(i):
            wpe = pe_done[("mt", i)]
            def f(e, wpe=wpe):
                e.wait_ge(sems["pe"], wpe)
                e.tensor_copy(MA[:], psM[:, 0:128])
                e.tensor_copy(MB[0:32, :], psM[0:32, 128:256]).then_inc(sems["ve"], 1)
            op("vector", f)
            ve_done[("mev", i)] = P.inc("ve")

        def emit_B(i):
            wv = ve_done[("mev", i)]
            wx = se_done[("xln", i + 1)]
            wev = ve_done.get(("attnev", i - 1), 0)
            def f(e, i=i, wv=wv, wx=wx, wev=wev):
                e.wait_ge(sems["ve"], wv)
                e.wait_ge(sems["se"], wx)
                if wev:
                    e.wait_ge(sems["ve"], wev)
                for dm, ps in ((0, psB0), (1, psB1)):
                    e.matmul(ps[:], MA[:], XLN[i % 2][:, dm * 512:(dm + 1) * 512],
                             start=True, stop=False)
                    last = e.matmul(ps[:], MB[0:32, :],
                                    XLN[(i + 1) % 2][0:32, dm * 512:(dm + 1) * 512],
                                    start=False, stop=True)
                last.then_inc(sems["pe"], 1)
            op("tensor", f)
            pe_done[("band", i)] = P.inc("pe")

        def emit_attn_evac(i):
            wpe = pe_done[("band", i)]
            wv2 = ve_done.get(("evac2", i - 3), 0)
            def f(e, i=i, wpe=wpe, wv2=wv2):
                e.wait_ge(sems["pe"], wpe)
                if wv2:
                    e.wait_ge(sems["ve"], wv2)
                for dm, ps in ((0, psB0), (1, psB1)):
                    last = e.scalar_tensor_tensor(out=X2[i % 3][:, dm * 512:(dm + 1) * 512],
                                                  in0=ps[:], scalar=1.0,
                                                  in1=XT[i % 2][:, dm * 512:(dm + 1) * 512],
                                                  op0=ALU.mult, op1=ALU.add)
                last.then_inc(sems["ve"], 1)
            op("vector", f)
            ve_done[("attnev", i)] = P.inc("ve")

        def emit_chain_head(i):
            r = i % 3
            wev = ve_done[("attnev", i)]
            x2t = X2[i % 3]

            def s_sqB(e, x2t=x2t, r=r, wev=wev):
                e.wait_ge(sems["ve"], wev)
                e.activation(out=SCR16[:], in_=x2t[:], func=AF.Square,
                             bias=0.0, scale=1.0,
                             accum_out=col(r, CB_SQ)).then_inc(sems["se"], 1)
            op("scalar", s_sqB)
            se_sqB = P.inc("se")

            def v_stB(e, x2t=x2t, r=r, wev=wev, se_sqB=se_sqB):
                e.wait_ge(sems["ve"], wev)
                FN(e, e.reduce_sum(col(r, CB_SUM), x2t[:], axis=AX.X))
                VG(e, None)
                e.wait_ge(sems["se"], se_sqB)
                FN(e, e.tensor_scalar(out=col(r, CB_MEAN), in0=col(r, CB_SUM),
                                      scalar1=1.0 / D, scalar2=None, op0=ALU.mult))
                VG(e, None)
                FN(e, e.tensor_tensor(col(r, CB_MSQ), col(r, CB_MEAN),
                                      col(r, CB_MEAN), ALU.mult))
                VG(e, None)
                FN(e, e.tensor_scalar(out=col(r, CB_MSQE), in0=col(r, CB_MSQ),
                                      scalar1=1.0, scalar2=1e-5,
                                      op0=ALU.mult, op1=ALU.subtract))
                VG(e, None)
                e.tensor_scalar(out=col(r, CB_VAR), in0=col(r, CB_SQ),
                                scalar1=1.0 / D, scalar2=col(r, CB_MSQE),
                                op0=ALU.mult, op1=ALU.subtract).then_inc(sems["ve"], 1)
            op("vector", v_stB)
            ve_vB = P.inc("ve")

            def s_sdB(e, r=r, ve_vB=ve_vB):
                e.wait_ge(sems["ve"], ve_vB)
                e.activation(out=col(r, CB_STD), in_=col(r, CB_VAR), func=AF.Sqrt,
                             bias=0.0, scale=1.0).then_inc(sems["se"], 1)
            op("scalar", s_sdB)
            se_sdB = P.inc("se")

            def v_invB(e, r=r, se_sdB=se_sdB):
                e.wait_ge(sems["se"], se_sdB)
                FN(e, e.reciprocal(col(r, CB_INV), col(r, CB_STD)))
                VG(e, None)
                e.tensor_scalar(out=col(r, CB_NMI), in0=col(r, CB_MEAN), scalar1=-1.0,
                                scalar2=col(r, CB_INV), op0=ALU.mult, op1=ALU.mult).then_inc(sems["ve"], 1)
            op("vector", v_invB)
            ve_invB = P.inc("ve")

            def s_z(e, x2t=x2t, r=r, ve_invB=ve_invB, i=i):
                if i <= 2 and ve_wq_done:
                    e.wait_ge(sems["ve"], ve_wq_done)
                if i <= 2:
                    e.wait_ge(sems["dc"], DC_ALL)
                e.wait_ge(sems["ve"], ve_invB)
                e.activation(out=ZY[:], in_=x2t[:], func=AF.Identity,
                             bias=col(r, CB_NMI), scale=col(r, CB_INV)
                             ).then_inc(sems["se"], 1)
            op("scalar", s_z)
            se_z = P.inc("se")

            def v_b2(e, x2t=x2t, se_z=se_z):
                e.wait_ge(sems["se"], se_z)
                e.tensor_tensor(x2t[:], x2t[:], B2[:], ALU.add).then_inc(sems["ve"], 1)
            op("vector", v_b2)
            P.inc("ve")

            def s_sqG(e, r=r, se_z=se_z):
                e.activation(out=SCR16[:], in_=ZY[:], func=AF.Square,
                             bias=0.0, scale=1.0,
                             accum_out=col(r, CG_SQ)).then_inc(sems["se"], 1)
            op("scalar", s_sqG)
            se_sqG = P.inc("se")

            def v_stG(e, r=r, se_sqG=se_sqG):
                e.wait_ge(sems["se"], se_sqG)
                FN(e, e.reduce_sum(col(r, CG_SUM), ZY[:], axis=AX.X))
                VG(e, None)
                FN(e, e.tensor_scalar(out=col(r, CG_M), in0=col(r, CG_SUM),
                                      scalar1=1.0 / D, scalar2=None, op0=ALU.mult))
                VG(e, None)
                FN(e, e.tensor_tensor(col(r, CG_MSQ), col(r, CG_M),
                                      col(r, CG_M), ALU.mult))
                VG(e, None)
                FN(e, e.tensor_scalar(out=col(r, CG_MSQE), in0=col(r, CG_MSQ),
                                      scalar1=1.0, scalar2=1e-5,
                                      op0=ALU.mult, op1=ALU.subtract))
                VG(e, None)
                e.tensor_scalar(out=col(r, CG_VAR), in0=col(r, CG_SQ),
                                scalar1=1.0 / D, scalar2=col(r, CG_MSQE),
                                op0=ALU.mult, op1=ALU.subtract).then_inc(sems["ve"], 1)
            op("vector", v_stG)
            ve_vG = P.inc("ve")

            def s_sdG(e, r=r, ve_vG=ve_vG):
                e.wait_ge(sems["ve"], ve_vG)
                e.activation(out=col(r, CG_STD), in_=col(r, CG_VAR), func=AF.Sqrt,
                             bias=0.0, scale=1.0).then_inc(sems["se"], 1)
            op("scalar", s_sdG)
            se_sdG = P.inc("se")

            def v_ln1a(e, r=r, se_sdG=se_sdG):
                e.wait_ge(sems["se"], se_sdG)
                FN(e, e.reciprocal(col(r, CG_INV), col(r, CG_STD)))
                e.scalar_tensor_tensor(out=ZY[:], in0=ZY[:], scalar=col(r, CG_M),
                                       in1=G1[:], op0=ALU.subtract, op1=ALU.mult).then_inc(sems["ve"], 1)
            op("vector", v_ln1a)
            ve_ln1a = P.inc("ve")

            def s_ln1b(e, r=r, ve_ln1a=ve_ln1a):
                e.wait_ge(sems["ve"], ve_ln1a)
                e.activation(out=ZY[:], in_=ZY[:], func=AF.Identity,
                             bias=0.0, scale=col(r, CG_INV)).then_inc(sems["se"], 1)
            op("scalar", s_ln1b)
            se_ln1b = P.inc("se")

            wdt1 = dt_done.get(("q1t", i - 1), 0)
            def v_ln1c(e, r=r, se_ln1b=se_ln1b, wdt1=wdt1):
                e.wait_ge(sems["se"], se_ln1b)
                e.tensor_tensor(ZY[:], ZY[:], B1LN[:], ALU.add)
                FN(e, e.reduce_max(col(r, CQ1_MXR), ZY[:], axis=AX.X,
                                   apply_absolute_value=True))
                VG(e, None)
                FN(e, e.tensor_scalar(out=col(r, CQ1_MX), in0=col(r, CQ1_MXR),
                                      scalar1=1e-5, scalar2=None, op0=ALU.max))
                VG(e, None)
                FN(e, e.reciprocal(col(r, CQ1_T), col(r, CQ1_MX)))
                VG(e, None)
                FN(e, e.tensor_scalar(out=col(r, CQ1_S), in0=col(r, CQ1_T), scalar1=127.0,
                                      scalar2=0.0, op0=ALU.mult, op1=ALU.add))
                FN(e, e.tensor_scalar(out=col(r, CQ1_G), in0=col(r, CQ1_MX),
                                      scalar1=MW1[:, 0:1], scalar2=INV127[:, 0:1],
                                      op0=ALU.mult, op1=ALU.mult))
                if wdt1:
                    e.wait_ge(sems["dt1"], wdt1)
                e.tensor_scalar(out=ZY[:], in0=ZY[:], scalar1=col(r, CQ1_S),
                                scalar2=12582912.0, op0=ALU.mult, op1=ALU.add)
                e.tensor_scalar(out=Q1[:], in0=ZY[:], scalar1=12582912.0,
                                scalar2=None, op0=ALU.subtract).then_inc(sems["ve"], 1)
            op("vector", v_ln1c)
            ve_done[("round1", i)] = P.inc("ve")

        def emit_q1T(i):
            wv = ve_done[("round1", i)]
            wpe = pe_done.get(("c7", i - 1), 0)
            wdb = P.val("db")
            def f(e, i=i, wv=wv, wpe=wpe, wdb=wdb):
                e.wait_ge(sems["ve"], wv)
                if wpe:
                    e.wait_ge(sems["pe"], wpe)
                if wdb:
                    e.wait_ge(sems["db"], wdb)
                for g in range(8):
                    e.dma_start_transpose(Q1T[:, g * 128:(g + 1) * 128],
                                          Q1[:, g * 128:(g + 1) * 128]
                                          ).then_inc(sems["dt1"], 16)
                e.dma_start_transpose(Q1T[:, 0:128],
                                      Q1[:, 0:128]).then_inc(sems["dt1"], 16)
            op("sync", f)
            dt_done[("q1t", i)] = P.inc("dt1", 16 * 9)
            if DBGOUT == 10:
                wv2 = ve_done[("round1", i)]
                def fd(e, i=i, wv2=wv2):
                    e.wait_ge(sems["ve"], wv2)
                    e.dma_start(out=out_dr.ap()[i * 128:(i + 1) * 128, :],
                                in_=ZY[:]).then_inc(sems["do"], 16)
                op("sync", fd)
                P.inc("do", 16)

        def emit_C(i):
            wdt = dt_done[("q1t", i)]
            w_r2 = ve_done.get(("round2", i - 1), 0)
            for hc in range(8):
                ps = psC0 if hc % 2 == 0 else psC1
                wev = ve_done.get(("evac1", i, hc - 2), 0)
                if hc == 0:
                    wev = ve_done.get(("evac1", i - 1, 6), 0)
                elif hc == 1:
                    wev = ve_done.get(("evac1", i - 1, 7), 0)
                def f(e, i=i, hc=hc, ps=ps, wdt=wdt, wev=wev):
                    if hc == 0:
                        if i <= 2 and ve_wq_done:
                            e.wait_ge(sems["ve"], ve_wq_done)
                        e.wait_ge(sems["dt1"], wdt)
                    if wev:
                        e.wait_ge(sems["ve"], wev)
                    for g in range(8):
                        last = e.matmul(ps[:], Q1T[:, g * 128:(g + 1) * 128],
                                        W1T[g][:, hc * 512:(hc + 1) * 512],
                                        start=(g == 0), stop=(g == 7))
                    last.then_inc(sems["pe"], 1)
                op("tensor", f)
                pe_done[("c", i, hc)] = P.inc("pe")
                if hc == 7:
                    pe_done[("c7", i)] = pe_done[("c", i, hc)]

                r = i % 3
                wpe = pe_done[("c", i, hc)]
                def fv(e, i=i, hc=hc, ps=ps, r=r, wpe=wpe, w_r2=w_r2):
                    e.wait_ge(sems["pe"], wpe)
                    if hc == 0 and w_r2:
                        e.wait_ge(sems["ve"], w_r2)
                    e.scalar_tensor_tensor(out=GEL[:, hc * 512:(hc + 1) * 512],
                                           in0=ps[:], scalar=col(r, CQ1_G),
                                           in1=B1[:, hc * 512:(hc + 1) * 512],
                                           op0=ALU.mult, op1=ALU.add).then_inc(sems["ve"], 1)
                op("vector", fv)
                ve_done[("evac1", i, hc)] = P.inc("ve")

                wv1 = ve_done[("evac1", i, hc)]
                def fs(e, i=i, hc=hc, wv1=wv1):
                    e.wait_ge(sems["ve"], wv1)
                    e.activation(out=GEL[:, hc * 512:(hc + 1) * 512],
                                 in_=GEL[:, hc * 512:(hc + 1) * 512], func=AF.Gelu,
                                 bias=0.0, scale=1.0,
                                 accum_out=GSUM[i % 2][:, hc:hc + 1]).then_inc(sems["se"], 1)
                op("scalar", fs)
                se_done[("gelu", i, hc)] = P.inc("se")
            if DBGOUT == 13:
                ws2 = se_done[("gelu", i, 7)]
                def fd(e, i=i, ws2=ws2):
                    e.wait_ge(sems["se"], ws2)
                    e.dma_start(out=out_dr.ap()[i * 128:(i + 1) * 128, :],
                                in_=GEL[:, 0:D]).then_inc(sems["do"], 16)
                op("sync", fd)
                P.inc("do", 16)

        def emit_chain_mid(i):
            r = i % 3
            wse = se_done[("gelu", i, 7)]
            wdt2 = pe_done.get(("q2tlast", i - 1), 0)

            def s_sqH(e, i=i, r=r, wse=wse, wdt2=wdt2):
                e.wait_ge(sems["se"], wse)
                if wdt2:
                    e.wait_ge(sems["pe"], wdt2)
                FN(e, e.activation(out=Q2[:, 0:2048], in_=GEL[:, 0:2048], func=AF.Square,
                                   bias=0.0, scale=1.0, accum_out=col(r, CH_A0)))
                e.activation(out=Q2[:, 2048:4096], in_=GEL[:, 2048:4096], func=AF.Square,
                             bias=0.0, scale=1.0, accum_out=col(r, CH_A1)
                             ).then_inc(sems["se"], 1)
            op("scalar", s_sqH)
            se_sqH = P.inc("se")
            if MCUT < 2:
                ve_done[("round2", i)] = 0
                return

            def v_stH(e, i=i, r=r, se_sqH=se_sqH):
                e.wait_ge(sems["se"], se_sqH)
                FN(e, e.reduce_sum(col(r, CH_SUM), GSUM[i % 2][:], axis=AX.X))
                FN(e, e.tensor_tensor(col(r, CH_SQ), col(r, CH_A0), col(r, CH_A1), ALU.add))
                FN(e, e.tensor_scalar(out=col(r, CH_M), in0=col(r, CH_SUM),
                                      scalar1=1.0 / H, scalar2=None, op0=ALU.mult))
                VG(e, None)
                FN(e, e.tensor_tensor(col(r, CH_MSQ), col(r, CH_M), col(r, CH_M), ALU.mult))
                VG(e, None)
                FN(e, e.tensor_scalar(out=col(r, CH_MSQE), in0=col(r, CH_MSQ),
                                      scalar1=1.0, scalar2=1e-5,
                                      op0=ALU.mult, op1=ALU.subtract))
                VG(e, None)
                e.tensor_scalar(out=col(r, CH_VAR), in0=col(r, CH_SQ),
                                scalar1=1.0 / H, scalar2=col(r, CH_MSQE),
                                op0=ALU.mult, op1=ALU.subtract).then_inc(sems["ve"], 1)
            op("vector", v_stH)
            ve_vH = P.inc("ve")

            def s_sdH(e, r=r, ve_vH=ve_vH):
                e.wait_ge(sems["ve"], ve_vH)
                e.activation(out=col(r, CH_STD), in_=col(r, CH_VAR), func=AF.Sqrt,
                             bias=0.0, scale=1.0).then_inc(sems["se"], 1)
            op("scalar", s_sdH)
            se_sdH = P.inc("se")
            if MCUT < 3:
                ve_done[("round2", i)] = 0
                return

            def v_ln2a(e, r=r, se_sdH=se_sdH):
                e.wait_ge(sems["se"], se_sdH)
                FN(e, e.reciprocal(col(r, CH_INV), col(r, CH_STD)))
                e.scalar_tensor_tensor(out=GEL[:], in0=GEL[:], scalar=col(r, CH_M),
                                       in1=G2[:], op0=ALU.subtract, op1=ALU.mult).then_inc(sems["ve"], 1)
            op("vector", v_ln2a)
            ve_ln2a = P.inc("ve")
            if MCUT == 4:
                ve_done[("round2", i)] = 0
                return

            def s_ln2b(e, r=r, ve_ln2a=ve_ln2a):
                e.wait_ge(sems["ve"], ve_ln2a)
                e.activation(out=GEL[:], in_=GEL[:], func=AF.Identity,
                             bias=0.0, scale=col(r, CH_INV)).then_inc(sems["se"], 1)
            op("scalar", s_ln2b)
            se_ln2b = P.inc("se")
            if MCUT == 5:
                ve_done[("round2", i)] = 0
                return

            def v_ln2c(e, r=r, se_ln2b=se_ln2b):
                e.wait_ge(sems["se"], se_ln2b)
                e.tensor_tensor(GEL[:], GEL[:], B2LN[:], ALU.add)
                FN(e, e.reduce_max(col(r, CQ2_MXR), GEL[:], axis=AX.X,
                                   apply_absolute_value=True))
                VG(e, None)
                FN(e, e.tensor_scalar(out=col(r, CQ2_MX), in0=col(r, CQ2_MXR),
                                      scalar1=1e-5, scalar2=None, op0=ALU.max))
                VG(e, None)
                FN(e, e.reciprocal(col(r, CQ2_T), col(r, CQ2_MX)))
                VG(e, None)
                FN(e, e.tensor_scalar(out=col(r, CQ2_S), in0=col(r, CQ2_T), scalar1=127.0,
                                      scalar2=0.0, op0=ALU.mult, op1=ALU.add))
                FN(e, e.tensor_scalar(out=col(r, CQ2_G), in0=col(r, CQ2_MX),
                                      scalar1=MW2[:, 0:1], scalar2=INV127[:, 0:1],
                                      op0=ALU.mult, op1=ALU.mult))

                for qq in range(4):
                    e.tensor_scalar(out=GEL[:, qq * D:(qq + 1) * D],
                                    in0=GEL[:, qq * D:(qq + 1) * D],
                                    scalar1=col(r, CQ2_S), scalar2=12582912.0,
                                    op0=ALU.mult, op1=ALU.add)
                    last = e.tensor_scalar(out=Q2[:, qq * D:(qq + 1) * D],
                                           in0=GEL[:, qq * D:(qq + 1) * D],
                                           scalar1=12582912.0, scalar2=None, op0=ALU.subtract)
                last.then_inc(sems["ve"], 1)
            op("vector", v_ln2c)
            ve_done[("round2", i)] = P.inc("ve")

        def emit_q2T(i):
            wv = ve_done[("round2", i)]
            wpe = pe_done.get(("d1", i - 1), 0)
            wev = ve_done.get(("evac2", i - 1), 0)
            for g in range(4):
                psb = (psD0 if g % 2 == 0 else psD1)[:].bitcast(BF16)
                wse_prev = se_done.get(("q2tev", i, g - 2), 0)
                def ft(e, i=i, g=g, psb=psb, wv=wv, wpe=wpe, wev=wev,
                       wse_prev=wse_prev):
                    if g == 0:
                        e.wait_ge(sems["ve"], wv)
                        if wpe:
                            e.wait_ge(sems["pe"], wpe)
                        if wev:
                            e.wait_ge(sems["ve"], wev)
                    if wse_prev:
                        e.wait_ge(sems["se"], wse_prev)
                    for k in range(8):
                        last = e.transpose(psb[:, k * 128:(k + 1) * 128],
                                           Q2[:, (g * 8 + k) * 128:(g * 8 + k + 1) * 128],
                                           IDENTB[:])
                    last.then_inc(sems["pe"], 1)
                op("tensor", ft)
                pe_done[("q2t", i, g)] = P.inc("pe")
                wpe2 = pe_done[("q2t", i, g)]
                def fe(e, i=i, g=g, psb=psb, wpe2=wpe2):
                    e.wait_ge(sems["pe"], wpe2)
                    e.activation(out=Q2T[:, g * 1024:(g + 1) * 1024], in_=psb[:],
                                 func=AF.Identity, bias=0.0,
                                 scale=1.0).then_inc(sems["se"], 1)
                op("scalar", fe)
                se_done[("q2tev", i, g)] = P.inc("se")
            pe_done[("q2tlast", i)] = pe_done[("q2t", i, 3)]
            dt_done[("q2t", i)] = se_done[("q2tev", i, 3)]
            if DBGOUT in (11, 12):
                wv2 = ve_done[("round2", i)]
                off = 0 if DBGOUT == 11 else 1024
                def fd(e, i=i, wv2=wv2, off=off):
                    e.wait_ge(sems["ve"], wv2)
                    e.dma_start(out=out_dr.ap()[i * 128:(i + 1) * 128, :],
                                in_=GEL[:, off:off + D]).then_inc(sems["do"], 16)
                op("sync", fd)
                P.inc("do", 16)

        def emit_D(i):
            wdt = dt_done[("q2t", i)]
            wev = ve_done.get(("evac2", i - 1), 0)
            def f(e, i=i, wdt=wdt, wev=wev):
                if i <= 2 and ve_wq_done:
                    e.wait_ge(sems["ve"], ve_wq_done)
                e.wait_ge(sems["se"], wdt)
                if wev:
                    e.wait_ge(sems["ve"], wev)
                for j in range(32):
                    for dc, ps in ((0, psD0), (1, psD1)):
                        last = e.matmul(ps[:], Q2T[:, j * 128:(j + 1) * 128],
                                        W2T[j][:, dc * 512:(dc + 1) * 512],
                                        start=(j == 0), stop=(j == 31))
                last.then_inc(sems["pe"], 1)
            op("tensor", f)
            pe_done[("d", i, 0)] = pe_done[("d", i, 1)] = P.inc("pe")
            pe_done[("d1", i)] = pe_done[("d", i, 1)]

        def emit_evac2(i):
            r = i % 3
            wpe = pe_done[("d", i, 1)]
            wdo = do_done.get(i - 1, 0)
            def f(e, i=i, r=r, wpe=wpe, wdo=wdo):
                e.wait_ge(sems["pe"], wpe)
                if wdo:
                    e.wait_ge(sems["do"], wdo)
                for dc, ps in ((0, psD0), (1, psD1)):
                    last = e.scalar_tensor_tensor(out=OUT_T[:, dc * 512:(dc + 1) * 512],
                                                  in0=ps[:], scalar=col(r, CQ2_G),
                                                  in1=X2[i % 3][:, dc * 512:(dc + 1) * 512],
                                                  op0=ALU.mult, op1=ALU.add)
                last.then_inc(sems["ve"], 1)
            op("vector", f)
            ve_done[("evac2", i)] = P.inc("ve")

        def emit_store(i):
            wv = ve_done[("evac2", i)]
            def f(e, i=i, wv=wv):
                e.wait_ge(sems["ve"], wv)
                e.dma_start(out=out_dr.ap()[i * 128:(i + 1) * 128, :],
                            in_=OUT_T[:]).then_inc(sems["do"], 16)
            op("sync", f)
            do_done[i] = P.inc("do", 16)

        def emit_store_debug_x2(i):
            wv = ve_done[("attnev", i)]
            def f(e, i=i, wv=wv):
                e.wait_ge(sems["ve"], wv)
                if DBGOUT == 1:
                    e.dma_start(out=out_dr.ap()[i * 128:(i + 1) * 128, :],
                                in_=XT[i % 2][:]).then_inc(sems["do"], 16)
                elif DBGOUT == 2:
                    e.dma_start(out=out_dr.ap()[i * 128:(i + 1) * 128, :],
                                in_=XLN[i % 2][:]).then_inc(sems["do"], 16)
                elif DBGOUT == 3:
                    e.dma_start(out=out_dr.ap()[i * 128:(i + 1) * 128, 0:K],
                                in_=WP[:]).then_inc(sems["do"], 16)
                elif DBGOUT == 4:
                    e.dma_start(out=out_dr.ap()[i * 128:(i + 1) * 128, 0:WROW],
                                in_=MT_SB[:]).then_inc(sems["do"], 16)
                elif DBGOUT == 8:
                    e.dma_start(out=out_dr.ap()[i * 128:i * 128 + 32, 0:128],
                                in_=MB[0:32, :]).then_inc(sems["do"], 16)
                elif DBGOUT == 7:
                    e.dma_start(out=out_dr.ap()[i * 128:(i + 1) * 128, 0:128],
                                in_=MA[:]).then_inc(sems["do"], 16)
                elif DBGOUT == 6:
                    e.dma_start(out=out_dr.ap()[i * 128:(i + 1) * 128, 0:NCOLS],
                                in_=ST[i % 3][:]).then_inc(sems["do"], 16)
                else:
                    e.dma_start(out=out_dr.ap()[i * 128:(i + 1) * 128, :],
                                in_=X2[i % 3][:]).then_inc(sems["do"], 16)
            op("sync", f)
            do_done[i] = P.inc("do", 16)

        for k in (range(-2, min(NT + 3, MAXCYC)) if not DEBUG_PHASE0_ONLY else []):
            if 0 <= k - 3 <= NT - 1 and not DEBUG_ATTN_ONLY and FCUT >= 6:
                emit_D(k - 3)
                if FCUT >= 7:
                    emit_evac2(k - 3)
                    if FCUT >= 8:
                        emit_store(k - 3)
            if 0 <= k <= NT - 1 and ACUT >= 7:
                emit_Mevac(k)
            if k < 0 and 0 <= k + 2 <= NTL - 1:
                emit_load_x(k + 2)
            if 0 <= k + 1 <= NTL - 1 and ACUT >= 2:
                emit_stats_alpha(k + 1)
                if ACUT >= 3:
                    emit_xln(k + 1)
            if 0 <= k - 2 <= NT - 1 and not DEBUG_ATTN_ONLY and FCUT >= 4:
                emit_chain_mid(k - 2)
            if 0 <= k + 1 <= NT - 1 and ACUT >= 4:
                emit_A(k + 1)
                if ACUT >= 5:
                    emit_band_dma(k + 1)
            if 0 <= k <= NT - 1 and ACUT >= 7:
                emit_B(k)
                emit_attn_evac(k)
                if DEBUG_ATTN_ONLY or FCUT < 8:
                    emit_store_debug_x2(k)
            if False:
                pass
            if k >= 0 and 0 <= k + 2 <= NTL - 1:
                emit_load_x(k + 2)
            if 0 <= k - 1 <= NT - 1 and not DEBUG_ATTN_ONLY and FCUT >= 3:
                emit_C(k - 1)
            if 0 <= k - 2 <= NT - 1 and not DEBUG_ATTN_ONLY and FCUT >= 5 and MCUT >= 3:
                emit_q2T(k - 2)
            if 0 <= k <= NT - 1 and not DEBUG_ATTN_ONLY and FCUT >= 1:
                emit_chain_head(k)
                if FCUT >= 2:
                    emit_q1T(k)
            if 0 <= k + 1 <= NT - 1 and ACUT >= 6:
                emit_MT(k + 1)
            if SERIAL:
                barrier()

        do_total = P.val("do")
        op("sync", lambda e, v=do_total: e.wait_ge(sems["do"], v))

        with nc.Block() as block:
            @block.sync
            def _(eng):
                for f in P.ops["sync"]:
                    f(eng)

            @block.vector
            def _(eng):
                for f in P.ops["vector"]:
                    f(eng)

            @block.scalar
            def _(eng):
                for f in P.ops["scalar"]:
                    f(eng)

            @block.tensor
            def _(eng):
                for f in P.ops["tensor"]:
                    f(eng)

            @block.gpsimd
            def _(eng):
                for f in P.ops["gpsimd"]:
                    f(eng)

    nc._dbg_plan = dict(P.cnt)
    return nc


def _bcast128(v):
    v = np.asarray(v, np.float32)
    return np.ascontiguousarray(np.broadcast_to(v[None, :], (128, v.shape[0])))


def _shard_inputs(L0_attn_x, L0_attn_W, L0_attn_b, L0_ffn_ln1_g, L0_ffn_ln1_b,
                  L0_ffn_w1, L0_ffn_b1, L0_ffn_ln2_g, L0_ffn_ln2_b, L0_ffn_w2,
                  L0_ffn_b2):
    x = np.ascontiguousarray(np.asarray(L0_attn_x, dtype=np.float32))
    B, L, Dm = x.shape
    shared = dict(
        w1t=np.ascontiguousarray(np.asarray(L0_ffn_w1, np.float32).T),
        w2t=np.ascontiguousarray(np.asarray(L0_ffn_w2, np.float32).T),
        wat=np.ascontiguousarray(np.asarray(L0_attn_W, np.float32).T),
        battn=_bcast128(L0_attn_b),
        g1=_bcast128(L0_ffn_ln1_g),
        b1ln=_bcast128(L0_ffn_ln1_b),
        g2=_bcast128(L0_ffn_ln2_g),
        b2ln=_bcast128(L0_ffn_ln2_b),
        b1=_bcast128(L0_ffn_b1),
        b2=_bcast128(L0_ffn_b2),
    )
    in_maps = []
    for c in range(8):
        b = c // 2
        half = c % 2
        s = half * SEQ
        xs = np.zeros((NTL * 128, Dm), np.float32)
        avail = min(NTL * 128, L - s)
        xs[:avail] = x[b, s:s + avail]
        m = dict(shared)
        m["x"] = xs
        in_maps.append(m)
    return in_maps


def kernel(**inputs):
    in_maps = _shard_inputs(**inputs)
    nc = build_program()
    res = run_bass_kernel_spmd(nc, in_maps, core_ids=list(range(8)))
    out = np.empty((4, 8192, 1024), np.float32)
    for c in range(8):
        b = c // 2
        half = c % 2
        out[b, half * SEQ:(half + 1) * SEQ] = np.asarray(res.results[c]["out"],
                                                         np.float32)
    return out


# version 15
# speedup vs baseline: 1.6072x; 1.0020x over previous
import contextlib
import numpy as np
import concourse.bass as bass
import concourse.mybir as mybir
from concourse.bass_utils import run_bass_kernel_spmd

F32 = mybir.dt.float32
F16 = mybir.dt.float16
BF16 = mybir.dt.bfloat16
FP8 = mybir.dt.float8e4
AF = mybir.ActivationFunctionType
ALU = mybir.AluOpType
AX = mybir.AxisListType

D = 1024
H = 4096
K = 32
NT = 32
NTL = 33
SEQ = 4096
WROW = 160

SERIAL = False
DEBUG_ATTN_ONLY = False
DEBUG_PHASE0_ONLY = False
import os as _os
P0_CUT = int(_os.environ.get('P0_CUT', '9'))
MAXCYC = int(_os.environ.get('MAXCYC', '99'))
ACUT = int(_os.environ.get('ACUT', '9'))
AIN = int(_os.environ.get('AIN', '9'))
DBGOUT = int(_os.environ.get('DBGOUT', '5'))
FCUT = int(_os.environ.get('FCUT', '9'))
DJN = int(_os.environ.get('DJN', '32'))
MCUT = int(_os.environ.get('MCUT', '9'))
DDC = int(_os.environ.get('DDC', '2'))

(CA_SUM, CA_SQ, CA_MEAN, CA_MSQ, CA_MSQE, CA_VAR, CA_STD, CA_INV, CA_NMI,
 CB_SUM, CB_SQ, CB_MEAN, CB_MSQ, CB_MSQE, CB_VAR, CB_STD, CB_INV, CB_NMI,
 CG_SUM, CG_SQ, CG_M, CG_MSQ, CG_MSQE, CG_VAR, CG_STD, CG_INV,
 CH_A0, CH_A1, CH_SUM, CH_SQ, CH_M, CH_MSQ, CH_MSQE, CH_VAR, CH_STD, CH_INV,
 CQ1_MXR, CQ1_MX, CQ1_T, CQ1_S, CQ1_G,
 CQ2_MXR, CQ2_MX, CQ2_T, CQ2_S, CQ2_G,
 CS_NEGMX, CS_Z, CS_RZ) = range(49)
NCOLS = 50


class Plan:
    ENGINES = ("sync", "vector", "scalar", "tensor", "gpsimd")

    def __init__(self):
        self.ops = {e: [] for e in self.ENGINES}
        self.cnt = {}

    def inc(self, sem, by=1):
        self.cnt[sem] = self.cnt.get(sem, 0) + by
        return self.cnt[sem]

    def val(self, sem):
        return self.cnt.get(sem, 0)

    def add(self, engine, fn):
        self.ops[engine].append(fn)


def build_program():
    nc = bass.Bass("TRN2", target_bir_lowering=False,
                   detect_race_conditions=(_os.environ.get('RACED', '0') == '1'))

    x_in = nc.dram_tensor("x", [NTL * 128, D], F32, kind="ExternalInput")
    w1t_in = nc.dram_tensor("w1t", [D, H], F32, kind="ExternalInput")
    w2t_in = nc.dram_tensor("w2t", [H, D], F32, kind="ExternalInput")
    wat_in = nc.dram_tensor("wat", [D, K], F32, kind="ExternalInput")
    ba_in = nc.dram_tensor("battn", [128, K], F32, kind="ExternalInput")
    g1_in = nc.dram_tensor("g1", [128, D], F32, kind="ExternalInput")
    b1ln_in = nc.dram_tensor("b1ln", [128, D], F32, kind="ExternalInput")
    g2_in = nc.dram_tensor("g2", [128, H], F32, kind="ExternalInput")
    b2ln_in = nc.dram_tensor("b2ln", [128, H], F32, kind="ExternalInput")
    b1_in = nc.dram_tensor("b1", [128, H], F32, kind="ExternalInput")
    b2_in = nc.dram_tensor("b2", [128, D], F32, kind="ExternalInput")
    out_dr = nc.dram_tensor("out", [SEQ, D], F32, kind="ExternalOutput")
    whbm = nc.dram_tensor("whbm", [2 * (128 * WROW + 64)], F32, kind="Internal")

    identf_dr = nc.inline_tensor(np.eye(128, dtype=np.float32), name="identf_c")

    ctx = contextlib.ExitStack()

    def mk_sb(name, shape, dtype):
        return ctx.enter_context(nc.sbuf_tensor(name, shape, dtype))

    def mk_ps(name):
        return ctx.enter_context(nc.psum_tensor(name, [128, 512], F32))

    with ctx:
        W1T = [mk_sb(f"w1t_{g}", [128, H], FP8) for g in range(8)]
        W2T = [mk_sb(f"w2t_{j}", [128, D], FP8) for j in range(32)]
        WT = mk_sb("wt", [128, 8 * K], F32)
        G1 = mk_sb("g1t", [128, D], F32)
        B1LN = mk_sb("b1lnt", [128, D], F32)
        G2 = mk_sb("g2t", [128, H], F32)
        B2LN = mk_sb("b2lnt", [128, H], F32)
        B1 = mk_sb("b1t", [128, H], F32)
        B2 = mk_sb("b2t", [128, D], F32)
        BATB = mk_sb("batb", [128, K], F32)
        IDENT = mk_sb("ident", [128, 128], F32)
        IDENTB = mk_sb("identb", [128, 128], BF16)
        ONES128 = mk_sb("ones128", [128, 128], F32)
        ONECOL = mk_sb("onecol", [128, 1], F32)
        EPS = mk_sb("epst", [128, 1], F32)
        C127 = mk_sb("c127", [128, 1], F32)
        INV127 = mk_sb("inv127", [128, 1], F32)
        MW1 = mk_sb("mw1", [128, 1], F32)
        MW2 = mk_sb("mw2", [128, 1], F32)
        SCL1 = mk_sb("scl1", [128, 1], F32)
        SCL2 = mk_sb("scl2", [128, 1], F32)
        WPART = mk_sb("wpart", [128, 68], F32)
        ZEROT = mk_sb("zerot", [128, 128], F32)
        SPACER = mk_sb("spacer", [128, 128], F32)

        XT = [mk_sb(f"x_{r}", [128, D], F32) for r in range(2)]
        XLN = [mk_sb(f"xln_{r}", [128, D], F32) for r in range(2)]
        SCR16 = mk_sb("scr16", [128, D], F16)
        if _os.environ.get('XLNFM_REAL', '0') == '1':
            XLNFM = mk_sb("xlnfm", [128, 512], F32)[:]
        else:
            XLNFM = SCR16[:].bitcast(F32)
        LG = mk_sb("lg", [32, 128], F32)
        LOGITS = mk_sb("logits", [128, K], F32)
        WP = mk_sb("wp", [128, K], F32)
        MT_SB = mk_sb("mt_sb", [128, WROW], F32)
        MA = mk_sb("ma", [128, 128], F32)
        MB = mk_sb("mb", [32, 128], F32)
        X2 = [mk_sb(f"x2_{r}", [128, D], F32) for r in range(3)]
        ZY = mk_sb("zy", [128, D], F32)
        Q1 = mk_sb("q1", [128, D], BF16)
        Q1T = mk_sb("q1t", [128, D], BF16)
        Q2 = mk_sb("q2", [128, H], BF16)
        Q2T = mk_sb("q2t", [128, H], BF16)
        GEL = mk_sb("gel", [128, H], F32)
        OUT_T = mk_sb("out_t", [128, D], F32)
        ST = [mk_sb(f"st_{r}", [128, NCOLS], F32) for r in range(3)]
        GSUM = [mk_sb(f"gsum_{r}", [128, 8], F32) for r in range(2)]

        psA = mk_ps("psA")
        psM = mk_ps("psM")
        psB0 = mk_ps("psB0")
        psB1 = mk_ps("psB1")
        psC0 = mk_ps("psC0")
        psC1 = mk_ps("psC1")
        psD0 = mk_ps("psD0")
        psD1 = mk_ps("psD1")

        sems = {}
        for s in ("dx", "dw", "dc", "dt1", "dt2", "db", "do", "pe", "ve", "se",
                  "bar", "vh", "fn"):
            sems[s] = ctx.enter_context(nc.semaphore(s))

        P = Plan()
        pe_done, ve_done, se_done, dt_done, db_done, dx_done, do_done = \
            {}, {}, {}, {}, {}, {}, {}

        def op(eng, fn):
            P.add(eng, fn)

        vhc = [0]

        def VG(e, inst):
            e.tensor_copy(SPACER[:], ZEROT[:])

        def FN(e, inst):
            inst.then_inc(sems["fn"], 1)

        def col(r, c):
            return ST[r][:, c:c + 1]

        def barrier():
            snap = {s: P.val(s) for s in
                    ("pe", "ve", "se", "dx", "dw", "dt1", "dt2", "db", "do")}
            for eng in Plan.ENGINES:
                def b(e, snap=snap):
                    for s, v in snap.items():
                        if v > 0:
                            e.wait_ge(sems[s], v)
                op(eng, b)

        def ph0_memsets(e):
            e.memset(ONES128[:], 1.0)
            FN(e, e.memset(WPART[:, 64:68], 0.0))
            FN(e, e.memset(ONECOL[:], 1.0))
            FN(e, e.memset(EPS[:], 1e-5))
            e.memset(ZEROT[:], 0.0)
            FN(e, e.memset(C127[:], 127.0))
            VG(e, None)
            FN(e, e.reciprocal(INV127[:], C127[:]))
            e.tensor_copy(SPACER[:], ZEROT[:]).then_inc(sems["ve"], 1)
        op("vector", ph0_memsets)
        ve_ms = P.inc("ve")

        def ph0_loads(e):
            e.dma_start(out=IDENT[:], in_=identf_dr.ap()).then_inc(sems["dc"], 16)
            for g in range(8):
                e.dma_start(out=WT[:, g * K:(g + 1) * K],
                            in_=wat_in.ap()[g * 128:(g + 1) * 128, :]).then_inc(sems["dc"], 16)
            e.dma_start(out=BATB[:], in_=ba_in.ap()).then_inc(sems["dc"], 16)
            for src_d, dst_t in ((g1_in, G1), (b1ln_in, B1LN), (b2_in, B2),
                                 (g2_in, G2), (b2ln_in, B2LN), (b1_in, B1)):
                e.dma_start(out=dst_t[:], in_=src_d.ap()).then_inc(sems["dc"], 16)
            e.dma_start(out=BATB[:], in_=ba_in.ap()).then_inc(sems["dc"], 16)
        op("sync", ph0_loads)
        DC_ALL = 16 * 17
        P.inc("dc", DC_ALL)

        def ph0_identb(e):
            e.wait_ge(sems["dc"], 48)
            e.tensor_copy(IDENTB[:], IDENT[:]).then_inc(sems["ve"], 1)
        op("vector", ph0_identb)
        ve_identb = P.inc("ve")

        def ph0_whbm(e):
            e.wait_ge(sems["ve"], ve_ms)
            for r in range(2):
                e.dma_start(
                    out=bass.AP(whbm, r * (128 * WROW + 64) + 32, [[WROW, 128], [1, 128]]),
                    in_=ZEROT[:],
                ).then_inc(sems["db"], 16)
        op("gpsimd", ph0_whbm)
        P.inc("db", 32)

        op("tensor", lambda e: e.wait_ge(sems["ve"], ve_ms))

        def wpass1(nrowchunks, rowfree, src_dr, pcol0):
            npieces_per = rowfree // D
            pc = pcol0
            for cidx in range(nrowchunks):
                for si in range(npieces_per):
                    a = si * D
                    buf_off = ((cidx * npieces_per + si) % 4) * D
                    prev_se = P.val("se") - 3
                    def ld(e, cidx=cidx, a=a, buf_off=buf_off, src_dr=src_dr,
                           prev_se=prev_se):
                        if prev_se > 0:
                            e.wait_ge(sems["se"], prev_se)
                        e.dma_start(out=GEL[:, buf_off:buf_off + D],
                                    in_=src_dr.ap()[cidx * 128:(cidx + 1) * 128, a:a + D]
                                    ).then_inc(sems["dw"], 16)
                    op("gpsimd", ld)
                    dwv = P.inc("dw", 16)
                    def ab(e, dwv=dwv + 16, buf_off=buf_off, pc=pc):
                        e.wait_ge(sems["dw"], dwv)
                        e.activation(out=Q2[:, 0:D], in_=GEL[:, buf_off:buf_off + D],
                                     func=AF.Abs, bias=0.0, scale=1.0,
                                     accum_out=WPART[:, pc:pc + 1]).then_inc(sems["se"], 1)
                    op("scalar", ab)
                    P.inc("se")
                    pc += 1

        if P0_CUT >= 3:
            wpass1(8, H, w1t_in, 0)
            wpass1(32, D, w2t_in, 32)

        sev_abs = P.val("se")

        def wcomb_v(e):
            e.wait_ge(sems["se"], sev_abs)
            FN(e, e.reduce_sum(WPART[:, 64:65], WPART[:, 0:32], axis=AX.X))
            e.reduce_sum(WPART[:, 65:66], WPART[:, 32:64], axis=AX.X).then_inc(sems["ve"], 1)
        if P0_CUT >= 4:
            op("vector", wcomb_v)
        vev_c = P.inc("ve") if P0_CUT >= 4 else 0

        def wcomb_t(e):
            e.wait_ge(sems["ve"], vev_c)
            e.matmul(psM[0:1, 0:1], WPART[:, 64:65], ONECOL[:, 0:1], start=True, stop=True)
            e.matmul(psM[0:1, 1:2], WPART[:, 65:66], ONECOL[:, 0:1], start=True, stop=True).then_inc(sems["pe"], 1)
        if P0_CUT >= 4:
            op("tensor", wcomb_t)
        pev_c = P.inc("pe") if P0_CUT >= 4 else 0

        def wmean_v(e):
            e.wait_ge(sems["pe"], pev_c)
            e.tensor_scalar(out=WPART[0:1, 66:68], in0=psM[0:1, 0:2],
                            scalar1=float(1.0 / (D * H)), scalar2=1e-5,
                            op0=ALU.mult, op1=ALU.max).then_inc(sems["ve"], 1)
        if P0_CUT >= 4:
            op("vector", wmean_v)
        vev_m = P.inc("ve") if P0_CUT >= 4 else 0

        def wbc_t(e):
            e.wait_ge(sems["ve"], vev_m)
            e.matmul(psM[:, 2:3], ONES128[:], WPART[:, 66:67], start=True, stop=True)
            e.matmul(psM[:, 3:4], ONES128[:], WPART[:, 67:68], start=True, stop=True).then_inc(sems["pe"], 1)
        if P0_CUT >= 4:
            op("tensor", wbc_t)
        pev_b = P.inc("pe") if P0_CUT >= 4 else 0

        def wbc_v(e):
            e.wait_ge(sems["pe"], pev_b)
            FN(e, e.tensor_copy(MW1[:], psM[:, 2:3]))
            FN(e, e.tensor_copy(MW2[:], psM[:, 3:4]))
            VG(e, None)
            FN(e, e.reciprocal(SCL1[:], MW1[:]))
            e.reciprocal(SCL2[:], MW2[:]).then_inc(sems["ve"], 1)
        if P0_CUT >= 4:
            op("vector", wbc_v)
        vev_scl = P.inc("ve") if P0_CUT >= 4 else 0

        def wpass2(nrowchunks, rowfree, src_dr, dst_fn):
            npieces_per = rowfree // D
            first_overall = True
            for cidx in range(nrowchunks):
                for si in range(npieces_per):
                    a = si * D
                    buf_off = ((cidx * npieces_per + si) % 4) * D
                    prev_ve = P.val("ve") - 3
                    def ld(e, cidx=cidx, a=a, buf_off=buf_off, src_dr=src_dr,
                           prev_ve=prev_ve):
                        if prev_ve > 0:
                            e.wait_ge(sems["ve"], prev_ve)
                        e.dma_start(out=GEL[:, buf_off:buf_off + D],
                                    in_=src_dr.ap()[cidx * 128:(cidx + 1) * 128, a:a + D]
                                    ).then_inc(sems["dw"], 16)
                    op("gpsimd", ld)
                    dwv = P.inc("dw", 16)
                    dst_t, dst_off = dst_fn(cidx, a)
                    scl = SCL1 if src_dr is w1t_in else SCL2
                    fo = first_overall
                    first_overall = False
                    def qz(e, dwv=dwv + 16, buf_off=buf_off, dst_t=dst_t, dst_off=dst_off,
                           scl=scl, fo=fo):
                        e.wait_ge(sems["dw"], dwv)
                        if fo:
                            e.wait_ge(sems["ve"], vev_scl)
                            e.wait_ge(sems["se"], sev_abs)
                        e.tensor_scalar(out=ZY[:], in0=GEL[:, buf_off:buf_off + D],
                                        scalar1=scl[:, 0:1], scalar2=12582912.0,
                                        op0=ALU.mult, op1=ALU.add)
                        e.tensor_scalar(out=ZY[:], in0=ZY[:],
                                        scalar1=12582912.0, scalar2=None, op0=ALU.subtract)
                        e.tensor_scalar(out=dst_t[:, dst_off:dst_off + D],
                                        in0=ZY[:], scalar1=1.0, scalar2=-1.0,
                                        op0=ALU.min, op1=ALU.max).then_inc(sems["ve"], 1)
                    op("vector", qz)
                    P.inc("ve")

        if P0_CUT >= 5:
            wpass2(8, H, w1t_in, lambda cidx, a: (W1T[cidx], a))
            wpass2(32, D, w2t_in, lambda cidx, a: (W2T[cidx], 0))
            nlast = (8 * 4 + 32)
            ve_slot_free = P.val("ve") - 3
            def ld_tail(e, slot=(nlast % 4) * D, wv=ve_slot_free):
                e.wait_ge(sems["ve"], wv)
                e.dma_start(out=GEL[:, slot:slot + D],
                            in_=w1t_in.ap()[0:128, 0:D]).then_inc(sems["dw"], 16)
            op("gpsimd", ld_tail)
            P.inc("dw", 16)

        barrier()
        ve_wq_done = 0

        if DBGOUT == 14:
            wv_all = P.val("ve")
            def wdump(e, wv_all=wv_all):
                e.wait_ge(sems["ve"], wv_all)
                for g in range(8):
                    e.dma_start(out=out_dr.ap()[g * 128:(g + 1) * 128, :],
                                in_=W1T[g][:].bitcast(F32)).then_inc(sems["do"], 16)
                for j in range(24):
                    e.dma_start(out=out_dr.ap()[1024 + j * 128:1024 + (j + 1) * 128, 0:256],
                                in_=W2T[j][:].bitcast(F32)).then_inc(sems["do"], 16)
            op("sync", wdump)
            P.inc("do", 16 * 32)

        if DEBUG_PHASE0_ONLY:
            def dbg(e):
                e.tensor_copy(ZY[:], W1T[0][:, 0:D])
                e.tensor_copy(X2[0][:], W2T[0][:])
                e.tensor_copy(OUT_T[:, 0:1], MW1[:])
                e.tensor_copy(OUT_T[:, 1:2], MW2[:])
                e.tensor_copy(OUT_T[:, 2:3], SCL1[:])
                e.tensor_copy(OUT_T[:, 3:4], SCL2[:])
                e.tensor_copy(OUT_T[:, 4:132], IDENT[:])
                e.tensor_copy(OUT_T[:, 132:164], WT[:, 0:32])
                e.tensor_copy(OUT_T[:, 164:232], WPART[:]).then_inc(sems["ve"], 1)
            op("vector", dbg)
            vd = P.inc("ve")
            def dbg_st(e, vd=vd):
                e.wait_ge(sems["ve"], vd)
                e.dma_start(out=out_dr.ap()[0:128, :], in_=ZY[:]).then_inc(sems["do"], 16)
                e.dma_start(out=out_dr.ap()[128:256, :], in_=X2[0][:]).then_inc(sems["do"], 16)
                e.dma_start(out=out_dr.ap()[256:384, :], in_=G2[:, 0:D]).then_inc(sems["do"], 16)
                e.dma_start(out=out_dr.ap()[384:512, :], in_=B1[:, 0:D]).then_inc(sems["do"], 16)
                e.dma_start(out=out_dr.ap()[512:640, :], in_=OUT_T[:]).then_inc(sems["do"], 16)
                e.wait_ge(sems["do"], 80)
            op("sync", dbg_st)
            P.inc("do", 80)

        def emit_load_x(i):
            wv = ve_done.get(("attnev", i - 2), 0)
            def f(e, i=i, wv=wv):
                if wv:
                    e.wait_ge(sems["ve"], wv)
                e.dma_start(out=XT[i % 2][:],
                            in_=x_in.ap()[i * 128:(i + 1) * 128, :]).then_inc(sems["dx"], 16)
            op("sync", f)
            dx_done[i] = P.inc("dx", 16)

        def emit_stats_alpha(i):
            r = i % 3
            dxv = dx_done[i]
            w_r1 = ve_done.get(("round1", i - 2), 0)
            w_r2 = ve_done.get(("round2", i - 3), 0)
            def fs(e, i=i, r=r, dxv=dxv, w_r1=w_r1, w_r2=w_r2):
                e.wait_ge(sems["dx"], dxv)
                if w_r1:
                    e.wait_ge(sems["ve"], w_r1)
                if w_r2:
                    e.wait_ge(sems["ve"], w_r2)
                e.activation(out=SCR16[:], in_=XT[i % 2][:], func=AF.Square,
                             bias=0.0, scale=1.0,
                             accum_out=col(r, CA_SQ)).then_inc(sems["se"], 1)
            op("scalar", fs)
            se_sq = P.inc("se")

            def fv(e, i=i, r=r, dxv=dxv, se_sq=se_sq):
                e.wait_ge(sems["dx"], dxv)
                FN(e, e.reduce_sum(col(r, CA_SUM), XT[i % 2][:], axis=AX.X))
                VG(e, None)
                e.wait_ge(sems["se"], se_sq)
                FN(e, e.tensor_scalar(out=col(r, CA_MEAN), in0=col(r, CA_SUM),
                                      scalar1=1.0 / D, scalar2=None, op0=ALU.mult))
                VG(e, None)
                FN(e, e.tensor_tensor(col(r, CA_MSQ), col(r, CA_MEAN),
                                      col(r, CA_MEAN), ALU.mult))
                VG(e, None)
                FN(e, e.tensor_scalar(out=col(r, CA_MSQE), in0=col(r, CA_MSQ),
                                      scalar1=1.0, scalar2=1e-5,
                                      op0=ALU.mult, op1=ALU.subtract))
                VG(e, None)
                e.tensor_scalar(out=col(r, CA_VAR), in0=col(r, CA_SQ),
                                scalar1=1.0 / D, scalar2=col(r, CA_MSQE),
                                op0=ALU.mult, op1=ALU.subtract).then_inc(sems["ve"], 1)
            op("vector", fv)
            ve_var = P.inc("ve")

            def fs2(e, r=r, ve_var=ve_var):
                e.wait_ge(sems["ve"], ve_var)
                e.activation(out=col(r, CA_STD), in_=col(r, CA_VAR), func=AF.Sqrt,
                             bias=0.0, scale=1.0).then_inc(sems["se"], 1)
            op("scalar", fs2)
            se_sd = P.inc("se")

            def fv2(e, r=r, se_sd=se_sd):
                e.wait_ge(sems["se"], se_sd)
                FN(e, e.reciprocal(col(r, CA_INV), col(r, CA_STD)))
                VG(e, None)
                e.tensor_scalar(out=col(r, CA_NMI), in0=col(r, CA_MEAN), scalar1=-1.0,
                                scalar2=col(r, CA_INV), op0=ALU.mult, op1=ALU.mult).then_inc(sems["ve"], 1)
            op("vector", fv2)
            ve_done[("statsA", i)] = P.inc("ve")

        def emit_xln(i):
            r = i % 3
            wv = ve_done[("statsA", i)]
            wpe = pe_done.get(("band", i - 2), 0)
            def f(e, i=i, r=r, wv=wv, wpe=wpe):
                e.wait_ge(sems["ve"], wv)
                if wpe:
                    e.wait_ge(sems["pe"], wpe)
                e.activation(out=XLN[i % 2][:], in_=XT[i % 2][:], func=AF.Identity,
                             bias=col(r, CA_NMI), scale=col(r, CA_INV)
                             ).then_inc(sems["se"], 1)
            op("scalar", f)
            se_done[("xln", i)] = P.inc("se")

        def emit_A(i):
            sxln = se_done[("xln", i)]
            wv_prev = ve_done.get(("Ae2", i - 1), 0)

            def t1(e, i=i, sxln=sxln, wv_prev=wv_prev):
                e.wait_ge(sems["se"], sxln)
                if i <= 4 and vev_scl:
                    e.wait_ge(sems["ve"], vev_scl)
                if i <= 2:
                    e.wait_ge(sems["dc"], DC_ALL)
                if wv_prev:
                    e.wait_ge(sems["ve"], wv_prev)
                for g in range(4):
                    last = e.transpose(psA[:, g * 128:(g + 1) * 128],
                                       XLN[i % 2][:, g * 128:(g + 1) * 128], IDENT[:])
                last.then_inc(sems["pe"], 1)
            op("tensor", t1)
            pe_w1 = P.inc("pe")

            def v1(e, pe_w1=pe_w1):
                e.wait_ge(sems["pe"], pe_w1)
                e.tensor_copy(XLNFM[:, 0:512], psA[:, 0:512]).then_inc(sems["ve"], 1)
            op("vector", v1)
            ve_e1 = P.inc("ve")
            if AIN < 2:
                ve_done[("softmax", i)] = 0
                ve_done[("Ae2", i)] = ve_e1
                ve_done[("lgmx", i)] = 0
                return

            def t2(e, i=i, ve_e1=ve_e1):
                e.wait_ge(sems["ve"], ve_e1)
                for g in range(4):
                    e.transpose(psA[:, g * 128:(g + 1) * 128],
                                XLN[i % 2][:, 512 + g * 128:512 + (g + 1) * 128], IDENT[:])
                for g in range(4):
                    last = e.matmul(psM[0:32, 256:384], WT[:, g * K:(g + 1) * K],
                                    XLNFM[:, g * 128:(g + 1) * 128],
                                    start=(g == 0), stop=False)
                last.then_inc(sems["pe"], 1)
            op("tensor", t2)
            pe_w2 = P.inc("pe")

            def v2(e, pe_w2=pe_w2):
                e.wait_ge(sems["pe"], pe_w2)
                e.tensor_copy(XLNFM[:, 0:512], psA[:, 0:512]).then_inc(sems["ve"], 1)
            op("vector", v2)
            ve_e2 = P.inc("ve")
            ve_done[("Ae2", i)] = ve_e2
            if AIN < 3:
                ve_done[("softmax", i)] = 0
                ve_done[("lgmx", i)] = 0
                return

            def t3(e, ve_e2=ve_e2):
                e.wait_ge(sems["ve"], ve_e2)
                for g in range(4):
                    last = e.matmul(psM[0:32, 256:384], WT[:, (4 + g) * K:(5 + g) * K],
                                    XLNFM[:, g * 128:(g + 1) * 128],
                                    start=False, stop=(g == 3))
                last.then_inc(sems["pe"], 1)
            op("tensor", t3)
            pe_lmm = P.inc("pe")

            def s1(e, pe_lmm=pe_lmm):
                e.wait_ge(sems["pe"], pe_lmm)
                e.activation(out=LG[:], in_=psM[0:32, 256:384], func=AF.Identity,
                             bias=0.0, scale=1.0).then_inc(sems["se"], 1)
            op("scalar", s1)
            se_lg = P.inc("se")

            def t4(e, se_lg=se_lg):
                e.wait_ge(sems["se"], se_lg)
                e.transpose(psM[:, 384:416], LG[:], IDENT[0:32, 0:32]).then_inc(sems["pe"], 1)
            op("tensor", t4)
            pe_lgt = P.inc("pe")
            if AIN < 4:
                ve_done[("softmax", i)] = 0
                ve_done[("lgmx", i)] = 0
                return

            r = i % 3
            def v3(e, r=r, pe_lgt=pe_lgt, i=i):
                if i <= 2:
                    e.wait_ge(sems["dc"], DC_ALL)
                e.wait_ge(sems["pe"], pe_lgt)
                e.tensor_tensor(LOGITS[:], psM[:, 384:416], BATB[:], ALU.add)
                e.reduce_max(col(r, CS_NEGMX), LOGITS[:], axis=AX.X, negate=True).then_inc(sems["ve"], 1)
            op("vector", v3)
            ve_mx = P.inc("ve")
            ve_done[("lgmx", i)] = ve_mx

            wdb = db_done.get(("wst", i - 1), 0)
            def s2(e, r=r, ve_mx=ve_mx, wdb=wdb):
                e.wait_ge(sems["ve"], ve_mx)
                if wdb:
                    e.wait_ge(sems["db"], wdb)
                e.activation(out=WP[:], in_=LOGITS[:], func=AF.Exp,
                             bias=col(r, CS_NEGMX), scale=1.0,
                             accum_out=col(r, CS_Z)).then_inc(sems["se"], 1)
            op("scalar", s2)
            se_exp = P.inc("se")

            def v4(e, r=r, se_exp=se_exp):
                e.wait_ge(sems["se"], se_exp)
                FN(e, e.reciprocal(col(r, CS_RZ), col(r, CS_Z)))
                VG(e, None)
                e.tensor_scalar(out=WP[:], in0=WP[:], scalar1=col(r, CS_RZ),
                                scalar2=0.0, op0=ALU.mult, op1=ALU.add).then_inc(sems["ve"], 1)
            op("vector", v4)
            ve_done[("softmax", i)] = P.inc("ve")

        def emit_band_dma(i):
            wv = ve_done[("softmax", i)]
            base = (i % 2) * (128 * WROW + 64)
            wprev = db_done.get(("mtl", i - 2), 0)
            wdt1 = P.val("dt1")
            wdt2 = P.val("dt2")
            def f(e, wv=wv, base=base, wprev=wprev, wdt1=wdt1, wdt2=wdt2):
                e.wait_ge(sems["ve"], wv)
                if wprev:
                    e.wait_ge(sems["db"], wprev)
                if wdt1:
                    e.wait_ge(sems["dt1"], wdt1)
                if wdt2:
                    e.wait_ge(sems["dt2"], wdt2)
                e.dma_start(out=bass.AP(whbm, base, [[WROW, 128], [1, K]]),
                            in_=WP[:]).then_inc(sems["db"], 16)
            op("gpsimd", f)
            st_v = P.inc("db", 16)
            db_done[("wst", i)] = st_v

            wmt = pe_done.get(("mt", i - 1), 0)
            def f2(e, st_v=st_v, base=base, wmt=wmt):
                e.wait_ge(sems["db"], st_v)
                if wmt:
                    e.wait_ge(sems["pe"], wmt)
                e.dma_start(out=MT_SB[:],
                            in_=bass.AP(whbm, base, [[WROW - 1, 128], [1, WROW]])
                            ).then_inc(sems["db"], 16)
                e.dma_start(out=MT_SB[:, 0:4],
                            in_=bass.AP(whbm, base, [[WROW - 1, 128], [1, 4]])
                            ).then_inc(sems["db"], 16)
            op("gpsimd", f2)
            P.inc("db", 16)
            db_done[("mtl", i)] = P.inc("db", 16)

        def emit_MT(i):
            wdb = db_done[("mtl", i)]
            wv = ve_done.get(("lgmx", i), 0)
            def f(e, wdb=wdb, wv=wv):
                e.wait_ge(sems["db"], wdb)
                if wv:
                    e.wait_ge(sems["ve"], wv)
                e.transpose(psM[:, 0:128], MT_SB[:, 0:128], IDENT[:])
                e.transpose(psM[0:32, 128:256], MT_SB[:, 128:160], IDENT[:]).then_inc(sems["pe"], 1)
            op("tensor", f)
            pe_done[("mt", i)] = P.inc("pe")

        def emit_Mevac(i):
            wpe = pe_done[("mt", i)]
            def f(e, wpe=wpe):
                e.wait_ge(sems["pe"], wpe)
                e.tensor_copy(MA[:], psM[:, 0:128])
                e.tensor_copy(MB[0:32, :], psM[0:32, 128:256]).then_inc(sems["ve"], 1)
            op("vector", f)
            ve_done[("mev", i)] = P.inc("ve")

        def emit_B(i):
            wv = ve_done[("mev", i)]
            wx = se_done[("xln", i + 1)]
            wev = ve_done.get(("attnev", i - 1), 0)
            def f(e, i=i, wv=wv, wx=wx, wev=wev):
                e.wait_ge(sems["ve"], wv)
                e.wait_ge(sems["se"], wx)
                if wev:
                    e.wait_ge(sems["ve"], wev)
                for dm, ps in ((0, psB0), (1, psB1)):
                    e.matmul(ps[:], MA[:], XLN[i % 2][:, dm * 512:(dm + 1) * 512],
                             start=True, stop=False)
                    last = e.matmul(ps[:], MB[0:32, :],
                                    XLN[(i + 1) % 2][0:32, dm * 512:(dm + 1) * 512],
                                    start=False, stop=True)
                last.then_inc(sems["pe"], 1)
            op("tensor", f)
            pe_done[("band", i)] = P.inc("pe")

        def emit_attn_evac(i):
            wpe = pe_done[("band", i)]
            wv2 = ve_done.get(("evac2", i - 3), 0)
            def f(e, i=i, wpe=wpe, wv2=wv2):
                e.wait_ge(sems["pe"], wpe)
                if wv2:
                    e.wait_ge(sems["ve"], wv2)
                for dm, ps in ((0, psB0), (1, psB1)):
                    last = e.scalar_tensor_tensor(out=X2[i % 3][:, dm * 512:(dm + 1) * 512],
                                                  in0=ps[:], scalar=1.0,
                                                  in1=XT[i % 2][:, dm * 512:(dm + 1) * 512],
                                                  op0=ALU.mult, op1=ALU.add)
                last.then_inc(sems["ve"], 1)
            op("vector", f)
            ve_done[("attnev", i)] = P.inc("ve")

        def emit_chain_head(i):
            r = i % 3
            wev = ve_done[("attnev", i)]
            x2t = X2[i % 3]

            def s_sqB(e, x2t=x2t, r=r, wev=wev):
                e.wait_ge(sems["ve"], wev)
                e.activation(out=SCR16[:], in_=x2t[:], func=AF.Square,
                             bias=0.0, scale=1.0,
                             accum_out=col(r, CB_SQ)).then_inc(sems["se"], 1)
            op("scalar", s_sqB)
            se_sqB = P.inc("se")

            def v_stB(e, x2t=x2t, r=r, wev=wev, se_sqB=se_sqB):
                e.wait_ge(sems["ve"], wev)
                FN(e, e.reduce_sum(col(r, CB_SUM), x2t[:], axis=AX.X))
                VG(e, None)
                e.wait_ge(sems["se"], se_sqB)
                FN(e, e.tensor_scalar(out=col(r, CB_MEAN), in0=col(r, CB_SUM),
                                      scalar1=1.0 / D, scalar2=None, op0=ALU.mult))
                VG(e, None)
                FN(e, e.tensor_tensor(col(r, CB_MSQ), col(r, CB_MEAN),
                                      col(r, CB_MEAN), ALU.mult))
                VG(e, None)
                FN(e, e.tensor_scalar(out=col(r, CB_MSQE), in0=col(r, CB_MSQ),
                                      scalar1=1.0, scalar2=1e-5,
                                      op0=ALU.mult, op1=ALU.subtract))
                VG(e, None)
                e.tensor_scalar(out=col(r, CB_VAR), in0=col(r, CB_SQ),
                                scalar1=1.0 / D, scalar2=col(r, CB_MSQE),
                                op0=ALU.mult, op1=ALU.subtract).then_inc(sems["ve"], 1)
            op("vector", v_stB)
            ve_vB = P.inc("ve")

            def s_sdB(e, r=r, ve_vB=ve_vB):
                e.wait_ge(sems["ve"], ve_vB)
                e.activation(out=col(r, CB_STD), in_=col(r, CB_VAR), func=AF.Sqrt,
                             bias=0.0, scale=1.0).then_inc(sems["se"], 1)
            op("scalar", s_sdB)
            se_sdB = P.inc("se")

            def v_invB(e, r=r, se_sdB=se_sdB):
                e.wait_ge(sems["se"], se_sdB)
                FN(e, e.reciprocal(col(r, CB_INV), col(r, CB_STD)))
                VG(e, None)
                e.tensor_scalar(out=col(r, CB_NMI), in0=col(r, CB_MEAN), scalar1=-1.0,
                                scalar2=col(r, CB_INV), op0=ALU.mult, op1=ALU.mult).then_inc(sems["ve"], 1)
            op("vector", v_invB)
            ve_invB = P.inc("ve")

            def s_z(e, x2t=x2t, r=r, ve_invB=ve_invB, i=i):
                if i <= 2 and ve_wq_done:
                    e.wait_ge(sems["ve"], ve_wq_done)
                if i <= 2:
                    e.wait_ge(sems["dc"], DC_ALL)
                e.wait_ge(sems["ve"], ve_invB)
                e.activation(out=ZY[:], in_=x2t[:], func=AF.Identity,
                             bias=col(r, CB_NMI), scale=col(r, CB_INV)
                             ).then_inc(sems["se"], 1)
            op("scalar", s_z)
            se_z = P.inc("se")

            def v_b2(e, x2t=x2t, se_z=se_z):
                e.wait_ge(sems["se"], se_z)
                e.tensor_tensor(x2t[:], x2t[:], B2[:], ALU.add).then_inc(sems["ve"], 1)
            op("vector", v_b2)
            P.inc("ve")

            def s_sqG(e, r=r, se_z=se_z):
                e.activation(out=SCR16[:], in_=ZY[:], func=AF.Square,
                             bias=0.0, scale=1.0,
                             accum_out=col(r, CG_SQ)).then_inc(sems["se"], 1)
            op("scalar", s_sqG)
            se_sqG = P.inc("se")

            def v_stG(e, r=r, se_sqG=se_sqG):
                e.wait_ge(sems["se"], se_sqG)
                FN(e, e.reduce_sum(col(r, CG_SUM), ZY[:], axis=AX.X))
                VG(e, None)
                FN(e, e.tensor_scalar(out=col(r, CG_M), in0=col(r, CG_SUM),
                                      scalar1=1.0 / D, scalar2=None, op0=ALU.mult))
                VG(e, None)
                FN(e, e.tensor_tensor(col(r, CG_MSQ), col(r, CG_M),
                                      col(r, CG_M), ALU.mult))
                VG(e, None)
                FN(e, e.tensor_scalar(out=col(r, CG_MSQE), in0=col(r, CG_MSQ),
                                      scalar1=1.0, scalar2=1e-5,
                                      op0=ALU.mult, op1=ALU.subtract))
                VG(e, None)
                e.tensor_scalar(out=col(r, CG_VAR), in0=col(r, CG_SQ),
                                scalar1=1.0 / D, scalar2=col(r, CG_MSQE),
                                op0=ALU.mult, op1=ALU.subtract).then_inc(sems["ve"], 1)
            op("vector", v_stG)
            ve_vG = P.inc("ve")

            def s_sdG(e, r=r, ve_vG=ve_vG):
                e.wait_ge(sems["ve"], ve_vG)
                e.activation(out=col(r, CG_STD), in_=col(r, CG_VAR), func=AF.Sqrt,
                             bias=0.0, scale=1.0).then_inc(sems["se"], 1)
            op("scalar", s_sdG)
            se_sdG = P.inc("se")

            def v_ln1a(e, r=r, se_sdG=se_sdG):
                e.wait_ge(sems["se"], se_sdG)
                FN(e, e.reciprocal(col(r, CG_INV), col(r, CG_STD)))
                e.scalar_tensor_tensor(out=ZY[:], in0=ZY[:], scalar=col(r, CG_M),
                                       in1=G1[:], op0=ALU.subtract, op1=ALU.mult).then_inc(sems["ve"], 1)
            op("vector", v_ln1a)
            ve_ln1a = P.inc("ve")

            def s_ln1b(e, r=r, ve_ln1a=ve_ln1a):
                e.wait_ge(sems["ve"], ve_ln1a)
                e.activation(out=ZY[:], in_=ZY[:], func=AF.Identity,
                             bias=0.0, scale=col(r, CG_INV)).then_inc(sems["se"], 1)
            op("scalar", s_ln1b)
            se_ln1b = P.inc("se")

            wdt1 = dt_done.get(("q1t", i - 1), 0)
            def v_ln1c(e, r=r, se_ln1b=se_ln1b, wdt1=wdt1):
                e.wait_ge(sems["se"], se_ln1b)
                e.tensor_tensor(ZY[:], ZY[:], B1LN[:], ALU.add)
                FN(e, e.reduce_max(col(r, CQ1_MXR), ZY[:], axis=AX.X,
                                   apply_absolute_value=True))
                VG(e, None)
                FN(e, e.tensor_scalar(out=col(r, CQ1_MX), in0=col(r, CQ1_MXR),
                                      scalar1=1e-5, scalar2=None, op0=ALU.max))
                VG(e, None)
                FN(e, e.reciprocal(col(r, CQ1_T), col(r, CQ1_MX)))
                VG(e, None)
                FN(e, e.tensor_scalar(out=col(r, CQ1_S), in0=col(r, CQ1_T), scalar1=127.0,
                                      scalar2=0.0, op0=ALU.mult, op1=ALU.add))
                FN(e, e.tensor_scalar(out=col(r, CQ1_G), in0=col(r, CQ1_MX),
                                      scalar1=MW1[:, 0:1], scalar2=INV127[:, 0:1],
                                      op0=ALU.mult, op1=ALU.mult))
                if wdt1:
                    e.wait_ge(sems["dt1"], wdt1)
                e.tensor_scalar(out=ZY[:], in0=ZY[:], scalar1=col(r, CQ1_S),
                                scalar2=12582912.0, op0=ALU.mult, op1=ALU.add)
                e.tensor_scalar(out=Q1[:], in0=ZY[:], scalar1=12582912.0,
                                scalar2=None, op0=ALU.subtract).then_inc(sems["ve"], 1)
            op("vector", v_ln1c)
            ve_done[("round1", i)] = P.inc("ve")

        def emit_q1T(i):
            wv = ve_done[("round1", i)]
            wpe = pe_done.get(("c7", i - 1), 0)
            wdb = P.val("db")
            def f(e, i=i, wv=wv, wpe=wpe, wdb=wdb):
                e.wait_ge(sems["ve"], wv)
                if wpe:
                    e.wait_ge(sems["pe"], wpe)
                if wdb:
                    e.wait_ge(sems["db"], wdb)
                for g in range(8):
                    e.dma_start_transpose(Q1T[:, g * 128:(g + 1) * 128],
                                          Q1[:, g * 128:(g + 1) * 128]
                                          ).then_inc(sems["dt1"], 16)
                e.dma_start_transpose(Q1T[:, 0:128],
                                      Q1[:, 0:128]).then_inc(sems["dt1"], 16)
            op("sync", f)
            dt_done[("q1t", i)] = P.inc("dt1", 16 * 9)
            if DBGOUT == 10:
                wv2 = ve_done[("round1", i)]
                def fd(e, i=i, wv2=wv2):
                    e.wait_ge(sems["ve"], wv2)
                    e.dma_start(out=out_dr.ap()[i * 128:(i + 1) * 128, :],
                                in_=ZY[:]).then_inc(sems["do"], 16)
                op("sync", fd)
                P.inc("do", 16)

        def emit_C(i):
            wdt = dt_done[("q1t", i)]
            w_r2 = ve_done.get(("round2", i - 1), 0)
            for hc in range(8):
                ps = psC0 if hc % 2 == 0 else psC1
                wev = ve_done.get(("evac1", i, hc - 2), 0)
                if hc == 0:
                    wev = ve_done.get(("evac1", i - 1, 6), 0)
                elif hc == 1:
                    wev = ve_done.get(("evac1", i - 1, 7), 0)
                def f(e, i=i, hc=hc, ps=ps, wdt=wdt, wev=wev):
                    if hc == 0:
                        if i <= 2 and ve_wq_done:
                            e.wait_ge(sems["ve"], ve_wq_done)
                        e.wait_ge(sems["dt1"], wdt)
                    if wev:
                        e.wait_ge(sems["ve"], wev)
                    for g in range(8):
                        last = e.matmul(ps[:], Q1T[:, g * 128:(g + 1) * 128],
                                        W1T[g][:, hc * 512:(hc + 1) * 512],
                                        start=(g == 0), stop=(g == 7))
                    last.then_inc(sems["pe"], 1)
                op("tensor", f)
                pe_done[("c", i, hc)] = P.inc("pe")
                if hc == 7:
                    pe_done[("c7", i)] = pe_done[("c", i, hc)]

                r = i % 3
                wpe = pe_done[("c", i, hc)]
                def fv(e, i=i, hc=hc, ps=ps, r=r, wpe=wpe, w_r2=w_r2):
                    e.wait_ge(sems["pe"], wpe)
                    if hc == 0 and w_r2:
                        e.wait_ge(sems["ve"], w_r2)
                    e.scalar_tensor_tensor(out=GEL[:, hc * 512:(hc + 1) * 512],
                                           in0=ps[:], scalar=col(r, CQ1_G),
                                           in1=B1[:, hc * 512:(hc + 1) * 512],
                                           op0=ALU.mult, op1=ALU.add).then_inc(sems["ve"], 1)
                op("vector", fv)
                ve_done[("evac1", i, hc)] = P.inc("ve")

                wv1 = ve_done[("evac1", i, hc)]
                def fs(e, i=i, hc=hc, wv1=wv1):
                    e.wait_ge(sems["ve"], wv1)
                    e.activation(out=GEL[:, hc * 512:(hc + 1) * 512],
                                 in_=GEL[:, hc * 512:(hc + 1) * 512], func=AF.Gelu,
                                 bias=0.0, scale=1.0,
                                 accum_out=GSUM[i % 2][:, hc:hc + 1]).then_inc(sems["se"], 1)
                op("scalar", fs)
                se_done[("gelu", i, hc)] = P.inc("se")
            if DBGOUT == 13:
                ws2 = se_done[("gelu", i, 7)]
                def fd(e, i=i, ws2=ws2):
                    e.wait_ge(sems["se"], ws2)
                    e.dma_start(out=out_dr.ap()[i * 128:(i + 1) * 128, :],
                                in_=GEL[:, 0:D]).then_inc(sems["do"], 16)
                op("sync", fd)
                P.inc("do", 16)

        def emit_chain_mid(i):
            r = i % 3
            wse = se_done[("gelu", i, 7)]
            wdt2 = pe_done.get(("q2tlast", i - 1), 0)

            def s_sqH(e, i=i, r=r, wse=wse, wdt2=wdt2):
                e.wait_ge(sems["se"], wse)
                if wdt2:
                    e.wait_ge(sems["pe"], wdt2)
                FN(e, e.activation(out=Q2[:, 0:2048], in_=GEL[:, 0:2048], func=AF.Square,
                                   bias=0.0, scale=1.0, accum_out=col(r, CH_A0)))
                e.activation(out=Q2[:, 2048:4096], in_=GEL[:, 2048:4096], func=AF.Square,
                             bias=0.0, scale=1.0, accum_out=col(r, CH_A1)
                             ).then_inc(sems["se"], 1)
            op("scalar", s_sqH)
            se_sqH = P.inc("se")
            if MCUT < 2:
                ve_done[("round2", i)] = 0
                return

            def v_stH(e, i=i, r=r, se_sqH=se_sqH):
                e.wait_ge(sems["se"], se_sqH)
                FN(e, e.reduce_sum(col(r, CH_SUM), GSUM[i % 2][:], axis=AX.X))
                FN(e, e.tensor_tensor(col(r, CH_SQ), col(r, CH_A0), col(r, CH_A1), ALU.add))
                FN(e, e.tensor_scalar(out=col(r, CH_M), in0=col(r, CH_SUM),
                                      scalar1=1.0 / H, scalar2=None, op0=ALU.mult))
                VG(e, None)
                FN(e, e.tensor_tensor(col(r, CH_MSQ), col(r, CH_M), col(r, CH_M), ALU.mult))
                VG(e, None)
                FN(e, e.tensor_scalar(out=col(r, CH_MSQE), in0=col(r, CH_MSQ),
                                      scalar1=1.0, scalar2=1e-5,
                                      op0=ALU.mult, op1=ALU.subtract))
                VG(e, None)
                e.tensor_scalar(out=col(r, CH_VAR), in0=col(r, CH_SQ),
                                scalar1=1.0 / H, scalar2=col(r, CH_MSQE),
                                op0=ALU.mult, op1=ALU.subtract).then_inc(sems["ve"], 1)
            op("vector", v_stH)
            ve_vH = P.inc("ve")

            def s_sdH(e, r=r, ve_vH=ve_vH):
                e.wait_ge(sems["ve"], ve_vH)
                e.activation(out=col(r, CH_STD), in_=col(r, CH_VAR), func=AF.Sqrt,
                             bias=0.0, scale=1.0).then_inc(sems["se"], 1)
            op("scalar", s_sdH)
            se_sdH = P.inc("se")
            if MCUT < 3:
                ve_done[("round2", i)] = 0
                return

            def v_ln2a(e, r=r, se_sdH=se_sdH):
                e.wait_ge(sems["se"], se_sdH)
                FN(e, e.reciprocal(col(r, CH_INV), col(r, CH_STD)))
                e.scalar_tensor_tensor(out=GEL[:], in0=GEL[:], scalar=col(r, CH_M),
                                       in1=G2[:], op0=ALU.subtract, op1=ALU.mult).then_inc(sems["ve"], 1)
            op("vector", v_ln2a)
            ve_ln2a = P.inc("ve")
            if MCUT == 4:
                ve_done[("round2", i)] = 0
                return

            def s_ln2b(e, r=r, ve_ln2a=ve_ln2a):
                e.wait_ge(sems["ve"], ve_ln2a)
                e.activation(out=GEL[:], in_=GEL[:], func=AF.Identity,
                             bias=0.0, scale=col(r, CH_INV)).then_inc(sems["se"], 1)
            op("scalar", s_ln2b)
            se_ln2b = P.inc("se")
            if MCUT == 5:
                ve_done[("round2", i)] = 0
                return

            def v_ln2c(e, r=r, se_ln2b=se_ln2b):
                e.wait_ge(sems["se"], se_ln2b)
                e.tensor_tensor(GEL[:], GEL[:], B2LN[:], ALU.add)
                FN(e, e.reduce_max(col(r, CQ2_MXR), GEL[:], axis=AX.X,
                                   apply_absolute_value=True))
                VG(e, None)
                FN(e, e.tensor_scalar(out=col(r, CQ2_MX), in0=col(r, CQ2_MXR),
                                      scalar1=1e-5, scalar2=None, op0=ALU.max))
                VG(e, None)
                FN(e, e.reciprocal(col(r, CQ2_T), col(r, CQ2_MX)))
                VG(e, None)
                FN(e, e.tensor_scalar(out=col(r, CQ2_S), in0=col(r, CQ2_T), scalar1=127.0,
                                      scalar2=0.0, op0=ALU.mult, op1=ALU.add))
                FN(e, e.tensor_scalar(out=col(r, CQ2_G), in0=col(r, CQ2_MX),
                                      scalar1=MW2[:, 0:1], scalar2=INV127[:, 0:1],
                                      op0=ALU.mult, op1=ALU.mult))

                for qq in range(4):
                    e.tensor_scalar(out=GEL[:, qq * D:(qq + 1) * D],
                                    in0=GEL[:, qq * D:(qq + 1) * D],
                                    scalar1=col(r, CQ2_S), scalar2=12582912.0,
                                    op0=ALU.mult, op1=ALU.add)
                    last = e.tensor_scalar(out=Q2[:, qq * D:(qq + 1) * D],
                                           in0=GEL[:, qq * D:(qq + 1) * D],
                                           scalar1=12582912.0, scalar2=None, op0=ALU.subtract)
                last.then_inc(sems["ve"], 1)
            op("vector", v_ln2c)
            ve_done[("round2", i)] = P.inc("ve")

        def emit_q2T(i):
            wv = ve_done[("round2", i)]
            wpe = pe_done.get(("d1", i - 1), 0)
            wev = ve_done.get(("evac2", i - 1), 0)
            for g in range(4):
                psb = (psD0 if g % 2 == 0 else psD1)[:].bitcast(BF16)
                wse_prev = se_done.get(("q2tev", i, g - 2), 0)
                def ft(e, i=i, g=g, psb=psb, wv=wv, wpe=wpe, wev=wev,
                       wse_prev=wse_prev):
                    if g == 0:
                        e.wait_ge(sems["ve"], wv)
                        if wpe:
                            e.wait_ge(sems["pe"], wpe)
                        if wev:
                            e.wait_ge(sems["ve"], wev)
                    if wse_prev:
                        e.wait_ge(sems["se"], wse_prev)
                    for k in range(8):
                        last = e.transpose(psb[:, k * 128:(k + 1) * 128],
                                           Q2[:, (g * 8 + k) * 128:(g * 8 + k + 1) * 128],
                                           IDENTB[:])
                    last.then_inc(sems["pe"], 1)
                op("tensor", ft)
                pe_done[("q2t", i, g)] = P.inc("pe")
                wpe2 = pe_done[("q2t", i, g)]
                def fe(e, i=i, g=g, psb=psb, wpe2=wpe2):
                    e.wait_ge(sems["pe"], wpe2)
                    e.activation(out=Q2T[:, g * 1024:(g + 1) * 1024], in_=psb[:],
                                 func=AF.Identity, bias=0.0,
                                 scale=1.0).then_inc(sems["se"], 1)
                op("scalar", fe)
                se_done[("q2tev", i, g)] = P.inc("se")
            pe_done[("q2tlast", i)] = pe_done[("q2t", i, 3)]
            dt_done[("q2t", i)] = se_done[("q2tev", i, 3)]
            if DBGOUT in (11, 12):
                wv2 = ve_done[("round2", i)]
                off = 0 if DBGOUT == 11 else 1024
                def fd(e, i=i, wv2=wv2, off=off):
                    e.wait_ge(sems["ve"], wv2)
                    e.dma_start(out=out_dr.ap()[i * 128:(i + 1) * 128, :],
                                in_=GEL[:, off:off + D]).then_inc(sems["do"], 16)
                op("sync", fd)
                P.inc("do", 16)

        def emit_D(i):
            wdt = dt_done[("q2t", i)]
            wev = ve_done.get(("evac2", i - 1), 0)
            def f(e, i=i, wdt=wdt, wev=wev):
                if i <= 2 and ve_wq_done:
                    e.wait_ge(sems["ve"], ve_wq_done)
                e.wait_ge(sems["se"], wdt)
                if wev:
                    e.wait_ge(sems["ve"], wev)
                for j in range(32):
                    for dc, ps in ((0, psD0), (1, psD1)):
                        last = e.matmul(ps[:], Q2T[:, j * 128:(j + 1) * 128],
                                        W2T[j][:, dc * 512:(dc + 1) * 512],
                                        start=(j == 0), stop=(j == 31))
                last.then_inc(sems["pe"], 1)
            op("tensor", f)
            pe_done[("d", i, 0)] = pe_done[("d", i, 1)] = P.inc("pe")
            pe_done[("d1", i)] = pe_done[("d", i, 1)]

        def emit_evac2(i):
            r = i % 3
            wpe = pe_done[("d", i, 1)]
            wdo = do_done.get(i - 1, 0)
            def f(e, i=i, r=r, wpe=wpe, wdo=wdo):
                e.wait_ge(sems["pe"], wpe)
                if wdo:
                    e.wait_ge(sems["do"], wdo)
                for dc, ps in ((0, psD0), (1, psD1)):
                    last = e.scalar_tensor_tensor(out=OUT_T[:, dc * 512:(dc + 1) * 512],
                                                  in0=ps[:], scalar=col(r, CQ2_G),
                                                  in1=X2[i % 3][:, dc * 512:(dc + 1) * 512],
                                                  op0=ALU.mult, op1=ALU.add)
                last.then_inc(sems["ve"], 1)
            op("vector", f)
            ve_done[("evac2", i)] = P.inc("ve")

        def emit_store(i):
            wv = ve_done[("evac2", i)]
            def f(e, i=i, wv=wv):
                e.wait_ge(sems["ve"], wv)
                e.dma_start(out=out_dr.ap()[i * 128:(i + 1) * 128, :],
                            in_=OUT_T[:]).then_inc(sems["do"], 16)
            op("sync", f)
            do_done[i] = P.inc("do", 16)

        def emit_store_debug_x2(i):
            wv = ve_done[("attnev", i)]
            def f(e, i=i, wv=wv):
                e.wait_ge(sems["ve"], wv)
                if DBGOUT == 1:
                    e.dma_start(out=out_dr.ap()[i * 128:(i + 1) * 128, :],
                                in_=XT[i % 2][:]).then_inc(sems["do"], 16)
                elif DBGOUT == 2:
                    e.dma_start(out=out_dr.ap()[i * 128:(i + 1) * 128, :],
                                in_=XLN[i % 2][:]).then_inc(sems["do"], 16)
                elif DBGOUT == 3:
                    e.dma_start(out=out_dr.ap()[i * 128:(i + 1) * 128, 0:K],
                                in_=WP[:]).then_inc(sems["do"], 16)
                elif DBGOUT == 4:
                    e.dma_start(out=out_dr.ap()[i * 128:(i + 1) * 128, 0:WROW],
                                in_=MT_SB[:]).then_inc(sems["do"], 16)
                elif DBGOUT == 8:
                    e.dma_start(out=out_dr.ap()[i * 128:i * 128 + 32, 0:128],
                                in_=MB[0:32, :]).then_inc(sems["do"], 16)
                elif DBGOUT == 7:
                    e.dma_start(out=out_dr.ap()[i * 128:(i + 1) * 128, 0:128],
                                in_=MA[:]).then_inc(sems["do"], 16)
                elif DBGOUT == 6:
                    e.dma_start(out=out_dr.ap()[i * 128:(i + 1) * 128, 0:NCOLS],
                                in_=ST[i % 3][:]).then_inc(sems["do"], 16)
                else:
                    e.dma_start(out=out_dr.ap()[i * 128:(i + 1) * 128, :],
                                in_=X2[i % 3][:]).then_inc(sems["do"], 16)
            op("sync", f)
            do_done[i] = P.inc("do", 16)

        for k in (range(-2, min(NT + 3, MAXCYC)) if not DEBUG_PHASE0_ONLY else []):
            if 0 <= k - 3 <= NT - 1 and not DEBUG_ATTN_ONLY and FCUT >= 6:
                emit_D(k - 3)
                if FCUT >= 7:
                    emit_evac2(k - 3)
                    if FCUT >= 8:
                        emit_store(k - 3)
            if 0 <= k <= NT - 1 and ACUT >= 7:
                emit_Mevac(k)
            if k < 0 and 0 <= k + 2 <= NTL - 1:
                emit_load_x(k + 2)
            if 0 <= k + 1 <= NTL - 1 and ACUT >= 2:
                emit_stats_alpha(k + 1)
                if ACUT >= 3:
                    emit_xln(k + 1)
            if 0 <= k - 2 <= NT - 1 and not DEBUG_ATTN_ONLY and FCUT >= 4:
                emit_chain_mid(k - 2)
            if 0 <= k + 1 <= NT - 1 and ACUT >= 4:
                emit_A(k + 1)
                if ACUT >= 5:
                    emit_band_dma(k + 1)
            if 0 <= k <= NT - 1 and ACUT >= 7:
                emit_B(k)
                emit_attn_evac(k)
                if DEBUG_ATTN_ONLY or FCUT < 8:
                    emit_store_debug_x2(k)
            if False:
                pass
            if k >= 0 and 0 <= k + 2 <= NTL - 1:
                emit_load_x(k + 2)
            if 0 <= k - 1 <= NT - 1 and not DEBUG_ATTN_ONLY and FCUT >= 3:
                emit_C(k - 1)
            if 0 <= k - 2 <= NT - 1 and not DEBUG_ATTN_ONLY and FCUT >= 5 and MCUT >= 3:
                emit_q2T(k - 2)
            if 0 <= k <= NT - 1 and not DEBUG_ATTN_ONLY and FCUT >= 1:
                emit_chain_head(k)
                if FCUT >= 2:
                    emit_q1T(k)
            if 0 <= k + 1 <= NT - 1 and ACUT >= 6:
                emit_MT(k + 1)
            if SERIAL:
                barrier()

        do_total = P.val("do")
        op("sync", lambda e, v=do_total: e.wait_ge(sems["do"], v))

        with nc.Block() as block:
            @block.sync
            def _(eng):
                for f in P.ops["sync"]:
                    f(eng)

            @block.vector
            def _(eng):
                for f in P.ops["vector"]:
                    f(eng)

            @block.scalar
            def _(eng):
                for f in P.ops["scalar"]:
                    f(eng)

            @block.tensor
            def _(eng):
                for f in P.ops["tensor"]:
                    f(eng)

            @block.gpsimd
            def _(eng):
                for f in P.ops["gpsimd"]:
                    f(eng)

    nc._dbg_plan = dict(P.cnt)
    return nc


def _bcast128(v):
    v = np.asarray(v, np.float32)
    return np.ascontiguousarray(np.broadcast_to(v[None, :], (128, v.shape[0])))


def _shard_inputs(L0_attn_x, L0_attn_W, L0_attn_b, L0_ffn_ln1_g, L0_ffn_ln1_b,
                  L0_ffn_w1, L0_ffn_b1, L0_ffn_ln2_g, L0_ffn_ln2_b, L0_ffn_w2,
                  L0_ffn_b2):
    x = np.ascontiguousarray(np.asarray(L0_attn_x, dtype=np.float32))
    B, L, Dm = x.shape
    shared = dict(
        w1t=np.ascontiguousarray(np.asarray(L0_ffn_w1, np.float32).T),
        w2t=np.ascontiguousarray(np.asarray(L0_ffn_w2, np.float32).T),
        wat=np.ascontiguousarray(np.asarray(L0_attn_W, np.float32).T),
        battn=_bcast128(L0_attn_b),
        g1=_bcast128(L0_ffn_ln1_g),
        b1ln=_bcast128(L0_ffn_ln1_b),
        g2=_bcast128(L0_ffn_ln2_g),
        b2ln=_bcast128(L0_ffn_ln2_b),
        b1=_bcast128(L0_ffn_b1),
        b2=_bcast128(L0_ffn_b2),
    )
    in_maps = []
    for c in range(8):
        b = c // 2
        half = c % 2
        s = half * SEQ
        xs = np.zeros((NTL * 128, Dm), np.float32)
        avail = min(NTL * 128, L - s)
        xs[:avail] = x[b, s:s + avail]
        m = dict(shared)
        m["x"] = xs
        in_maps.append(m)
    return in_maps


def kernel(**inputs):
    in_maps = _shard_inputs(**inputs)
    nc = build_program()
    res = run_bass_kernel_spmd(nc, in_maps, core_ids=list(range(8)))
    out = np.empty((4, 8192, 1024), np.float32)
    for c in range(8):
        b = c // 2
        half = c % 2
        out[b, half * SEQ:(half + 1) * SEQ] = np.asarray(res.results[c]["out"],
                                                         np.float32)
    return out
